# version 06
# baseline (speedup 1.0000x reference)
import numpy as np
import ml_dtypes
import concourse.bass as bass
import concourse.mybir as mybir
from concourse.bass_utils import run_bass_kernel_spmd

F32 = mybir.dt.float32
BF16 = mybir.dt.bfloat16
I16 = mybir.dt.int16
AF = mybir.ActivationFunctionType
ALU = mybir.AluOpType
AX = mybir.AxisListType
BF = ml_dtypes.bfloat16

B, T, H = 4, 8192, 1024
DI = 2048
R = 16
CH = 64
NCORE = 8
TC = T // 2
I_FFN = 2816
TT = 512
NTILE = TC // TT
NCHT = TT // CH
GRP = 128
SCALE = 0.125
LN_EPS = 1e-5
NORM_EPS = 1e-5


class Prog:

    def __init__(self):
        self.ops = []
        self.events = {}
        self.counts = {"q": 0, "t": 0, "v": 0, "s": 0, "p": 0, "pc": 0}
        self.ns = ""

    def _n(self, name):
        if name.startswith("@"):
            return name
        if self.ns and name.startswith(self.ns):
            return name
        return self.ns + name

    def op(self, eng, fn, waits=(), sig=None):
        if isinstance(waits, str):
            waits = (waits,)
        waits = tuple(self._n(w) for w in waits if w is not None)
        if sig is not None:
            sig = self._n(sig)
        for w in waits:
            assert w in self.events, f"wait on unplanned event {w}"
        self.ops.append((eng, fn, waits, sig))
        if sig is not None:
            self.counts[eng] += 16 if eng == "q" else 1
            assert sig not in self.events, f"dup event {sig}"
            self.events[sig] = (eng, self.counts[eng])
        return sig

    def barrier(self, tag, end_events):
        for eng in ("q", "t", "v", "s", "p"):
            self.op(eng, lambda e: e.nop(), waits=tuple(end_events),
                    sig=f"@bar_{tag}_{eng}")
        return [f"@bar_{tag}_{e}" for e in ("q", "t", "v", "s", "p")]

    def emit(self, nc):
        sems = {}
        ctxs = []
        for key in self.counts:
            g = nc.semaphore(f"sem_{key}")
            sems[key] = g.__enter__()
            ctxs.append(g)
        blk_g = nc.Block()
        block = blk_g.__enter__()
        ctxs.append(blk_g)

        name_map = {"q": "sync", "t": "tensor", "v": "vector", "s": "scalar",
                    "p": "gpsimd", "pc": "gpsimd"}
        streams = {v: [] for v in name_map.values()}
        for o in self.ops:
            streams[name_map[o[0]]].append(o)
        events = self.events

        def run_stream(eng_handle, ops):
            waited = {}
            for eng_key, fn, waits, sig in ops:
                for w in waits:
                    wk, wv = events[w]
                    if waited.get(wk, -1) < wv:
                        eng_handle.wait_ge(sems[wk], wv)
                        waited[wk] = wv
                inst = fn(eng_handle)
                if sig is not None:
                    ek, _ = events[sig]
                    inst.then_inc(sems[ek], 16 if ek == "q" else 1)

        for name in ("sync", "tensor", "vector", "scalar", "gpsimd"):
            ops = streams[name]
            if not ops:
                continue
            def make(oplist):
                def body(eng):
                    run_stream(eng, oplist)
                return body
            getattr(block, name)(make(ops))

        for g in reversed(ctxs):
            g.__exit__(None, None, None)


class Arena:
    def __init__(self, nc, name, cols_bf16):
        self.g = nc.sbuf_tensor(name, [128, cols_bf16], BF16)
        self.t = self.g.__enter__()
        self.cols = cols_bf16
        self.off = 0
        self.marks = []

    def alloc(self, shape, dtype=BF16):
        n = 1
        for d in shape[1:]:
            n *= d
        w = {BF16: 1, I16: 1, F32: 2}[dtype]
        if self.off % 2:
            self.off += 1
        start = self.off
        self.off += n * w
        assert self.off <= self.cols, f"arena overflow {self.off} > {self.cols}"
        view = self.t[:, start:start + n * w]
        if dtype != BF16:
            view = view.bitcast(dtype)
        if shape[0] != 128:
            view = view[0:shape[0]]
        if len(shape) > 2:
            names = " ".join(f"d{i}" for i in range(1, len(shape)))
            kw = {f"d{i}": shape[i] for i in range(2, len(shape))}
            view = view.rearrange(f"p ({names}) -> p {names}", **kw)
        return view

    def alloc_pair(self, shape, dtype=F32):
        assert shape[0] == 64
        full = self.alloc([128] + list(shape[1:]), dtype)
        return full[0:64], full[64:128]

    def clear(self):
        self.off = 0
        self.marks = []

    def mark(self):
        self.marks.append(self.off)

    def reset(self):
        self.off = self.marks.pop()

    def close(self):
        self.g.__exit__(None, None, None)


def emit_ln_group(prog, nm, xg, xng, ln_m, ln_v, ln_tr, P, waits):
    op = prog.op
    op("v", lambda e: e.tensor_reduce(ln_m[0:P], xg[0:P], AX.X, ALU.add),
       waits=waits, sig=nm + "_red")
    op("v", lambda e: e.tensor_scalar(ln_m[0:P], ln_m[0:P], -1.0 / H, None,
                                      ALU.mult), waits=(nm + "_red",),
       sig=nm + "_m")
    op("v", lambda e: e.memset(ln_v[0:P], 0.0))
    op("v", lambda e: e.tensor_scalar(ln_tr[0:P], xg[0:P], ln_m[0:P], None,
                                      ALU.add), waits=(nm + "_m",),
       sig=nm + "_c")
    op("s", lambda e: e.activation(xg[0:P], ln_tr[0:P], AF.Square,
                                   accum_out=ln_v[0:P]),
       waits=(nm + "_c",), sig=nm + "_sq")
    op("v", lambda e: e.tensor_scalar(ln_v[0:P], ln_v[0:P], 1.0 / H, LN_EPS,
                                      ALU.mult, ALU.add), waits=(nm + "_sq",),
       sig=nm + "_t")
    op("v", lambda e: e.reciprocal(ln_v[0:P], ln_v[0:P]),
       waits=(nm + "_t",), sig=nm + "_r")
    op("s", lambda e: e.activation(ln_v[0:P], ln_v[0:P], AF.Sqrt),
       waits=(nm + "_r",), sig=nm + "_rs")
    op("v", lambda e: e.tensor_scalar(xng[0:P], ln_tr[0:P], ln_v[0:P],
                                      None, ALU.mult),
       waits=(nm + "_rs",), sig=nm)
    op("v", lambda e: e.tensor_scalar(xg[0:P], ln_tr[0:P], ln_v[0:P],
                                      None, ALU.mult), sig=nm + "_f")
    return nm


def emit_quant_tile(prog, a, nm, xf, wout, P, W, repm32, waits):
    NB = W // 16
    a.mark()
    bm8 = a.alloc([P, NB * 8], F32)
    bm4 = a.alloc([P, NB * 4], F32)
    bm1 = a.alloc([P, NB], F32)
    sc32 = a.alloc([P, NB], F32)
    rcp32 = a.alloc([P, NB], F32)
    zs = a.alloc([P, W], F32)
    az = a.alloc([P, W], F32)
    d1 = a.alloc([P, W], F32)
    rcpx = a.alloc([P, W], F32)
    scx = a.alloc([P, W], F32)
    q1 = a.alloc([P, W], F32)
    n1 = a.alloc([P, W], I16)
    i2 = a.alloc([P, W], I16)
    n1c = az
    n2c = d1
    sgn = rcpx
    a.reset()

    state = {"last": None, "n": 0}

    def vop(fn, waits=(), sig=None, eng="v"):
        s_ = sig or f"{nm}_a{state['n']}"
        state["n"] += 1
        w = tuple(waits) + ((state["last"],) if state["last"] else ())
        prog.op(eng, fn, waits=w, sig=s_)
        state["last"] = s_
        return s_

    vop(lambda e: e.tensor_single_scalar(zs, xf, -1.0, ALU.mult), waits=waits)
    vop(lambda e: e.tensor_tensor(az, xf, zs, ALU.max))
    a3 = az.rearrange("p (nb bk) -> p nb bk", bk=16)
    vop(lambda e: e.tensor_tensor(bm8.rearrange("p (nb k) -> p nb k", k=8),
                                  a3[:, :, 0:8], a3[:, :, 8:16], ALU.max))
    b8 = bm8.rearrange("p (nb k) -> p nb k", k=8)
    vop(lambda e: e.tensor_tensor(bm4.rearrange("p (nb k) -> p nb k", k=4),
                                  b8[:, :, 0:4], b8[:, :, 4:8], ALU.max))
    b4 = bm4.rearrange("p (nb k) -> p nb k", k=4)
    vop(lambda e: e.tensor_tensor(
        bm8[:, 0:NB * 2].rearrange("p (nb k) -> p nb k", k=2),
        b4[:, :, 0:2], b4[:, :, 2:4], ALU.max))
    b2 = bm8[:, 0:NB * 2].rearrange("p (nb k) -> p nb k", k=2)
    vop(lambda e: e.tensor_tensor(bm1.unsqueeze(2), b2[:, :, 0:1],
                                  b2[:, :, 1:2], ALU.max), sig=nm + "_bm")
    vop(lambda e: e.tensor_scalar(sc32, bm1, 1.0 / 6.0, 1e-12,
                                  ALU.mult, ALU.max), sig=nm + "_s32")
    vop(lambda e: e.reciprocal(rcp32, sc32))
    vop(lambda e: e.memset(d1, 0.0))
    vop(lambda e: e.tensor_copy(
        d1.rearrange("p (nb bk) -> p nb bk", bk=16)[:, :, 0:1],
        rcp32.unsqueeze(2)), sig=nm + "_d1")
    vop(lambda e: e.tensor_tensor_scan(rcpx, repm32[0:P, 0:W], d1, 0.0,
                                       ALU.mult, ALU.add), sig=nm + "_rx")
    vop(lambda e: e.memset(d1, 0.0))
    vop(lambda e: e.tensor_copy(
        d1.rearrange("p (nb bk) -> p nb bk", bk=16)[:, :, 0:1],
        sc32.unsqueeze(2)), sig=nm + "_d2")
    vop(lambda e: e.tensor_tensor_scan(scx, repm32[0:P, 0:W], d1, 0.0,
                                       ALU.mult, ALU.add), sig=nm + "_sx")
    vop(lambda e: e.tensor_tensor(zs, az, rcpx, ALU.mult), sig=nm + "_zs")
    vop(lambda e: e.activation(sgn, xf, AF.Sign), eng="s", sig=nm + "_sg")
    vop(lambda e: e.tensor_scalar(n1, zs, 2.0, None, ALU.mult))
    vop(lambda e: e.tensor_single_scalar(i2, n1, 0, ALU.max))
    vop(lambda e: e.tensor_single_scalar(n1c, i2, 4, ALU.min))
    vop(lambda e: e.tensor_scalar(n1, zs, -2.0, None, ALU.add))
    vop(lambda e: e.tensor_single_scalar(i2, n1, 0, ALU.max))
    vop(lambda e: e.tensor_single_scalar(n2c, i2, 2, ALU.min))
    vop(lambda e: e.scalar_tensor_tensor(q1, n1c, 0.5, n2c, ALU.mult,
                                         ALU.add))
    vop(lambda e: e.tensor_scalar(n1, zs, -4.5, None, ALU.add))
    vop(lambda e: e.tensor_single_scalar(i2, n1, 0, ALU.max))
    vop(lambda e: e.tensor_single_scalar(n2c, i2, 1, ALU.min))
    vop(lambda e: e.scalar_tensor_tensor(q1, n2c, 2.0, q1, ALU.mult,
                                         ALU.add))
    vop(lambda e: e.tensor_tensor(q1, q1, sgn, ALU.mult),
        waits=(nm + "_sg",))
    return vop(lambda e: e.tensor_tensor(wout, q1, scx, ALU.mult), sig=nm)


def emit_cast_weight(prog, nm, dram, sb_dst, rows, cols, wstage, crows=128,
                     prev=None):
    op = prog.op
    nch = (rows + crows - 1) // crows
    last = None
    for c in range(nch):
        pr = min(crows, rows - c * crows)
        w = (f"{nm}_c{c-1}",) if c >= 1 else ((prev,) if prev else ())
        op("q", lambda e, c=c, pr=pr, cols=cols: e.dma_start(
            out=wstage[0:pr, 0:cols], in_=dram[c * crows:c * crows + pr, :]),
            waits=w, sig=f"{nm}_ld{c}")
        last = op("s", lambda e, c=c, pr=pr, cols=cols: e.copy(
            sb_dst[:, c, 0:cols][0:pr], wstage[0:pr, 0:cols]),
            waits=(f"{nm}_ld{c}",), sig=f"{nm}_c{c}")
    return last


FUSED_INTERNAL = {"xn_tm", "xnT", "srwT", "oL", "Send", "qeT", "Ecols",
                  "wgq_sh", "wuq_sh", "wdq_sh", "s_out", "Sgather",
                  "wgq_g", "wuq_g", "wdq_g"}


def make_mk(nc, cache, fused):
    def mk(name, shape, dtype, kind):
        if name in cache:
            return cache[name]
        if fused and name in FUSED_INTERNAL:
            kind = "Internal"
        ap = nc.dram_tensor(name, shape, dtype, kind=kind).ap()
        cache[name] = ap
        return ap
    return mk


def emit_phase_a(nc, prog, a, mk, ntile=NTILE, debug=False, fused=False,
                 qeT_buf=None, E_buf=None):
    d_x = mk("x", [TC + 3, H], F32, "ExternalInput")
    d_upT = mk("upT", [H, DI], F32, "ExternalInput")
    d_qkT = mk("qkT", [DI, 128], F32, "ExternalInput")
    d_gtT = mk("gtT", [DI, 128], F32, "ExternalInput")
    d_ig1T = mk("ig1T", [DI, R], F32, "ExternalInput")
    d_ig2T = mk("ig2T", [R, DI], F32, "ExternalInput")
    d_convw = mk("convw", [128, 64], F32, "ExternalInput")
    d_convb = mk("convb", [128, 16], F32, "ExternalInput")
    d_igb2 = mk("igb2", [128, 16], F32, "ExternalInput")
    d_rw = mk("rw", [128, 16], F32, "ExternalInput")
    d_gb = mk("gb", [64, 1], F32, "ExternalInput")
    d_tb = mk("tb", [64, 1], F32, "ExternalInput")
    d_ident = mk("ident", [128, 128], BF16, "ExternalInput")
    d_id64f = mk("id64f", [64, 64], F32, "ExternalInput")
    d_mask = mk("maskLT", [64, 64], F32, "ExternalInput")
    d_ones64 = mk("ones64", [64, 64], F32, "ExternalInput")
    d_ones1x = mk("ones1x", [1, 64], BF16, "ExternalInput")
    d_repm32 = mk("repm32", [128, 1024], F32, "ExternalInput")
    d_wg_sh = mk("wg_sh", [352, H], F32, "ExternalInput")
    d_wu_sh = mk("wu_sh", [352, H], F32, "ExternalInput")
    d_wd_sh = mk("wd_sh", [128, I_FFN], F32, "ExternalInput")

    o_xn = mk("xn_tm", [TC, H], F32, "ExternalOutput")
    o_xnT = mk("xnT", [H, TC], BF16, "ExternalOutput")
    o_srwT = mk("srwT", [TC, DI], BF16, "ExternalOutput")
    o_oL = mk("oL", [TC, DI], BF16, "ExternalOutput")
    o_Send = mk("Send", [64, DI], F32, "ExternalOutput")
    o_qeT = mk("qeT", [64, TC], BF16, "ExternalOutput")
    o_E = mk("Ecols", [64, 64], F32, "ExternalOutput")
    o_wgq = mk("wgq_sh", [352, H], BF16, "ExternalOutput")
    o_wuq = mk("wuq_sh", [352, H], BF16, "ExternalOutput")
    o_wdq = mk("wdq_sh", [128, I_FFN], BF16, "ExternalOutput")
    if debug:
        o_dbg = mk("dbg", [128, 4 * H], BF16, "ExternalOutput")
        o_dbgf = mk("dbgf", [64, 64], F32, "ExternalOutput")

    op = prog.op

    upT = a.alloc([128, 8, DI], BF16)
    qkT = a.alloc([128, 16, 128], BF16)
    gtT = a.alloc([128, 16, 128], BF16)
    ig1T = a.alloc([128, 16, R], BF16)
    ig2T = a.alloc([16, 16, 128], BF16)
    convw = a.alloc([128, 16, 4], F32)
    convb = a.alloc([128, 16], F32)
    igb2 = a.alloc([128, 16], F32)
    rw = a.alloc([128, 16], F32)
    gb = a.alloc([64, 1], F32)
    tb = a.alloc([64, 1], F32)
    ident = a.alloc([128, 128], BF16)
    id64f = a.alloc([64, 64], F32)
    maskLT = a.alloc([64, 64], F32)
    ones64 = a.alloc([64, 64], F32)
    ones1x = a.alloc([1, 64], BF16)
    repm32 = a.alloc([128, 1024], F32)
    S = a.alloc([64, DI], F32)
    Sb = a.alloc([64, DI], BF16)
    qeT_all = qeT_buf if qeT_buf is not None else a.alloc([64, TC], BF16)
    E_all = E_buf if E_buf is not None else a.alloc([64, 64], F32)
    Bacc = a.alloc([64, 16], F32)
    Pincl = a.alloc([64, 16], F32)

    a.mark()
    wstage = a.alloc([128, DI], F32)
    xg = a.alloc([128, H], F32)
    xng = a.alloc([128, H], BF16)
    ln_tr = a.alloc([128, H], F32)
    ln_m = a.alloc([128, 1], F32)
    ln_v = a.alloc([128, 1], F32)
    xnT_t = a.alloc([128, 8, TT], BF16)
    xup = a.alloc([128, 16, TT + 3], BF16)
    xup_halo = a.alloc([128, 16, 3], BF16)
    sh = a.alloc([128, 16, TT], BF16)
    srw_g = a.alloc([128, DI], BF16)
    u16 = a.alloc([16, TT], BF16)
    vt = a.alloc([64, NCHT, DI], BF16)
    qf = a.alloc([64, TT], F32)
    kf = a.alloc([64, TT], F32)
    gg = a.alloc([64, TT], F32)
    tau = a.alloc([64, TT], F32)
    tmp1 = a.alloc([64, TT], F32)
    tmp2 = a.alloc([64, TT], F32)
    tmp3 = a.alloc([64, TT], F32)
    ksq = tmp3
    rt = a.alloc([64, TT], F32)
    bpos = a.alloc([64, TT], F32)
    btr = a.alloc([64, TT], F32)
    d1b = tmp2
    rep64 = a.alloc([64, TT], F32)
    it = a.alloc([64, TT], F32)
    kn = a.alloc([64, TT], F32)
    kd = a.alloc([64, TT], F32)
    ke = a.alloc([64, TT], BF16)
    rcp1 = a.alloc([1, TT], F32)
    rcp1b = a.alloc([1, TT], BF16)
    btot = a.alloc([64, NCHT], F32)
    decay = a.alloc([64, NCHT], F32)
    tmpE = a.alloc([64, NCHT], F32)
    A_sb = a.alloc([64, 64], BF16)
    kdT = a.alloc([64, 64], BF16)
    oL_g = a.alloc([128, DI], BF16)

    pg = []
    def palloc(name, shape, dtype):
        g = nc.psum_tensor(prog.ns + name, shape, dtype)
        t = g.__enter__(); pg.append(g)
        return t[:]
    pA = palloc("pA", [128, TT], F32)
    pB = palloc("pB", [128, TT], F32)
    pTR = palloc("pTR", [128, 1024], BF16)
    pSH = palloc("pSH", [64, 512], F32)
    pO = palloc("pO", [64, 1024], F32)
    pS = palloc("pS", [128, 1024], F32)

    for nm, dst, src in [
        ("c_convw", convw.rearrange("p a b -> p (a b)"), d_convw),
        ("c_convb", convb, d_convb), ("c_igb2", igb2, d_igb2),
        ("c_rw", rw, d_rw), ("c_gb", gb, d_gb), ("c_tb", tb, d_tb),
        ("c_id", ident, d_ident), ("c_id64", id64f, d_id64f),
        ("c_mask", maskLT, d_mask), ("c_on", ones64, d_ones64),
        ("c_on1", ones1x, d_ones1x), ("c_repm32", repm32, d_repm32),
    ]:
        op("q", lambda e, dst=dst, src=src: e.dma_start(out=dst, in_=src), sig=nm)

    ev_up = emit_cast_weight(prog, "wup", d_upT, upT, H, DI, wstage)
    ev_qk = emit_cast_weight(prog, "wqk", d_qkT, qkT, DI, 128, wstage,
                             prev=ev_up)
    ev_gt = emit_cast_weight(prog, "wgt", d_gtT, gtT, DI, 128, wstage,
                             prev=ev_qk)
    ev_ig1 = emit_cast_weight(prog, "wig1", d_ig1T, ig1T, DI, R, wstage,
                              prev=ev_gt)
    op("q", lambda e: e.dma_start(out=wstage[0:16, 0:DI], in_=d_ig2T),
       waits=(ev_ig1,), sig="wig2_ld")
    ev_ig2 = op("s", lambda e: e.copy(ig2T.rearrange("p c f -> p (c f)"),
                                      wstage[0:16, 0:DI]),
                waits=("wig2_ld",), sig="wig2")

    op("v", lambda e: e.memset(S, 0.0), sig="S_init")
    op("v", lambda e: e.memset(Sb, 0.0), sig="Sb_init")
    op("v", lambda e: e.memset(Bacc[:, 0:1], 0.0), sig="Bacc_init")
    op("v", lambda e: e.memset(rep64, 1.0))
    op("v", lambda e: e.memset(
        rep64.rearrange("p (c t) -> p c t", t=CH)[:, :, 0:1], 0.0),
       sig="rep64_i")

    op("q", lambda e: e.dma_start(out=xg[0:3], in_=d_x[0:3, :]), sig="halo_ld")
    ev = emit_ln_group(prog, "halo_ln", xg, xng, ln_m, ln_v, ln_tr, 3,
                       ("halo_ld",))
    for j in range(8):
        op("t", lambda e, j=j: e.transpose(pTR[:, j * 128:j * 128 + 3],
                                           xng[0:3, 128 * j:128 * (j + 1)],
                                           ident[0:3, 0:3]),
           waits=(ev, "c_id") if j == 0 else (),
           sig="halo_tr" if j == 7 else None)
    op("v", lambda e: e.tensor_copy(
        xnT_t[:, :, 0:3],
        pTR.rearrange("p (c t) -> p c t", t=128)[:, :, 0:3]),
       waits=("halo_tr",), sig="halo_xnT")
    for d in range(16):
        wdd = []
        if d == 0:
            wdd = ["halo_xnT", ev_up]
        else:
            wdd = [f"halo_up{d-1}"]
        for c in range(8):
            op("t", lambda e, d=d, c=c: e.matmul(
                pA[:, 0:3], upT[:, c, 128 * d:128 * (d + 1)], xnT_t[:, c, 0:3],
                start=(c == 0), stop=(c == 7)),
               waits=tuple(wdd) if c == 0 else (),
               sig=f"halo_upm{d}" if c == 7 else None)
        op("s", lambda e, d=d: e.copy(xup_halo[:, d, :], pA[:, 0:3]),
           waits=(f"halo_upm{d}",), sig=f"halo_up{d}")

    for ti in range(ntile):
        t0 = ti * TT
        P = f"_t{ti}"
        Pm = f"_t{ti-1}"
        for g in range(4):
            gt0 = t0 + g * GRP
            if g > 0:
                wl = [f"ln{P}_g{g-1}_sq", f"xnT_tr{P}_g{g-1}",
                      f"xn_st{P}_g{g-1}"]
            elif ti > 0:
                wl = [f"ln{Pm}_g3_sq", f"xnT_tr{Pm}_g3", f"xn_st{Pm}_g3"]
            else:
                wl = ["halo_ln_sq", "halo_xnT"]
            op("q", lambda e, gt0=gt0: e.dma_start(
                out=xg, in_=d_x[3 + gt0:3 + gt0 + GRP, :]),
               waits=tuple(wl), sig=f"x_ld{P}_g{g}")
            ev = emit_ln_group(prog, f"ln{P}_g{g}", xg, xng, ln_m, ln_v, ln_tr,
                               128, (f"x_ld{P}_g{g}",))
            op("q", lambda e, gt0=gt0: e.dma_start(out=o_xn[gt0:gt0 + GRP, :],
                                                   in_=xg),
               waits=(ev + "_f",), sig=f"xn_st{P}_g{g}")

            wtr = [ev]
            if g > 0:
                wtr.append(f"xnT_ev{P}_g{g-1}")
            elif ti > 0:
                wtr.append(f"vt_ev{Pm}_c{NCHT-1}_h1")
            else:
                wtr.append("halo_xnT")
            for j in range(8):
                op("t", lambda e, j=j: e.transpose(
                    pTR[:, j * 128:(j + 1) * 128], xng[:, 128 * j:128 * (j + 1)],
                    ident),
                   waits=tuple(wtr) if j == 0 else (),
                   sig=f"xnT_tr{P}_g{g}" if j == 7 else None)
            wev = [f"xnT_tr{P}_g{g}"]
            if g == 0:
                if ti > 0:
                    wev += [f"up_mm{Pm}_d15", f"xnT_st{Pm}"]
                else:
                    wev.append("halo_up15")
            op("v", lambda e, g=g: e.tensor_copy(
                xnT_t[:, :, g * GRP:(g + 1) * GRP],
                pTR.rearrange("p (c t) -> p c t", t=128)),
               waits=tuple(wev), sig=f"xnT_ev{P}_g{g}")
        op("q", lambda e, t0=t0: e.dma_start(
            out=o_xnT.rearrange("(c p) t -> p c t", p=128)[:, :, t0:t0 + TT],
            in_=xnT_t),
           waits=(f"xnT_ev{P}_g3",), sig=f"xnT_st{P}")

        if ti == 0:
            op("v", lambda e: e.tensor_copy(
                xup[:, :, 0:3], xup_halo),
               waits=("halo_up15",), sig=f"halo_cp{P}")
        else:
            op("v", lambda e: e.tensor_copy(
                xup[:, :, 0:3], xup_halo),
               waits=(f"halo_sv{Pm}",), sig=f"halo_cp{P}")

        for d in range(16):
            pp = [pA, pB][d % 2]
            wgd = []
            if d >= 2:
                wgd.append(f"up_ev{P}_d{d-2}")
            if d == 0:
                wgd += [f"xnT_ev{P}_g3", f"halo_cp{P}"]
                if ti > 0:
                    wgd += [f"vmul{Pm}_d15", f"vt_ev{Pm}_c{NCHT-1}_h1",
                            f"kf_ev{Pm}", f"u_ev{Pm}"]
                else:
                    wgd.append("halo_up15")
            if d == 1 and ti > 0:
                wgd += [f"tau_ev{Pm}"]
            for c in range(8):
                op("t", lambda e, d=d, c=c, pp=pp: e.matmul(
                    pp, upT[:, c, 128 * d:128 * (d + 1)], xnT_t[:, c, :],
                    start=(c == 0), stop=(c == 7)),
                   waits=tuple(wgd) if c == 0 else (),
                   sig=f"up_mm{P}_d{d}" if c == 7 else None)
            op("s", lambda e, d=d, pp=pp: e.copy(xup[:, d, 3:TT + 3], pp),
               waits=(f"up_mm{P}_d{d}",), sig=f"up_ev{P}_d{d}")

        for d in range(16):
            wc = [f"up_ev{P}_d{d}"]
            if ti == 0 and d == 0:
                wc += ["c_convw", "c_convb"]
            if ti > 0 and d == 0:
                wc += [f"srw_st{Pm}_g3", f"srwtr{Pm}_g3_h1",
                       f"zsig{Pm}_d15"]
            op("v", lambda e, d=d: e.tensor_single_scalar(
                sh[:, d, :], xup[:, d, 0:TT], convw[:, d, 0:1], ALU.mult),
               waits=tuple(wc))
            op("v", lambda e, d=d: e.tensor_single_scalar(
                sh[:, d, :], sh[:, d, :], convb[:, d:d + 1], ALU.add))
            for j in range(1, 4):
                op("v", lambda e, d=d, j=j: e.scalar_tensor_tensor(
                    sh[:, d, :], xup[:, d, j:TT + j], convw[:, d, j:j + 1],
                    sh[:, d, :], ALU.mult, ALU.add),
                   sig=f"convacc{P}_d{d}" if j == 3 else None)
        op("s", lambda e: e.activation(sh.rearrange("p c t -> p (c t)"),
                                       sh.rearrange("p c t -> p (c t)"),
                                       AF.Silu),
           waits=(f"convacc{P}_d15",), sig=f"conv{P}")

        op("v", lambda e: e.tensor_copy(
            xup_halo, xup[:, :, TT:TT + 3]),
           waits=(f"convacc{P}_d15",), sig=f"halo_sv{P}")

        if debug and ti == 0:
            op("q", lambda e: e.dma_start(out=o_dbg[:, 0:TT],
                                          in_=xup[:, 0, 3:TT + 3]),
               waits=(f"up_ev{P}_d0",), sig="dbg3")
            op("q", lambda e: e.dma_start(out=o_dbg[:, TT:2 * TT],
                                          in_=sh[:, 0, :]),
               waits=(f"conv{P}",), sig="dbg4")
        wqk = [f"conv{P}"]
        if ti == 0:
            wqk.append(ev_qk)
        else:
            wqk.append(f"kf_ev{Pm}")
        for c in range(16):
            op("t", lambda e, c=c: e.matmul(pA, qkT[:, c, :], sh[:, c, :],
                                            start=(c == 0), stop=(c == 15)),
               waits=tuple(wqk) if c == 0 else (),
               sig=f"qk_mm{P}" if c == 15 else None)
        wgt = []
        if ti == 0:
            wgt.append(ev_gt)
        else:
            wgt.append(f"tau_ev{Pm}")
        for c in range(16):
            op("t", lambda e, c=c: e.matmul(pB, gtT[:, c, :], sh[:, c, :],
                                            start=(c == 0), stop=(c == 15)),
               waits=tuple(wgt) if c == 0 else (),
               sig=f"gt_mm{P}" if c == 15 else None)
        op("v", lambda e: e.tensor_copy(qf, pA[0:64]), waits=(f"qk_mm{P}",))
        op("v", lambda e: e.tensor_copy(kf, pA[64:128]), sig=f"kf_ev{P}")

        wpa = [f"gt_mm{P}"]
        if ti == 0:
            wpa.append("c_gb")
        op("s", lambda e: e.activation(tmp1, pB[0:64], AF.Abs, bias=gb),
           waits=tuple(wpa), sig=f"pa{P}")
        op("s", lambda e: e.activation(tmp2, tmp1, AF.Exp, scale=-1.0))
        op("s", lambda e: e.activation(tmp2, tmp2, AF.Ln, bias=1.0),
           sig=f"pl{P}")
        op("s", lambda e: e.activation(tmp1, pB[0:64], AF.Relu, bias=gb),
           sig=f"pr{P}")
        wtb = [] if ti > 0 else ["c_tb"]
        op("s", lambda e: e.activation(tau, pB[64:128], AF.Sigmoid, bias=tb),
           waits=tuple(wtb), sig=f"tau_ev{P}")
        op("v", lambda e: e.tensor_tensor(gg, tmp1, tmp2, ALU.add),
           waits=(f"pl{P}", f"pr{P}"), sig=f"gg{P}")
        op("s", lambda e: e.activation(tmp1, gg, AF.Ln), waits=(f"gg{P}",),
           sig=f"plng{P}")
        op("v", lambda e: e.tensor_tensor(tmp2, tau, tmp1, ALU.mult),
           waits=(f"plng{P}", f"tau_ev{P}"), sig=f"pt{P}")
        op("s", lambda e: e.activation(it, tmp2, AF.Exp), waits=(f"pt{P}",),
           sig=f"it{P}")
        op("v", lambda e: e.tensor_tensor(rt, gg, tau, ALU.mult),
           waits=(f"tau_ev{P}",), sig=f"rt{P}")

        for c in range(NCHT):
            sl = slice(c * CH, (c + 1) * CH)
            op("v", lambda e, sl=sl: e.tensor_tensor_scan(
                bpos[:, sl], ones64[:, 0:CH], rt[:, sl], 0.0, ALU.mult, ALU.add),
               waits=(f"rt{P}", "c_on") if c == 0 else (),
               sig=f"bpos{P}" if c == NCHT - 1 else None)
        op("v", lambda e: e.tensor_copy(
            btot.unsqueeze(2),
            bpos.rearrange("p (c t) -> p c t", t=CH)[:, :, CH - 1:CH]),
           waits=(f"bpos{P}",), sig=f"btot{P}")
        op("s", lambda e: e.activation(decay, btot, AF.Exp, scale=-1.0),
           waits=(f"btot{P}",), sig=f"decay{P}")
        wtm = ["Bacc_init"] if ti == 0 else [f"E{Pm}", f"bacc{Pm}"]
        op("v", lambda e: e.tensor_copy(tmpE[:, 0:1], Bacc[:, 0:1]),
           waits=tuple(wtm), sig=f"tE{P}_0")
        for cc in range(1, NCHT):
            op("v", lambda e, cc=cc: e.tensor_tensor(
                tmpE[:, cc:cc + 1], tmpE[:, cc - 1:cc], btot[:, cc - 1:cc],
                ALU.add), waits=(f"tE{P}_{cc-1}", f"btot{P}"),
               sig=f"tE{P}_{cc}")
        op("v", lambda e: e.tensor_single_scalar(tmpE, tmpE, 1.0, ALU.mult),
           waits=(f"tE{P}_{NCHT-1}",), sig=f"tmpE{P}")
        op("s", lambda e, ti=ti: e.activation(E_all[:, ti * NCHT:(ti + 1) * NCHT],
                                              tmpE, AF.Exp, scale=-1.0),
           waits=(f"tmpE{P}",), sig=f"E{P}")
        op("v", lambda e: e.tensor_tensor(
            Bacc[:, 0:1], tmpE[:, NCHT - 1:NCHT], btot[:, NCHT - 1:NCHT],
            ALU.add), waits=(f"tmpE{P}", f"E{P}"), sig=f"bacc{P}")
        if debug and ti == 0:
            op("q", lambda e: e.dma_start(out=o_dbgf[:, 0:NCHT], in_=btot),
               waits=(f"btot{P}",), sig="dbgf0")
            op("q", lambda e: e.dma_start(out=o_dbgf[:, 16:16 + NCHT], in_=tmpE),
               waits=(f"tmpE{P}",), sig="dbgf2")

        op("v", lambda e: e.tensor_tensor(ksq, kf, kf, ALU.mult),
           sig=f"ksq{P}")
        wkn = [f"ksq{P}"]
        if ti > 0:
            wkn.append(f"kdT_ev{Pm}_c{NCHT-1}")
        op("t", lambda e: e.matmul(pSH[0:1, :], ones64[:, 0:1], ksq,
                                   start=True, stop=True),
           waits=tuple(wkn), sig=f"nsq_mm{P}")
        op("s", lambda e: e.activation(rcp1, pSH[0:1, :], AF.Sqrt),
           waits=(f"nsq_mm{P}",), sig=f"nsqrt{P}")
        op("v", lambda e: e.tensor_scalar(rcp1, rcp1, 1.0, 1e-12,
                                          ALU.mult, ALU.max),
           waits=(f"nsqrt{P}",), sig=f"rmax{P}")
        op("v", lambda e: e.reciprocal(rcp1, rcp1), waits=(f"rmax{P}",),
           sig=f"rcpr{P}")
        op("v", lambda e: e.tensor_copy(rcp1b, rcp1), waits=(f"rcpr{P}",),
           sig=f"rcp1b{P}")
        wbc = [f"rcp1b{P}"]
        if ti == 0:
            wbc.append("c_on1")
        op("t", lambda e: e.matmul(pSH, ones1x, rcp1b, start=True, stop=True),
           waits=tuple(wbc), sig=f"bc_mm{P}")
        op("v", lambda e: e.tensor_tensor(kn, kf, pSH, ALU.mult),
           waits=(f"bc_mm{P}",))
        op("v", lambda e: e.tensor_tensor(kn, kn, it, ALU.mult),
           waits=(f"it{P}",), sig=f"kn{P}")

        op("s", lambda e: e.activation(tmp1, bpos, AF.Exp, scale=-1.0),
           waits=(f"bpos{P}", f"pt{P}"), sig=f"expb{P}")
        wqe = [f"expb{P}"]
        if ti > 0:
            wqe.append(f"Amm{Pm}_c{NCHT-1}")
        op("v", lambda e, t0=t0: e.scalar_tensor_tensor(
            qeT_all[:, t0:t0 + TT], qf, SCALE, tmp1, ALU.mult, ALU.mult),
           waits=tuple(wqe), sig=f"qe{P}")
        op("s", lambda e: e.activation(tmp2, bpos, AF.Exp), sig=f"expp{P}")
        op("v", lambda e: e.tensor_tensor(ke, kn, tmp2, ALU.mult),
           waits=(f"expp{P}", f"kn{P}"), sig=f"ke{P}")
        op("v", lambda e: e.memset(d1b, 0.0), waits=(f"ke{P}",))
        op("v", lambda e: e.tensor_copy(
            d1b.rearrange("p (c t) -> p c t", t=CH)[:, :, 0:1],
            btot.unsqueeze(2)), waits=(f"btot{P}",), sig=f"d1b{P}")
        op("v", lambda e: e.tensor_tensor_scan(btr, rep64, d1b, 0.0, ALU.mult,
                                               ALU.add),
           waits=(f"d1b{P}", "rep64_i"), sig=f"btr{P}")
        op("v", lambda e: e.tensor_tensor(tmp3, bpos, btr, ALU.subtract),
           waits=(f"btr{P}", f"nsq_mm{P}"), sig=f"sub{P}")
        op("s", lambda e: e.activation(tmp3, tmp3, AF.Exp), waits=(f"sub{P}",),
           sig=f"expd{P}")
        op("v", lambda e: e.tensor_tensor(kd, kn, tmp3, ALU.mult),
           waits=(f"expd{P}",), sig=f"kd{P}")

        for d in range(16):
            wsr = []
            if d == 0:
                wsr = [f"conv{P}", f"qk_mm{P}", f"gt_mm{P}"]
                if debug and ti == 0:
                    wsr.append("dbg4")
                if ti == 0:
                    wsr.append("c_rw")
            op("v", lambda e, d=d: e.tensor_scalar(
                sh[:, d, :], sh[:, d, :], rw[:, d:d + 1], None, ALU.mult),
               waits=tuple(wsr), sig=f"srw{P}" if d == 15 else None)
        for g in range(4):
            for hh in range(2):
                for jj in range(8):
                    j = hh * 8 + jj
                    wtr = []
                    if jj == 0:
                        wtr = [f"srw{P}"]
                        if hh == 1:
                            wtr.append(f"srw_ev{P}_g{g}_h0")
                        elif g > 0:
                            wtr.append(f"srw_ev{P}_g{g-1}_h1")
                        else:
                            wtr.append(f"xnT_ev{P}_g3")
                    op("t", lambda e, g=g, j=j, jj=jj: e.transpose(
                        pTR[:, jj * 128:(jj + 1) * 128],
                        sh[:, j, g * GRP:(g + 1) * GRP], ident),
                       waits=tuple(wtr),
                       sig=f"srwtr{P}_g{g}_h{hh}" if jj == 7 else None)
                we = [f"srwtr{P}_g{g}_h{hh}"]
                if hh == 0 and g > 0:
                    we.append(f"srw_st{P}_g{g-1}")
                op("v", lambda e, hh=hh: e.tensor_copy(
                    srw_g[:, hh * 1024:(hh + 1) * 1024], pTR),
                   waits=tuple(we), sig=f"srw_ev{P}_g{g}_h{hh}")
            op("q", lambda e, g=g, t0=t0: e.dma_start(
                out=o_srwT[t0 + g * GRP:t0 + (g + 1) * GRP, :], in_=srw_g),
               waits=(f"srw_ev{P}_g{g}_h1",), sig=f"srw_st{P}_g{g}")
        op("s", lambda e: e.copy(rcp1b[0:1, 0:1], rcp1b[0:1, 0:1]),
           waits=(f"srwtr{P}_g3_h1",), sig=f"srwtr_done{P}")

        wig = [f"conv{P}", f"tau_ev{P}"]
        if ti == 0:
            wig.append(ev_ig2)
        else:
            wig.append(f"u_ev{Pm}")
        for c in range(16):
            op("t", lambda e, c=c: e.matmul(pB[0:16, 0:TT], ig1T[:, c, :],
                                            xup[:, c, 3:TT + 3],
                                            start=(c == 0), stop=(c == 15)),
               waits=tuple(wig) if c == 0 else (),
               sig=f"u_mm{P}" if c == 15 else None)
        op("v", lambda e: e.tensor_copy(u16, pB[0:16, 0:TT]),
           waits=(f"u_mm{P}",), sig=f"u_ev{P}")
        for d in range(16):
            pp = [pA, pB][d % 2]
            wz = []
            if d >= 2:
                wz.append(f"zsig{P}_d{d-2}")
            if d == 0:
                wz += [f"u_ev{P}", f"kf_ev{P}"]
            if d == 1:
                wz += [f"u_ev{P}", f"tau_ev{P}"]
            op("t", lambda e, d=d, pp=pp: e.matmul(
                pp[:, 0:TT], ig2T[:, d, :], u16, start=True, stop=True),
               waits=tuple(wz), sig=f"z_mm{P}_d{d}")
            wzs = [f"z_mm{P}_d{d}", f"srwtr_done{P}"]
            if ti == 0 and d == 0:
                wzs.append("c_igb2")
            op("s", lambda e, d=d, pp=pp: e.activation(
                sh[:, d, :], pp[:, 0:TT], AF.Sigmoid, bias=igb2[:, d:d + 1]),
               waits=tuple(wzs), sig=f"zsig{P}_d{d}")
            wvm = [f"zsig{P}_d{d}", f"halo_sv{P}"]
            if debug and ti == 0 and d == 0:
                wvm.append("dbg3")
            op("v", lambda e, d=d: e.tensor_tensor(
                xup[:, d, 3:TT + 3], sh[:, d, :], xup[:, d, 3:TT + 3],
                ALU.mult),
               waits=tuple(wvm),
               sig=f"vmul{P}_d{d}")

        for c in range(NCHT):
            for hh in range(2):
                for dd in range(8):
                    d = hh * 8 + dd
                    wv = []
                    if dd == 0:
                        wv = [f"vmul{P}_d15"]
                        if not (c == 0 and hh == 0):
                            prevc, prevh = (c, 0) if hh == 1 else (c - 1, 1)
                            wv.append(f"vt_ev{P}_c{prevc}_h{prevh}")
                        else:
                            wv.append(f"srw_ev{P}_g3_h1")
                    op("t", lambda e, c=c, d=d, dd=dd: e.transpose(
                        pTR[0:64, dd * 128:(dd + 1) * 128],
                        xup[:, d, 3 + c * CH:3 + (c + 1) * CH], ident),
                       waits=tuple(wv),
                       sig=f"vtr{P}_c{c}_h{hh}" if dd == 7 else None)
                op("v", lambda e, c=c, hh=hh: e.tensor_copy(
                    vt[:, c, hh * 1024:(hh + 1) * 1024], pTR[0:64]),
                   waits=(f"vtr{P}_c{c}_h{hh}",), sig=f"vt_ev{P}_c{c}_h{hh}")

        for c in range(NCHT):
            gl = f"{P}_c{c}"
            glm = f"{P}_c{c-1}"
            lsl = slice(c * CH, (c + 1) * CH)
            qsl = slice(t0 + c * CH, t0 + (c + 1) * CH)
            wA = [f"ke{P}", f"qe{P}"]
            if c == 0:
                wA.append(f"kn{P}")
            else:
                wA += [f"Amask{glm}", f"kdT_ev{glm}"]
            op("t", lambda e, lsl=lsl, qsl=qsl: e.matmul(
                pSH[:, 0:64], ke[:, lsl], qeT_all[:, qsl],
                start=True, stop=True),
               waits=tuple(wA), sig=f"Amm{gl}")
            wK = [f"kd{P}"]
            if ti == 0 and c == 0:
                wK.append("c_id64")
            op("t", lambda e, lsl=lsl: e.transpose(
                pSH[:, 64:128], kd[:, lsl], id64f),
               waits=tuple(wK), sig=f"kdTmm{gl}")
            wM = [f"kdTmm{gl}"]
            if ti == 0 and c == 0:
                wM.append("c_mask")
            op("v", lambda e: e.tensor_tensor(A_sb, pSH[:, 0:64], maskLT,
                                              ALU.mult),
               waits=tuple(wM), sig=f"Amask{gl}")
            op("v", lambda e: e.tensor_copy(kdT, pSH[:, 64:128]),
               sig=f"kdT_ev{gl}")

            for h in range(2):
                hsl = slice(h * 1024, (h + 1) * 1024)
                wo = [f"vt_ev{P}_c{c}_h{h}", f"Amask{gl}"]
                if ti == 0 and c == 0 and h == 0:
                    wo.append("Sb_init")
                if c > 0:
                    wo.append(f"SbU{glm}_h{h}")
                elif ti > 0:
                    wo.append(f"SbU{Pm}_c{NCHT-1}_h{h}")
                if h == 1:
                    wo.append(f"oev{gl}_h0")
                elif c > 0:
                    wo.append(f"oev{glm}_h1")
                elif ti > 0:
                    wo.append(f"oev{Pm}_c{NCHT-1}_h1")
                for s2 in range(2):
                    op("t", lambda e, qsl=qsl, h=h, s2=s2: e.matmul(
                        pO[:, s2 * 512:(s2 + 1) * 512], qeT_all[:, qsl],
                        Sb[:, h * 1024 + s2 * 512:h * 1024 + (s2 + 1) * 512],
                        start=True, stop=False),
                       waits=tuple(wo) if s2 == 0 else ())
                for s2 in range(2):
                    op("t", lambda e, c=c, h=h, s2=s2: e.matmul(
                        pO[:, s2 * 512:(s2 + 1) * 512], A_sb,
                        vt[:, c, h * 1024 + s2 * 512:h * 1024 + (s2 + 1) * 512],
                        start=False, stop=True),
                       sig=f"omm{gl}_h{h}" if s2 == 1 else None)
                row = 64 * (c % 2)
                woe = [f"omm{gl}_h{h}"]
                if c % 2 == 0 and h == 0:
                    if c > 1:
                        woe.append(f"oL_st{P}_c{c-1}")
                    elif ti > 0:
                        woe.append(f"oL_st{Pm}_c{NCHT-1}")
                op("s", lambda e, row=row, hsl=hsl: e.copy(
                    oL_g[row:row + 64, hsl], pO),
                   waits=tuple(woe), sig=f"oev{gl}_h{h}")
                wS = [f"kdT_ev{gl}", f"vt_ev{P}_c{c}_h{h}"]
                if c > 0:
                    wS.append(f"SU{glm}_h{h}")
                elif ti > 0:
                    wS.append(f"SU{Pm}_c{NCHT-1}_h{h}")
                if h == 1:
                    wS.append(f"SU{gl}_h0")
                for s2 in range(2):
                    op("t", lambda e, c=c, h=h, s2=s2: e.matmul(
                        pS[0:64, s2 * 512:(s2 + 1) * 512], kdT,
                        vt[:, c, h * 1024 + s2 * 512:h * 1024 + (s2 + 1) * 512],
                        start=True, stop=True),
                       waits=tuple(wS) if s2 == 0 else (),
                       sig=f"Smm{gl}_h{h}" if s2 == 1 else None)
                wSU = [f"Smm{gl}_h{h}", f"decay{P}"]
                if ti == 0 and c == 0 and h == 0:
                    wSU.append("S_init")
                op("v", lambda e, c=c, hsl=hsl: e.scalar_tensor_tensor(
                    S[:, hsl], S[:, hsl], decay[:, c:c + 1], pS[0:64],
                    ALU.mult, ALU.add),
                   waits=tuple(wSU), sig=f"SU{gl}_h{h}")
                op("s", lambda e, hsl=hsl: e.copy(Sb[:, hsl], S[:, hsl]),
                   waits=(f"SU{gl}_h{h}",), sig=f"SbU{gl}_h{h}")
            if c % 2 == 1:
                op("q", lambda e, t0=t0, c=c: e.dma_start(
                    out=o_oL[t0 + (c - 1) * CH:t0 + (c + 1) * CH, :], in_=oL_g),
                   waits=(f"oev{gl}_h1", f"oev{glm}_h1"),
                   sig=f"oL_st{gl}")

    lastP = f"_t{ntile-1}"
    op("q", lambda e: e.dma_start(out=o_Send, in_=S),
       waits=(f"SU{lastP}_c{NCHT-1}_h1",), sig="S_st")
    op("q", lambda e: e.dma_start(out=o_qeT, in_=qeT_all),
       waits=(f"qe{lastP}",), sig="qeT_st")
    op("q", lambda e: e.dma_start(out=o_E, in_=E_all),
       waits=(f"E{lastP}",), sig="E_st")

    a.reset()
    a.mark()
    wstage = a.alloc([128, I_FFN], F32)
    wq_out = a.alloc([128, I_FFN], BF16)
    last_gla = f"oL_st{lastP}_c{NCHT-1}"
    prev_st = None
    for wi, (dsrc, dout, rows, cols) in enumerate([
            (d_wg_sh, o_wgq, 352, H), (d_wu_sh, o_wuq, 352, H),
            (d_wd_sh, o_wdq, 128, I_FFN)]):
        nch = (rows + 127) // 128
        for cidx in range(nch):
            pr = min(128, rows - cidx * 128)
            nm = f"wq{wi}_{cidx}"
            wl = [last_gla, f"srw_st{lastP}_g3", f"xnT_st{lastP}",
                  f"vt_ev{lastP}_c{NCHT-1}_h1", f"vmul{lastP}_d15",
                  f"kd{lastP}", f"bacc{lastP}"]
            if prev_st:
                wl.append(prev_st)
            op("q", lambda e, dsrc=dsrc, cidx=cidx, pr=pr, cols=cols:
               e.dma_start(out=wstage[0:pr, 0:cols],
                           in_=dsrc[cidx * 128:cidx * 128 + pr, :]),
               waits=tuple(wl), sig=nm + "_ld")
            evs = []
            npiece = 2 if cols == H else 4
            pw = cols // npiece
            for hw in range(npiece):
                w0 = hw * pw
                evs.append(emit_quant_tile(
                    prog, a, nm + f"_q{hw}", wstage[0:pr, w0:w0 + pw],
                    wq_out[0:pr, w0:w0 + pw], pr, pw, repm32,
                    (nm + "_ld", "c_repm32")))
            prev_st = op("q", lambda e, dout=dout, cidx=cidx, pr=pr, cols=cols:
                         e.dma_start(out=dout[cidx * 128:cidx * 128 + pr, :],
                                     in_=wq_out[0:pr, 0:cols]),
                         waits=tuple(evs), sig=nm + "_st")

    a.reset()
    for g in reversed(pg):
        g.__exit__(None, None, None)
    ends = dict(
        S_st=prog._n("S_st"), qeT_st=prog._n("qeT_st"), E_st=prog._n("E_st"),
        wq_st=prog._n(prev_st) if not prev_st.startswith(prog.ns) else prev_st,
        oL_st=prog._n(f"oL_st{lastP}_c{NCHT-1}"),
        xn_st=prog._n(f"xn_st{lastP}_g3"), xnT_st=prog._n(f"xnT_st{lastP}"),
        srw_st=prog._n(f"srw_st{lastP}_g3"),
        SbU=prog._n(f"SbU{lastP}_c{NCHT-1}_h1"),
        E=prog._n(f"E{lastP}"), bacc=prog._n(f"bacc{lastP}"),
        wg0=prog._n("wgq_sh_nm") if False else None,
    )
    ends = {k: v for k, v in ends.items() if v}
    return ends


def build_phase_a(ntile=NTILE, debug=False):
    nc = bass.Bass("TRN2", target_bir_lowering=False)
    prog = Prog()
    a = Arena(nc, "arena", 102 * 1024)
    mk = make_mk(nc, {}, False)
    ends = emit_phase_a(nc, prog, a, mk, ntile=ntile, debug=debug)
    prog.op("p", lambda e: e.nop(), waits=tuple(ends.values()))
    with nc.allow_low_precision(reason="bf16 pipeline per problem statement"):
        prog.emit(nc)
    a.close()
    return nc


def emit_phase_b(nc, prog, a, mk, ntile=NTILE, fused=False, qeT_buf=None,
                 E_buf=None, d_Sgather=None, cc_ev=None):
    d_xnT = mk("xnT", [H, TC], BF16, "ExternalInput")
    d_oL = mk("oL", [TC, DI], BF16, "ExternalInput")
    d_srwT = mk("srwT", [TC, DI], BF16, "ExternalInput")
    if not fused:
        d_qeT = mk("qeT", [64, TC], BF16, "ExternalInput")
        d_E = mk("Ecols", [64, 64], F32, "ExternalInput")
        d_Sin = mk("Sin", [64, DI], F32, "ExternalInput")
    d_sel = mk("sel", [64, 8], F32, "ExternalInput") if fused else None
    d_gateT = mk("gateT", [H, DI], F32, "ExternalInput")
    d_downT = mk("downT", [DI, H], F32, "ExternalInput")
    d_nw = mk("nw", [128, 16], F32, "ExternalInput")
    d_ident = mk("ident", [128, 128], BF16, "ExternalInput")
    o_s = mk("s_out", [TC, H], F32, "ExternalOutput")

    op = prog.op

    gateT = a.alloc([128, 8, DI], BF16)
    downT = a.alloc([128, 16, H], BF16)
    nw = a.alloc([128, 16], F32)
    ident = a.alloc([128, 128], BF16)
    SinB = a.alloc([64, DI], BF16)
    qeT_all = qeT_buf if qeT_buf is not None else a.alloc([64, TC], BF16)
    E_all = E_buf if E_buf is not None else a.alloc([64, 64], F32)
    qeC = a.alloc([64, 64], BF16)
    SinF = a.alloc([64, DI], F32)
    selb = a.alloc([64, 8], F32)
    wstage = a.alloc([128, DI], F32)
    Gbuf = wstage[0:64, 0:DI]
    xnT_t = a.alloc([128, 8, TT], BF16)
    oL_g = a.alloc([128, DI], BF16)
    srw_g = a.alloc([128, DI], BF16)
    o2a = a.alloc([128, DI], BF16)
    o2af = a.alloc([128, DI], F32)
    sfg = a.alloc([128, DI], F32)
    o3 = a.alloc([128, DI], F32)
    y = a.alloc([128, DI], BF16)
    yT = a.alloc([128, 16, 128], BF16)
    s_g = a.alloc([128, H], F32)
    ssq = a.alloc([128, 1], F32)

    pg = []
    def palloc(name, shape, dtype):
        g = nc.psum_tensor(prog.ns + name, shape, dtype)
        t = g.__enter__(); pg.append(g)
        return t[:]
    pA = palloc("pA", [128, 512], F32)
    pB = palloc("pB", [128, 512], F32)
    pC = palloc("pC", [64, 1024], F32)
    pS = palloc("pS", [128, 1024], F32)
    pTR = palloc("pTR", [128, 1024], BF16)

    op("q", lambda e: e.dma_start(out=ident, in_=d_ident), sig="c_id")
    if not fused:
        op("q", lambda e: e.dma_start(out=E_all, in_=d_E), sig="c_E")
        op("q", lambda e: e.dma_start(out=qeT_all, in_=d_qeT), sig="c_qeT")
        op("q", lambda e: e.dma_start(out=wstage[0:64, 0:DI], in_=d_Sin),
           sig="sin_ld")
        op("s", lambda e: e.copy(SinB, wstage[0:64, 0:DI]), waits=("sin_ld",),
           sig="sinb")
    else:
        op("q", lambda e: e.dma_start(out=selb, in_=d_sel), sig="sel_ld")
        op("v", lambda e: e.memset(SinF, 0.0), sig="sin_z")
        for j in range(8):
            op("q", lambda e, j=j: e.dma_start(
                out=Gbuf, in_=d_Sgather[64 * j:64 * (j + 1), :]),
               waits=(cc_ev,) if j == 0 else (f"sacc_{j-1}",),
               sig=f"sg_ld_{j}")
            op("v", lambda e, j=j: e.scalar_tensor_tensor(
                SinF, Gbuf, selb[:, j:j + 1], SinF, ALU.mult, ALU.add),
               waits=(f"sg_ld_{j}", "sel_ld", "sin_z"), sig=f"sacc_{j}")
        op("s", lambda e: e.copy(SinB, SinF), waits=("sacc_7",), sig="sinb")
    op("q", lambda e: e.dma_start(out=nw, in_=d_nw), sig="c_nw")
    ev_gw = emit_cast_weight(prog, "wgate", d_gateT, gateT, H, DI, wstage,
                             prev="sinb")
    ev_dw = emit_cast_weight(prog, "wdown", d_downT, downT, DI, H, wstage,
                             prev=ev_gw)
    for d in range(16):
        op("v", lambda e, d=d: e.tensor_scalar(downT[:, d, :], downT[:, d, :],
                                               nw[:, d:d + 1], None, ALU.mult),
           waits=(ev_dw, "c_nw") if d == 0 else (),
           sig="nwfold" if d == 15 else None)

    for ti in range(ntile):
        t0 = ti * TT
        P = f"_t{ti}"
        Pm = f"_t{ti-1}"
        wl = [f"fgmm{Pm}_g3_n3"] if ti > 0 else []
        op("q", lambda e, t0=t0: e.dma_start(
            out=xnT_t,
            in_=d_xnT.rearrange("(c p) t -> p c t", p=128)[:, :, t0:t0 + TT]),
           waits=tuple(wl), sig=f"xnT_ld{P}")
        for g in range(4):
            gl = f"{P}_g{g}"
            glm = f"{P}_g{g-1}" if g > 0 else f"{Pm}_g3"
            gt0 = t0 + g * GRP
            wld = [f"o2a{glm}"] if (g > 0 or ti > 0) else []
            op("q", lambda e, gt0=gt0: e.dma_start(
                out=oL_g, in_=d_oL[gt0:gt0 + GRP, :]),
               waits=tuple(wld), sig=f"oL_ld{gl}")
            op("q", lambda e, gt0=gt0: e.dma_start(
                out=srw_g, in_=d_srwT[gt0:gt0 + GRP, :]),
               sig=f"srw_ld{gl}")
            wo2 = [f"oL_ld{gl}", f"srw_ld{gl}"]
            if g > 0 or ti > 0:
                wo2.append(f"corr{glm}_c1_h1")
            op("v", lambda e: e.tensor_tensor(o2a, oL_g, srw_g, ALU.add),
               waits=tuple(wo2), sig=f"o2ab{gl}")
            op("s", lambda e: e.copy(o2af, o2a), waits=(f"o2ab{gl}",),
               sig=f"o2a{gl}")
            for ns in range(4):
                pp = [pA, pB][ns % 2]
                wf = [f"xnT_ld{P}"]
                if ns >= 2:
                    wf.append(f"sfg{gl}_n{ns-2}")
                elif g > 0 or ti > 0:
                    wf.append(f"sfg{glm}_n{ns + 2}")
                if ti == 0 and g == 0 and ns == 0:
                    wf.append(ev_gw)
                for c in range(8):
                    op("t", lambda e, g=g, ns=ns, c=c, pp=pp: e.matmul(
                        pp, xnT_t[:, c, g * GRP:(g + 1) * GRP],
                        gateT[:, c, ns * 512:(ns + 1) * 512],
                        start=(c == 0), stop=(c == 7)),
                       waits=tuple(wf) if c == 0 else (),
                       sig=f"fgmm{gl}_n{ns}" if c == 7 else None)
                wsf = [f"fgmm{gl}_n{ns}"]
                if g > 0 or ti > 0:
                    wsf.append(f"o3{glm}")
                    wsf.append(f"sq{glm}")
                op("s", lambda e, ns=ns, pp=pp: e.activation(
                    sfg[:, ns * 512:(ns + 1) * 512], pp, AF.Silu),
                   waits=tuple(wsf), sig=f"sfg{gl}_n{ns}")
            for c2 in range(2):
                Cg = (t0 + g * GRP) // CH + c2
                wq = [] if fused else ["c_qeT", "c_E"]
                if not (ti == 0 and g == 0 and c2 == 0):
                    wq.append(f"cmm{gl}_c0_h1" if c2 == 1 else
                              f"cmm{glm}_c1_h1")
                op("v", lambda e, Cg=Cg: e.tensor_scalar(
                    qeC, qeT_all[:, Cg * CH:(Cg + 1) * CH],
                    E_all[:, Cg:Cg + 1], None, ALU.mult),
                   waits=tuple(wq), sig=f"qeC{gl}_c{c2}")
                for h in range(2):
                    wc = [f"qeC{gl}_c{c2}", "sinb"]
                    if h == 1:
                        wc.append(f"corrc{gl}_c{c2}_h0")
                    elif not (ti == 0 and g == 0 and c2 == 0):
                        wc.append(f"corrc{gl}_c0_h1" if c2 == 1 else
                                  f"corrc{glm}_c1_h1")
                    for s2 in range(2):
                        op("t", lambda e, h=h, s2=s2: e.matmul(
                            pC[:, s2 * 512:(s2 + 1) * 512], qeC,
                            SinB[:, h * 1024 + s2 * 512:
                                 h * 1024 + (s2 + 1) * 512],
                            start=True, stop=True),
                           waits=tuple(wc) if s2 == 0 else (),
                           sig=f"cmm{gl}_c{c2}_h{h}" if s2 == 1 else None)
                    op("s", lambda e, c2=c2, h=h: e.copy(
                        o3[64 * c2:64 * c2 + 64][:, h * 1024:(h + 1) * 1024],
                        pC),
                       waits=(f"cmm{gl}_c{c2}_h{h}",),
                       sig=f"corrc{gl}_c{c2}_h{h}")
                    op("v", lambda e, c2=c2, h=h: e.tensor_tensor(
                        o3[64 * c2:64 * c2 + 64][:, h * 1024:(h + 1) * 1024],
                        o3[64 * c2:64 * c2 + 64][:, h * 1024:(h + 1) * 1024],
                        o2af[64 * c2:64 * c2 + 64][:, h * 1024:(h + 1) * 1024],
                        ALU.add),
                       waits=(f"corrc{gl}_c{c2}_h{h}", f"o2a{gl}"),
                       sig=f"corr{gl}_c{c2}_h{h}")
            op("v", lambda e: e.memset(ssq, 0.0))
            op("v", lambda e: e.tensor_tensor(o3, o3, sfg, ALU.mult),
               waits=(f"corr{gl}_c1_h1", f"corr{gl}_c0_h1", f"sfg{gl}_n3"),
               sig=f"o3{gl}")
            op("s", lambda e: e.activation(sfg, o3, AF.Square, accum_out=ssq),
               waits=(f"o3{gl}",), sig=f"sq{gl}")
            op("v", lambda e: e.tensor_scalar(ssq, ssq, 1.0 / DI, NORM_EPS,
                                              ALU.mult, ALU.add),
               waits=(f"sq{gl}",), sig=f"sqs{gl}")
            op("v", lambda e: e.reciprocal(ssq, ssq), waits=(f"sqs{gl}",),
               sig=f"rr{gl}")
            op("s", lambda e: e.activation(ssq, ssq, AF.Sqrt),
               waits=(f"rr{gl}",), sig=f"rs{gl}")
            wy = [f"rs{gl}"]
            if g > 0 or ti > 0:
                wy.append(f"yT_ev{glm}_h1")
            op("v", lambda e: e.tensor_scalar(y, o3, ssq[:, 0:1], None,
                                              ALU.mult),
               waits=tuple(wy), sig=f"y{gl}")
            for hh in range(2):
                for jj in range(8):
                    j = hh * 8 + jj
                    wtr = []
                    if jj == 0:
                        wtr = [f"y{gl}"]
                        if hh == 1:
                            wtr.append(f"yT_ev{gl}_h0")
                        elif g > 0 or ti > 0:
                            wtr.append(f"yT_ev{glm}_h1")
                        if ti == 0 and g == 0 and j == 0:
                            wtr.append("c_id")
                    op("t", lambda e, j=j, jj=jj: e.transpose(
                        pTR[:, jj * 128:(jj + 1) * 128],
                        y[:, j * 128:(j + 1) * 128], ident),
                       waits=tuple(wtr),
                       sig=f"yTtr{gl}_h{hh}" if jj == 7 else None)
                whh = [f"yTtr{gl}_h{hh}"]
                if hh == 0 and (g > 0 or ti > 0):
                    whh.append(f"dmm{glm}_n1")
                op("v", lambda e, hh=hh: e.tensor_copy(
                    yT[:, hh * 8:(hh + 1) * 8, :],
                    pTR.rearrange("p (c t) -> p c t", t=128)),
                   waits=tuple(whh), sig=f"yT_ev{gl}_h{hh}")
            for ns in range(2):
                wd = [f"yT_ev{gl}_h1", "nwfold"]
                if g > 0 or ti > 0:
                    wd.append(f"s_ev{glm}")
                for c in range(16):
                    op("t", lambda e, ns=ns, c=c: e.matmul(
                        pS[:, ns * 512:(ns + 1) * 512], yT[:, c, :],
                        downT[:, c, ns * 512:(ns + 1) * 512],
                        start=(c == 0), stop=(c == 15)),
                       waits=tuple(wd) if (c == 0 and ns == 0) else (),
                       sig=f"dmm{gl}_n{ns}" if c == 15 else None)
            wse = [f"dmm{gl}_n0", f"dmm{gl}_n1"]
            if g > 0 or ti > 0:
                wse.append(f"s_st{glm}")
            op("s", lambda e: e.copy(s_g, pS), waits=tuple(wse),
               sig=f"s_ev{gl}")
            op("q", lambda e, gt0=gt0: e.dma_start(
                out=o_s[gt0:gt0 + GRP, :], in_=s_g),
               waits=(f"s_ev{gl}",), sig=f"s_st{gl}")

    for g in reversed(pg):
        g.__exit__(None, None, None)
    lb = f"_t{ntile-1}_g3"
    return dict(s_st=prog._n(f"s_st{lb}"), s_ev=prog._n(f"s_ev{lb}"),
                yT=prog._n(f"yT_ev{lb}_h1"), sq=prog._n(f"sq{lb}"),
                o3=prog._n(f"o3{lb}"))


def build_phase_b(ntile=NTILE):
    nc = bass.Bass("TRN2", target_bir_lowering=False)
    prog = Prog()
    a = Arena(nc, "arena", 100 * 1024)
    mk = make_mk(nc, {}, False)
    ends = emit_phase_b(nc, prog, a, mk, ntile=ntile)
    prog.op("p", lambda e: e.nop(), waits=tuple(ends.values()))
    with nc.allow_low_precision(reason="bf16 pipeline per problem statement"):
        prog.emit(nc)
    a.close()
    return nc


def emit_phase_c(nc, prog, a, mk, ngrp=TC // GRP, fused=False, wq_evs=()):
    d_xn = mk("xn_tm", [TC, H], F32, "ExternalInput")
    d_s = mk("s_out" if fused else "s_in", [TC, H], F32, "ExternalInput")
    d_x = mk("x", [TC + 3, H], F32, "ExternalInput")
    if fused:
        d_wgq = mk("wgq_g", [I_FFN, H], BF16, "ExternalInput")
        d_wuq = mk("wuq_g", [I_FFN, H], BF16, "ExternalInput")
        d_wdq = mk("wdq_g", [H, I_FFN], BF16, "ExternalInput")
    else:
        d_wgq = mk("wgq", [I_FFN, H], BF16, "ExternalInput")
        d_wuq = mk("wuq", [I_FFN, H], BF16, "ExternalInput")
        d_wdq = mk("wdq", [H, I_FFN], BF16, "ExternalInput")
    d_repm32 = mk("repm32c", [128, 704], F32, "ExternalInput")
    d_ident = mk("ident", [128, 128], BF16, "ExternalInput")
    o_out = mk("out", [TC, H], F32, "ExternalOutput")

    op = prog.op

    wgqT = a.alloc([128, 8, I_FFN], BF16)
    wuqT = a.alloc([128, 8, I_FFN], BF16)
    wdqT = a.alloc([128, 22, H], BF16)
    repm32 = a.alloc([128, 704], F32)
    ident = a.alloc([128, 128], BF16)
    xn_g = a.alloc([128, H], F32)
    x_g = a.alloc([128, H], F32)
    s_gc = a.alloc([128, H], F32)
    xq = a.alloc([128, H], BF16)
    hbuf = a.alloc([128, I_FFN], F32)
    hq = a.alloc([128, I_FFN], BF16)
    hqT = a.alloc([128, 22, 128], BF16)
    xqT = hqT[:, 0:8, :]
    g_g = a.alloc([128, H], F32)
    sg1 = g_g[:, 0:512]
    wld = hbuf.bitcast(BF16)[:, 0:I_FFN]
    t1 = xn_g

    pg = []
    def palloc(name, shape, dtype):
        g = nc.psum_tensor(prog.ns + name, shape, dtype)
        t = g.__enter__(); pg.append(g)
        return t[:]
    pA = palloc("pA", [128, 512], F32)
    pB = palloc("pB", [128, 512], F32)
    pS = palloc("pS", [128, 1024], F32)
    pTR = palloc("pTR", [128, 1024], BF16)

    op("q", lambda e: e.dma_start(out=repm32, in_=d_repm32), sig="c_repm")
    op("q", lambda e: e.dma_start(out=ident, in_=d_ident), sig="c_id")

    def wtrans(nm, dram, dst, rows, cols, prev_ev):
        nrc = rows // 128
        ncc = cols // 128
        last_evac = prev_ev
        for rc in range(nrc):
            wlw = [f"{nm}_r{rc-1}_done"] if rc > 0 else (
                ([prev_ev] if prev_ev else []) + list(wq_evs))
            op("q", lambda e, rc=rc, cols=cols: e.dma_start(
                out=wld[:, 0:cols], in_=dram[rc * 128:(rc + 1) * 128, :]),
               waits=tuple(wlw), sig=f"{nm}_r{rc}_ld")
            evs = []
            for cc in range(ncc):
                jj = cc % 8
                wt = []
                if jj == 0:
                    wt = [f"{nm}_r{rc}_ld", "c_id"]
                    if cc >= 8:
                        wt.append(f"{nm}_r{rc}_e{cc - 8}")
                    elif last_evac:
                        wt.append(last_evac)
                op("t", lambda e, cc=cc, jj=jj: e.transpose(
                    pTR[:, jj * 128:(jj + 1) * 128],
                    wld[:, cc * 128:(cc + 1) * 128], ident),
                   waits=tuple(wt),
                   sig=f"{nm}_r{rc}_t{cc}" if (jj == 7 or cc == ncc - 1)
                   else None)
                if jj == 7 or cc == ncc - 1:
                    c0 = (cc // 8) * 8
                    nn = cc - c0 + 1
                    ev = op("v", lambda e, rc=rc, c0=c0, nn=nn:
                            e.tensor_copy(
                                dst[:, c0:c0 + nn, rc * 128:(rc + 1) * 128],
                                pTR[:, 0:nn * 128].rearrange(
                                    "p (c t) -> p c t", t=128)),
                            waits=(f"{nm}_r{rc}_t{cc}",),
                            sig=f"{nm}_r{rc}_e{c0}")
                    evs.append(ev)
                    last_evac = ev
            op("v", lambda e: e.tensor_copy(xn_g[0:1, 0:1], wld[0:1, 0:1]),
               waits=tuple(evs), sig=f"{nm}_r{rc}_done")
        return last_evac, f"{nm}_r{nrc-1}_done"

    ev_wg_e, ev_wg = wtrans("twg", d_wgq, wgqT, I_FFN, H,
                            wq_evs[0] if wq_evs else None)
    ev_wu_e, ev_wu = wtrans("twu", d_wuq, wuqT, I_FFN, H, ev_wg)
    ev_wd_e, ev_wd = wtrans("twd", d_wdq, wdqT, H, I_FFN, ev_wu)

    ISL = [(0, 512), (512, 1024), (1024, 1536), (1536, 2048), (2048, 2560),
           (2560, 2816)]

    for gi in range(ngrp):
        gl = f"_g{gi}"
        glm = f"_g{gi-1}"
        gt0 = gi * GRP
        wxn = [f"xq{glm}_h1_f"] if gi > 0 else [ev_wd]
        op("q", lambda e, gt0=gt0: e.dma_start(out=xn_g,
                                               in_=d_xn[gt0:gt0 + GRP, :]),
           waits=tuple(wxn), sig=f"xn_ld{gl}")
        wx = [f"out_st{glm}"] if gi > 0 else []
        op("q", lambda e, gt0=gt0: e.dma_start(
            out=x_g, in_=d_x[3 + gt0:3 + gt0 + GRP, :]),
           waits=tuple(wx), sig=f"x_ld{gl}")
        ws = [f"t1{glm}"] if gi > 0 else []
        op("q", lambda e, gt0=gt0: e.dma_start(out=s_gc,
                                               in_=d_s[gt0:gt0 + GRP, :]),
           waits=tuple(ws), sig=f"s_ld{gl}")
        for hw in range(2):
            w0 = hw * 512
            ev_xq = emit_quant_tile(
                prog, a, f"xq{gl}_h{hw}_f", xn_g[:, w0:w0 + 512],
                xq[:, w0:w0 + 512], 128, 512, repm32,
                (f"xn_ld{gl}", "c_repm"))
        for j in range(8):
            wt = []
            if j == 0:
                wt = [ev_xq]
                wt.append(f"hqT_ev{glm}_b2" if gi > 0 else ev_wd_e)
            op("t", lambda e, j=j: e.transpose(
                pTR[:, j * 128:(j + 1) * 128], xq[:, j * 128:(j + 1) * 128],
                ident),
               waits=tuple(wt), sig=f"xqtr{gl}" if j == 7 else None)
        wxv = [f"xqtr{gl}"]
        if gi > 0:
            wxv += [f"gumm{glm}_s5", f"ffnmm{glm}_n1"]
        op("v", lambda e: e.tensor_copy(
            xqT, pTR.rearrange("p (c t) -> p c t", t=128)),
           waits=tuple(wxv), sig=f"xqT_ev{gl}")
        for si, (i0, i1) in enumerate(ISL):
            wm = [f"xqT_ev{gl}"]
            if si == 0:
                if gi > 0:
                    wm.append(f"h_ev{glm}_s5")
            else:
                wm.append(f"h_ev{gl}_s{si-1}")
            for c in range(8):
                op("t", lambda e, c=c, i0=i0, i1=i1: e.matmul(
                    pA[:, 0:i1 - i0], xqT[:, c, :], wgqT[:, c, i0:i1],
                    start=(c == 0), stop=(c == 7)),
                   waits=tuple(wm) if c == 0 else ())
            for c in range(8):
                op("t", lambda e, c=c, i0=i0, i1=i1: e.matmul(
                    pB[:, 0:i1 - i0], xqT[:, c, :], wuqT[:, c, i0:i1],
                    start=(c == 0), stop=(c == 7)),
                   sig=f"gumm{gl}_s{si}" if c == 7 else None)
            wsg = [f"gumm{gl}_s{si}"]
            op("s", lambda e, i0=i0, i1=i1: e.activation(
                sg1[:, 0:i1 - i0], pA[:, 0:i1 - i0], AF.Silu),
               waits=tuple(wsg), sig=f"sg{gl}_s{si}")
            op("v", lambda e, i0=i0, i1=i1: e.tensor_tensor(
                hbuf[:, i0:i1], sg1[:, 0:i1 - i0], pB[:, 0:i1 - i0], ALU.mult),
               waits=(f"sg{gl}_s{si}",), sig=f"h_ev{gl}_s{si}")
        ev_hq = []
        for hw in range(4):
            w0 = hw * (I_FFN // 4)
            ev_hq.append(emit_quant_tile(
                prog, a, f"hq{gl}_f{hw}", hbuf[:, w0:w0 + I_FFN // 4],
                hq[:, w0:w0 + I_FFN // 4], 128, I_FFN // 4, repm32,
                (f"h_ev{gl}_s5",)))
        for j in range(22):
            jj = j % 8
            wt = []
            if jj == 0:
                wt = list(ev_hq)
                if j > 0:
                    wt.append(f"hqT_ev{gl}_b{j // 8 - 1}")
                else:
                    wt.append(f"xqT_ev{gl}")
            op("t", lambda e, j=j, jj=jj: e.transpose(
                pTR[:, jj * 128:(jj + 1) * 128], hq[:, j * 128:(j + 1) * 128],
                ident),
               waits=tuple(wt),
               sig=f"hqtr{gl}_b{j // 8}" if (jj == 7 or j == 21) else None)
            if jj == 7 or j == 21:
                c0 = (j // 8) * 8
                nn = j - c0 + 1
                whv = [f"hqtr{gl}_b{j // 8}"]
                if j // 8 == 0 and gi > 0:
                    whv.append(f"ffnmm{glm}_n1")
                op("v", lambda e, c0=c0, nn=nn: e.tensor_copy(
                    hqT[:, c0:c0 + nn, :],
                    pTR[:, 0:nn * 128].rearrange("p (c t) -> p c t", t=128)),
                   waits=tuple(whv), sig=f"hqT_ev{gl}_b{j // 8}")
        for ns in range(2):
            wd = [f"hqT_ev{gl}_b2"]
            if gi > 0 and ns == 0:
                wd.append(f"g_ev{glm}")
            for c in range(22):
                op("t", lambda e, ns=ns, c=c: e.matmul(
                    pS[:, ns * 512:(ns + 1) * 512], hqT[:, c, :],
                    wdqT[:, c, ns * 512:(ns + 1) * 512],
                    start=(c == 0), stop=(c == 21)),
                   waits=tuple(wd) if c == 0 else (),
                   sig=f"ffnmm{gl}_n{ns}" if c == 21 else None)
        wgv = [f"ffnmm{gl}_n0", f"ffnmm{gl}_n1"]
        if gi > 0:
            wgv.append(f"t1{glm}")
        op("s", lambda e: e.activation(g_g, pS, AF.Silu), waits=tuple(wgv),
           sig=f"g_ev{gl}")
        op("v", lambda e: e.tensor_tensor(t1, s_gc, g_g, ALU.mult),
           waits=(f"g_ev{gl}", f"s_ld{gl}"), sig=f"t1{gl}")
        op("v", lambda e: e.tensor_tensor(x_g, t1, x_g, ALU.add),
           waits=(f"x_ld{gl}",), sig=f"out{gl}")
        op("q", lambda e, gt0=gt0: e.dma_start(out=o_out[gt0:gt0 + GRP, :],
                                               in_=x_g),
           waits=(f"out{gl}",), sig=f"out_st{gl}")

    for g in reversed(pg):
        g.__exit__(None, None, None)
    return dict(out_st=prog._n(f"out_st_g{ngrp-1}"))


def build_phase_c(ngrp=TC // GRP):
    nc = bass.Bass("TRN2", target_bir_lowering=False)
    prog = Prog()
    a = Arena(nc, "arena", 102 * 1024)
    mk = make_mk(nc, {}, False)
    ends = emit_phase_c(nc, prog, a, mk, ngrp=ngrp)
    prog.op("p", lambda e: e.nop(), waits=tuple(ends.values()))
    with nc.allow_low_precision(reason="bf16 pipeline per problem statement"):
        prog.emit(nc)
    a.close()
    return nc


def build_fused(ntile=NTILE, ngrp=TC // GRP):
    nc = bass.Bass("TRN2", target_bir_lowering=False)
    prog = Prog()
    a = Arena(nc, "arena", 102 * 1024)
    cache = {}
    mk = make_mk(nc, cache, True)

    qeT_all = a.alloc([64, TC], BF16)
    E_all = a.alloc([64, 64], F32)
    a.mark()

    prog.ns = "A."
    ends_a = emit_phase_a(nc, prog, a, mk, ntile=ntile, fused=True,
                          qeT_buf=qeT_all, E_buf=E_all)
    prog.ns = ""

    d_Send = cache["Send"]
    d_Sg = mk("Sgather", [64 * NCORE, DI], F32, "Internal")
    d_wgg = mk("wgq_g", [I_FFN, H], BF16, "Internal")
    d_wug = mk("wuq_g", [I_FFN, H], BF16, "Internal")
    d_wdg = mk("wdq_g", [H, I_FFN], BF16, "Internal")
    rg = [list(range(NCORE))]
    prog.op("pc", lambda e: e.collective_compute(
        "AllGather", ALU.bypass, replica_groups=rg, ins=[d_Send.opt()],
        outs=[d_Sg.opt()]), waits=(ends_a["S_st"],), sig="@cc_S")
    prog.op("pc", lambda e: e.collective_compute(
        "AllGather", ALU.bypass, replica_groups=rg,
        ins=[cache["wgq_sh"].opt()],
        outs=[d_wgg.opt()]), waits=(ends_a["wq_st"],), sig="@cc_wg")
    prog.op("pc", lambda e: e.collective_compute(
        "AllGather", ALU.bypass, replica_groups=rg,
        ins=[cache["wuq_sh"].opt()],
        outs=[d_wug.opt()]), sig="@cc_wu")
    prog.op("pc", lambda e: e.collective_compute(
        "AllGather", ALU.bypass, replica_groups=rg,
        ins=[cache["wdq_sh"].opt()],
        outs=[d_wdg.opt()]), sig="@cc_wd")

    prog.barrier("ab", list(ends_a.values()))
    a.reset()
    a.mark()

    prog.ns = "B."
    ends_b = emit_phase_b(nc, prog, a, mk, ntile=ntile, fused=True,
                          qeT_buf=qeT_all, E_buf=E_all, d_Sgather=d_Sg,
                          cc_ev="@cc_S")
    prog.ns = ""

    prog.barrier("bc", list(ends_b.values()))
    a.reset()
    a.clear()

    prog.ns = "C."
    ends_c = emit_phase_c(nc, prog, a, mk, ngrp=ngrp, fused=True,
                          wq_evs=("@cc_wg", "@cc_wu", "@cc_wd"))
    prog.ns = ""

    prog.op("p", lambda e: e.nop(), waits=tuple(ends_c.values()))
    with nc.allow_low_precision(reason="bf16 pipeline per problem statement"):
        prog.emit(nc)
    a.close()
    return nc


def _host_prep(inputs):
    f = lambda k: np.ascontiguousarray(inputs[k], dtype=np.float32)
    up_w = f("L0_attn_up_w"); gate_w = f("L0_attn_gate_w")
    conv_w = f("L0_attn_conv_w"); conv_b = f("L0_attn_conv_b")
    q_w = f("L0_attn_q_w"); k_w = f("L0_attn_k_w")
    g_w = f("L0_attn_g_w"); g_b = f("L0_attn_g_b")
    tau_w = f("L0_attn_tau_w"); tau_b = f("L0_attn_tau_b")
    ig_w1 = f("L0_attn_ig_w1"); ig_w2 = f("L0_attn_ig_w2")
    ig_b2 = f("L0_attn_ig_b2"); rw_ = f("L0_attn_residual_weight")
    nw_ = f("L0_attn_norm_weight"); down_w = f("L0_attn_down_w")

    c = {}
    c["upT"] = np.ascontiguousarray(up_w.T)
    c["qkT"] = np.ascontiguousarray(np.concatenate([q_w.T, k_w.T], axis=1))
    c["gtT"] = np.ascontiguousarray(np.concatenate([g_w.T, tau_w.T], axis=1))
    c["ig1T"] = np.ascontiguousarray(ig_w1.T)
    c["ig2T"] = np.ascontiguousarray(ig_w2.T)
    c["convw"] = np.ascontiguousarray(
        conv_w.reshape(16, 128, 4).transpose(1, 0, 2).reshape(128, 64))
    c["convb"] = np.ascontiguousarray(conv_b.reshape(16, 128).T)
    c["igb2"] = np.ascontiguousarray(ig_b2.reshape(16, 128).T)
    c["rw"] = np.ascontiguousarray(rw_.reshape(16, 128).T)
    c["nw"] = np.ascontiguousarray(nw_.reshape(16, 128).T)
    c["gb"] = np.ascontiguousarray(g_b.reshape(64, 1))
    c["tb"] = np.ascontiguousarray(tau_b.reshape(64, 1))
    c["ident"] = np.eye(128, dtype=BF)
    c["id64f"] = np.eye(64, dtype=np.float32)
    jj, ii = np.meshgrid(np.arange(64), np.arange(64), indexing="ij")
    c["maskLT"] = (ii >= jj).astype(np.float32)
    c["ones64"] = np.ones((64, 64), np.float32)
    c["ones1x"] = np.ones((1, 64), dtype=BF)
    rm = np.ones((128, I_FFN), dtype=BF)
    rm[:, 0::16] = 0
    c["repm"] = rm
    rm32 = np.ones((128, 1024), dtype=np.float32)
    rm32[:, 0::16] = 0
    c["repm32"] = rm32
    c["gateT"] = np.ascontiguousarray(gate_w.T)
    c["downT"] = np.ascontiguousarray(down_w.T)
    c["wg"] = f("L0_ffn_w_gate"); c["wu"] = f("L0_ffn_w_up")
    c["wd"] = f("L0_ffn_w_down")
    c["x"] = f("L0_attn_hidden_states")
    return c


def _shard_x(x):
    out = []
    for cid in range(NCORE):
        b = cid // 2
        t0 = (cid % 2) * TC
        if t0 == 0:
            sl = np.concatenate([np.zeros((3, H), np.float32), x[b, 0:TC]], 0)
        else:
            sl = x[b, t0 - 3:t0 + TC]
        out.append(np.ascontiguousarray(sl, dtype=np.float32))
    return out


def kernel(**inputs):
    c = _host_prep(inputs)
    xs = _shard_x(c["x"])

    nc = build_fused()
    in_maps = []
    for cid in range(NCORE):
        r0 = cid * 352
        rd0 = cid * 128
        sel = np.zeros((64, 8), np.float32)
        if cid % 2 == 1:
            sel[:, cid - 1] = 1.0
        in_maps.append(dict(
            x=xs[cid], upT=c["upT"], qkT=c["qkT"], gtT=c["gtT"],
            ig1T=c["ig1T"], ig2T=c["ig2T"], convw=c["convw"], convb=c["convb"],
            igb2=c["igb2"], rw=c["rw"], gb=c["gb"], tb=c["tb"],
            ident=c["ident"], id64f=c["id64f"], maskLT=c["maskLT"],
            ones64=c["ones64"], ones1x=c["ones1x"], repm32=c["repm32"],
            wg_sh=np.ascontiguousarray(c["wg"][r0:r0 + 352]),
            wu_sh=np.ascontiguousarray(c["wu"][r0:r0 + 352]),
            wd_sh=np.ascontiguousarray(c["wd"][rd0:rd0 + 128]),
            sel=sel, gateT=c["gateT"], downT=c["downT"], nw=c["nw"],
            repm32c=c["repm32"][:, 0:704].copy(),
        ))
    rc = run_bass_kernel_spmd(nc, in_maps, core_ids=list(range(NCORE))).results

    out = np.empty((B, T, H), np.float32)
    for cid in range(NCORE):
        b = cid // 2
        t0 = (cid % 2) * TC
        out[b, t0:t0 + TC] = rc[cid]["out"]
    return out


def kernel_unfused(**inputs):
    c = _host_prep(inputs)
    xs = _shard_x(c["x"])

    nc_a = build_phase_a()
    in_maps = []
    for cid in range(NCORE):
        r0 = cid * 352
        rd0 = cid * 128
        in_maps.append(dict(
            x=xs[cid], upT=c["upT"], qkT=c["qkT"], gtT=c["gtT"],
            ig1T=c["ig1T"], ig2T=c["ig2T"], convw=c["convw"], convb=c["convb"],
            igb2=c["igb2"], rw=c["rw"], gb=c["gb"], tb=c["tb"],
            ident=c["ident"], id64f=c["id64f"], maskLT=c["maskLT"],
            ones64=c["ones64"], ones1x=c["ones1x"],
            repm32=c["repm32"],
            wg_sh=np.ascontiguousarray(c["wg"][r0:r0 + 352]),
            wu_sh=np.ascontiguousarray(c["wu"][r0:r0 + 352]),
            wd_sh=np.ascontiguousarray(c["wd"][rd0:rd0 + 128]),
        ))
    ra = run_bass_kernel_spmd(nc_a, in_maps, core_ids=list(range(NCORE))).results

    wgq = np.ascontiguousarray(
        np.concatenate([ra[i]["wgq_sh"] for i in range(NCORE)], 0))
    wuq = np.ascontiguousarray(
        np.concatenate([ra[i]["wuq_sh"] for i in range(NCORE)], 0))
    wdq = np.ascontiguousarray(
        np.concatenate([ra[i]["wdq_sh"] for i in range(NCORE)], 0))

    nc_b = build_phase_b()
    in_maps = []
    for cid in range(NCORE):
        sin = (np.zeros((64, DI), np.float32) if cid % 2 == 0
               else np.ascontiguousarray(ra[cid - 1]["Send"], np.float32))
        in_maps.append(dict(
            xnT=ra[cid]["xnT"], oL=ra[cid]["oL"], srwT=ra[cid]["srwT"],
            qeT=ra[cid]["qeT"], Ecols=ra[cid]["Ecols"], Sin=sin,
            gateT=c["gateT"], downT=c["downT"], nw=c["nw"], ident=c["ident"]))
    rb = run_bass_kernel_spmd(nc_b, in_maps, core_ids=list(range(NCORE))).results

    nc_c = build_phase_c()
    in_maps = []
    for cid in range(NCORE):
        in_maps.append(dict(
            xn_tm=ra[cid]["xn_tm"], s_in=rb[cid]["s_out"], x=xs[cid],
            wgq=wgq, wuq=wuq, wdq=wdq, repm32c=c["repm32"][:, 0:704].copy(),
            ident=c["ident"]))
    rc = run_bass_kernel_spmd(nc_c, in_maps, core_ids=list(range(NCORE))).results

    out = np.empty((B, T, H), np.float32)
    for cid in range(NCORE):
        b = cid // 2
        t0 = (cid % 2) * TC
        out[b, t0:t0 + TC] = rc[cid]["out"]
    return out


# version 08
# speedup vs baseline: 1.1761x; 1.1761x over previous
import numpy as np
import ml_dtypes
import concourse.bass as bass
import concourse.mybir as mybir
from concourse.bass_utils import run_bass_kernel_spmd

F32 = mybir.dt.float32
BF16 = mybir.dt.bfloat16
I16 = mybir.dt.int16
AF = mybir.ActivationFunctionType
ALU = mybir.AluOpType
AX = mybir.AxisListType
BF = ml_dtypes.bfloat16

B, T, H = 4, 8192, 1024
DI = 2048
R = 16
CH = 64
NCORE = 8
TC = T // 2
I_FFN = 2816
TT = 512
NTILE = TC // TT
NCHT = TT // CH
GRP = 128
SCALE = 0.125
LN_EPS = 1e-5
NORM_EPS = 1e-5


class Prog:

    def __init__(self):
        self.ops = []
        self.events = {}
        self.counts = {"q": 0, "t": 0, "v": 0, "s": 0, "p": 0, "pc": 0}
        self.ns = ""

    def _n(self, name):
        if name.startswith("@"):
            return name
        if self.ns and name.startswith(self.ns):
            return name
        return self.ns + name

    def op(self, eng, fn, waits=(), sig=None):
        if isinstance(waits, str):
            waits = (waits,)
        waits = tuple(self._n(w) for w in waits if w is not None)
        if sig is not None:
            sig = self._n(sig)
        for w in waits:
            assert w in self.events, f"wait on unplanned event {w}"
        self.ops.append((eng, fn, waits, sig))
        if sig is not None:
            self.counts[eng] += 16 if eng == "q" else 1
            assert sig not in self.events, f"dup event {sig}"
            self.events[sig] = (eng, self.counts[eng])
        return sig

    def barrier(self, tag, end_events):
        for eng in ("q", "t", "v", "s", "p"):
            self.op(eng, lambda e: e.nop(), waits=tuple(end_events),
                    sig=f"@bar_{tag}_{eng}")
        return [f"@bar_{tag}_{e}" for e in ("q", "t", "v", "s", "p")]

    def emit(self, nc):
        sems = {}
        ctxs = []
        for key in self.counts:
            g = nc.semaphore(f"sem_{key}")
            sems[key] = g.__enter__()
            ctxs.append(g)
        blk_g = nc.Block()
        block = blk_g.__enter__()
        ctxs.append(blk_g)

        name_map = {"q": "sync", "t": "tensor", "v": "vector", "s": "scalar",
                    "p": "gpsimd", "pc": "gpsimd"}
        streams = {v: [] for v in name_map.values()}
        for o in self.ops:
            streams[name_map[o[0]]].append(o)
        events = self.events

        def run_stream(eng_handle, ops):
            waited = {}
            for eng_key, fn, waits, sig in ops:
                for w in waits:
                    wk, wv = events[w]
                    if waited.get(wk, -1) < wv:
                        eng_handle.wait_ge(sems[wk], wv)
                        waited[wk] = wv
                inst = fn(eng_handle)
                if sig is not None:
                    ek, _ = events[sig]
                    inst.then_inc(sems[ek], 16 if ek == "q" else 1)

        for name in ("sync", "tensor", "vector", "scalar", "gpsimd"):
            ops = streams[name]
            if not ops:
                continue
            def make(oplist):
                def body(eng):
                    run_stream(eng, oplist)
                return body
            getattr(block, name)(make(ops))

        for g in reversed(ctxs):
            g.__exit__(None, None, None)


class Arena:
    def __init__(self, nc, name, cols_bf16):
        self.g = nc.sbuf_tensor(name, [128, cols_bf16], BF16)
        self.t = self.g.__enter__()
        self.cols = cols_bf16
        self.off = 0
        self.marks = []

    def alloc(self, shape, dtype=BF16):
        n = 1
        for d in shape[1:]:
            n *= d
        w = {BF16: 1, I16: 1, F32: 2}[dtype]
        if self.off % 2:
            self.off += 1
        start = self.off
        self.off += n * w
        assert self.off <= self.cols, f"arena overflow {self.off} > {self.cols}"
        view = self.t[:, start:start + n * w]
        if dtype != BF16:
            view = view.bitcast(dtype)
        if shape[0] != 128:
            view = view[0:shape[0]]
        if len(shape) > 2:
            names = " ".join(f"d{i}" for i in range(1, len(shape)))
            kw = {f"d{i}": shape[i] for i in range(2, len(shape))}
            view = view.rearrange(f"p ({names}) -> p {names}", **kw)
        return view

    def alloc_pair(self, shape, dtype=F32):
        assert shape[0] == 64
        full = self.alloc([128] + list(shape[1:]), dtype)
        return full[0:64], full[64:128]

    def clear(self):
        self.off = 0
        self.marks = []

    def mark(self):
        self.marks.append(self.off)

    def reset(self):
        self.off = self.marks.pop()

    def close(self):
        self.g.__exit__(None, None, None)


def emit_ln_group(prog, nm, xg, xng, ln_m, ln_v, ln_tr, P, waits):
    op = prog.op
    op("v", lambda e: e.tensor_reduce(ln_m[0:P], xg[0:P], AX.X, ALU.add),
       waits=waits, sig=nm + "_red")
    op("v", lambda e: e.tensor_scalar(ln_m[0:P], ln_m[0:P], -1.0 / H, None,
                                      ALU.mult), waits=(nm + "_red",),
       sig=nm + "_m")
    op("v", lambda e: e.memset(ln_v[0:P], 0.0))
    op("v", lambda e: e.tensor_scalar(ln_tr[0:P], xg[0:P], ln_m[0:P], None,
                                      ALU.add), waits=(nm + "_m",),
       sig=nm + "_c")
    op("s", lambda e: e.activation(xg[0:P], ln_tr[0:P], AF.Square,
                                   accum_out=ln_v[0:P]),
       waits=(nm + "_c",), sig=nm + "_sq")
    op("v", lambda e: e.tensor_scalar(ln_v[0:P], ln_v[0:P], 1.0 / H, LN_EPS,
                                      ALU.mult, ALU.add), waits=(nm + "_sq",),
       sig=nm + "_t")
    op("v", lambda e: e.reciprocal(ln_v[0:P], ln_v[0:P]),
       waits=(nm + "_t",), sig=nm + "_r")
    op("s", lambda e: e.activation(ln_v[0:P], ln_v[0:P], AF.Sqrt),
       waits=(nm + "_r",), sig=nm + "_rs")
    op("v", lambda e: e.tensor_scalar(xng[0:P], ln_tr[0:P], ln_v[0:P],
                                      None, ALU.mult),
       waits=(nm + "_rs",), sig=nm)
    op("v", lambda e: e.tensor_scalar(xg[0:P], ln_tr[0:P], ln_v[0:P],
                                      None, ALU.mult), sig=nm + "_f")
    return nm


QW_MAX = 704


def emit_quant_scratch(a):
    q = {}
    q["d1"] = a.alloc([128, QW_MAX], F32)
    q["d1b"] = a.alloc([128, QW_MAX], BF16)
    q["az"] = a.alloc([128, QW_MAX], F32)
    q["rcpx"] = a.alloc([128, QW_MAX], F32)
    q["scx"] = a.alloc([128, QW_MAX], BF16)
    q["zs"] = a.alloc([128, QW_MAX], F32)
    q["i1"] = a.alloc([128, QW_MAX], I16)
    q["i2"] = a.alloc([128, QW_MAX], I16)
    q["c1"] = a.alloc([128, QW_MAX], BF16)
    q["c2"] = a.alloc([128, QW_MAX], BF16)
    q["q1"] = a.alloc([128, QW_MAX], BF16)
    q["sgn"] = a.alloc([128, QW_MAX], BF16)
    q["bm1"] = a.alloc([128, QW_MAX // 16], F32)
    q["sc32"] = a.alloc([128, QW_MAX // 16], F32)
    q["rcp32"] = a.alloc([128, QW_MAX // 16], F32)
    q["scb"] = a.alloc([128, QW_MAX // 16], BF16)
    q["_prev"] = {}
    return q


def emit_quant_tile(prog, q, nm, xf, wout, P, W, repm32, repmb, waits,
                    fresh=False):
    NB = W // 16
    d1 = q["d1"][0:P, 0:W]
    d1b = q["d1b"][0:P, 0:W]
    az = q["az"][0:P, 0:W]
    rcpx = q["rcpx"][0:P, 0:W]
    scx = q["scx"][0:P, 0:W]
    zs = q["zs"][0:P, 0:W]
    i1 = q["i1"][0:P, 0:W]
    i2 = q["i2"][0:P, 0:W]
    c1 = q["c1"][0:P, 0:W]
    c2 = q["c2"][0:P, 0:W]
    q1 = q["q1"][0:P, 0:W]
    sgn = q["sgn"][0:P, 0:W]
    bm1 = q["bm1"][0:P, 0:NB]
    sc32 = q["sc32"][0:P, 0:NB]
    rcp32 = q["rcp32"][0:P, 0:NB]
    scb = q["scb"][0:P, 0:NB]
    pv = q["_prev"]

    state = {"last": None, "n": 0}

    def vop(fn, waits=(), sig=None):
        s_ = sig or f"{nm}_a{state['n']}"
        state["n"] += 1
        w = tuple(waits) + ((state["last"],) if state["last"] else ())
        prog.op("v", fn, waits=w, sig=s_)
        state["last"] = s_
        return s_

    def sop(fn, waits=(), sig=None):
        prog.op("s", fn, waits=tuple(waits), sig=sig)
        return sig

    sop(lambda e: e.activation(az, xf, AF.Abs),
        waits=tuple(waits) + pv.get("az", ()), sig=nm + "_az")
    sop(lambda e: e.activation(sgn, xf, AF.Sign),
        waits=pv.get("sgn", ()), sig=nm + "_sg")
    vop(lambda e: e.tensor_reduce(bm1.unsqueeze(2),
                                  xf.rearrange("p (nb bk) -> p nb bk", bk=16),
                                  AX.X, ALU.max, apply_absolute_value=True),
        waits=waits, sig=nm + "_bm")
    vop(lambda e: e.tensor_scalar(sc32, bm1, 1.0 / 6.0, 1e-12,
                                  ALU.mult, ALU.max), sig=nm + "_s32")
    vop(lambda e: e.reciprocal(rcp32, sc32), waits=(nm + "_s32",))
    vop(lambda e: e.tensor_copy(scb, sc32), waits=(nm + "_s32",))
    if fresh:
        vop(lambda e: e.memset(q["d1"][:], 0.0))
        vop(lambda e: e.memset(q["d1b"][:], 0.0))
    vop(lambda e: e.tensor_copy(
        d1.rearrange("p (nb bk) -> p nb bk", bk=16)[:, :, 0:1],
        rcp32.unsqueeze(2)), sig=nm + "_d1")
    vop(lambda e: e.tensor_tensor_scan(rcpx, repm32[0:P, 0:W], d1, 0.0,
                                       ALU.mult, ALU.add), sig=nm + "_rx")
    vop(lambda e: e.tensor_copy(
        d1b.rearrange("p (nb bk) -> p nb bk", bk=16)[:, :, 0:1],
        scb.unsqueeze(2)), sig=nm + "_d2")
    vop(lambda e: e.tensor_tensor_scan(scx, repmb[0:P, 0:W], d1b, 0.0,
                                       ALU.mult, ALU.add), sig=nm + "_sx")
    vop(lambda e: e.tensor_tensor(zs, az, rcpx, ALU.mult),
        waits=(nm + "_az", nm + "_rx") + pv.get("zs", ()), sig=nm + "_zs")
    sop(lambda e: e.activation(i1, zs, AF.Copy, scale=2.0),
        waits=(nm + "_zs",) + pv.get("i1", ()), sig=nm + "_n1")
    sop(lambda e: e.activation(i2, zs, AF.Copy, bias=-2.0),
        waits=pv.get("i2", ()), sig=nm + "_n2")
    vop(lambda e: e.tensor_scalar(c1, i1, 0, 4, ALU.max, ALU.min),
        waits=(nm + "_n1",), sig=nm + "_c1")
    vop(lambda e: e.tensor_scalar(c2, i2, 0, 2, ALU.max, ALU.min),
        waits=(nm + "_n2",), sig=nm + "_c2")
    vop(lambda e: e.scalar_tensor_tensor(q1, c1, 0.5, c2, ALU.mult, ALU.add))
    sop(lambda e: e.activation(i1, zs, AF.Copy, bias=-4.5),
        waits=(nm + "_c1",), sig=nm + "_n3")
    vop(lambda e: e.tensor_scalar(c2, i1, 0, 1, ALU.max, ALU.min),
        waits=(nm + "_n3",), sig=nm + "_c3")
    vop(lambda e: e.scalar_tensor_tensor(q1, c2, 2.0, q1, ALU.mult, ALU.add))
    vop(lambda e: e.tensor_tensor(q1, q1, sgn, ALU.mult),
        waits=(nm + "_sg",), sig=nm + "_sm")
    ev = vop(lambda e: e.tensor_tensor(wout, q1, scx, ALU.mult),
             waits=(nm + "_sx",), sig=nm)
    q["_prev"] = {"az": (nm + "_zs",), "sgn": (nm + "_sm",),
                  "i1": (nm + "_c3",), "i2": (nm + "_c2",),
                  "zs": (nm + "_n3",)}
    return ev


def emit_cast_weight(prog, nm, dram, sb_dst, rows, cols, wstage, crows=128,
                     prev=None):
    op = prog.op
    nch = (rows + crows - 1) // crows
    last = None
    for c in range(nch):
        pr = min(crows, rows - c * crows)
        w = (f"{nm}_c{c-1}",) if c >= 1 else ((prev,) if prev else ())
        op("q", lambda e, c=c, pr=pr, cols=cols: e.dma_start(
            out=wstage[0:pr, 0:cols], in_=dram[c * crows:c * crows + pr, :]),
            waits=w, sig=f"{nm}_ld{c}")
        last = op("s", lambda e, c=c, pr=pr, cols=cols: e.copy(
            sb_dst[:, c, 0:cols][0:pr], wstage[0:pr, 0:cols]),
            waits=(f"{nm}_ld{c}",), sig=f"{nm}_c{c}")
    return last


FUSED_INTERNAL = {"xn_tm", "xnT", "srwT", "oL", "Send", "qeT", "Ecols",
                  "wgq_sh", "wuq_sh", "wdq_sh", "s_out", "Sgather",
                  "wgq_g", "wuq_g", "wdq_g"}


def make_mk(nc, cache, fused):
    def mk(name, shape, dtype, kind):
        if name in cache:
            return cache[name]
        if fused and name in FUSED_INTERNAL:
            kind = "Internal"
        ap = nc.dram_tensor(name, shape, dtype, kind=kind).ap()
        cache[name] = ap
        return ap
    return mk


def emit_phase_a(nc, prog, a, mk, ntile=NTILE, debug=False, fused=False,
                 qeT_buf=None, E_buf=None):
    d_x = mk("x", [TC + 3, H], F32, "ExternalInput")
    d_upT = mk("upT", [H, DI], F32, "ExternalInput")
    d_qkT = mk("qkT", [DI, 128], F32, "ExternalInput")
    d_gtT = mk("gtT", [DI, 128], F32, "ExternalInput")
    d_ig1T = mk("ig1T", [DI, R], F32, "ExternalInput")
    d_ig2T = mk("ig2T", [R, DI], F32, "ExternalInput")
    d_convw = mk("convw", [128, 64], F32, "ExternalInput")
    d_convb = mk("convb", [128, 16], F32, "ExternalInput")
    d_igb2 = mk("igb2", [128, 16], F32, "ExternalInput")
    d_rw = mk("rw", [128, 16], F32, "ExternalInput")
    d_gb = mk("gb", [64, 1], F32, "ExternalInput")
    d_tb = mk("tb", [64, 1], F32, "ExternalInput")
    d_ident = mk("ident", [128, 128], BF16, "ExternalInput")
    d_id64f = mk("id64f", [64, 64], F32, "ExternalInput")
    d_mask = mk("maskLT", [64, 64], F32, "ExternalInput")
    d_ones64 = mk("ones64", [64, 64], F32, "ExternalInput")
    d_ones1x = mk("ones1x", [1, 64], BF16, "ExternalInput")
    d_repm32 = mk("repm32", [128, 1024], F32, "ExternalInput")
    d_repmb = mk("repmb", [128, QW_MAX], BF16, "ExternalInput")
    d_wg_sh = mk("wg_sh", [352, H], F32, "ExternalInput")
    d_wu_sh = mk("wu_sh", [352, H], F32, "ExternalInput")
    d_wd_sh = mk("wd_sh", [128, I_FFN], F32, "ExternalInput")

    o_xn = mk("xn_tm", [TC, H], F32, "ExternalOutput")
    o_xnT = mk("xnT", [H, TC], BF16, "ExternalOutput")
    o_srwT = mk("srwT", [TC, DI], BF16, "ExternalOutput")
    o_oL = mk("oL", [TC, DI], BF16, "ExternalOutput")
    o_Send = mk("Send", [64, DI], F32, "ExternalOutput")
    o_qeT = mk("qeT", [64, TC], BF16, "ExternalOutput")
    o_E = mk("Ecols", [64, 64], F32, "ExternalOutput")
    o_wgq = mk("wgq_sh", [352, H], BF16, "ExternalOutput")
    o_wuq = mk("wuq_sh", [352, H], BF16, "ExternalOutput")
    o_wdq = mk("wdq_sh", [128, I_FFN], BF16, "ExternalOutput")
    if debug:
        o_dbg = mk("dbg", [128, 4 * H], BF16, "ExternalOutput")
        o_dbgf = mk("dbgf", [64, 64], F32, "ExternalOutput")

    op = prog.op

    upT = a.alloc([128, 8, DI], BF16)
    qkT = a.alloc([128, 16, 128], BF16)
    gtT = a.alloc([128, 16, 128], BF16)
    ig1T = a.alloc([128, 16, R], BF16)
    ig2T = a.alloc([16, 16, 128], BF16)
    convw = a.alloc([128, 16, 4], F32)
    convb = a.alloc([128, 16], F32)
    igb2 = a.alloc([128, 16], F32)
    rw = a.alloc([128, 16], F32)
    gb = a.alloc([64, 1], F32)
    tb = a.alloc([64, 1], F32)
    ident = a.alloc([128, 128], BF16)
    id64f = a.alloc([64, 64], F32)
    maskLT = a.alloc([64, 64], F32)
    ones64 = a.alloc([64, 64], F32)
    ones1x = a.alloc([1, 64], BF16)
    repm32 = a.alloc([128, 1024], F32)
    repmb = a.alloc([128, QW_MAX], BF16)
    S = a.alloc([64, DI], F32)
    Sb = a.alloc([64, DI], BF16)
    qeT_all = qeT_buf if qeT_buf is not None else a.alloc([64, TC], BF16)
    E_all = E_buf if E_buf is not None else a.alloc([64, 64], F32)
    Bacc = a.alloc([64, 16], F32)
    Pincl = a.alloc([64, 16], F32)

    a.mark()
    wstage = a.alloc([128, DI], F32)
    xg = a.alloc([128, H], F32)
    xng = a.alloc([128, H], BF16)
    ln_tr = a.alloc([128, H], F32)
    ln_m = a.alloc([128, 1], F32)
    ln_v = a.alloc([128, 1], F32)
    xnT_t = a.alloc([128, 8, TT], BF16)
    xup = a.alloc([128, 16, TT + 3], BF16)
    xup_halo = a.alloc([128, 16, 3], BF16)
    sh = a.alloc([128, 16, TT], BF16)
    srw_g = a.alloc([128, DI], BF16)
    u16 = a.alloc([16, TT], BF16)
    vt = a.alloc([64, NCHT, DI], BF16)
    qf = a.alloc([64, TT], F32)
    kf = a.alloc([64, TT], F32)
    gg = a.alloc([64, TT], F32)
    tau = a.alloc([64, TT], F32)
    tmp1 = a.alloc([64, TT], F32)
    tmp2 = a.alloc([64, TT], F32)
    tmp3 = a.alloc([64, TT], F32)
    ksq = tmp3
    rt = a.alloc([64, TT], F32)
    bpos = a.alloc([64, TT], F32)
    btr = a.alloc([64, TT], F32)
    d1b = tmp2
    rep64 = a.alloc([64, TT], F32)
    it = a.alloc([64, TT], F32)
    kn = a.alloc([64, TT], F32)
    kd = a.alloc([64, TT], F32)
    ke = a.alloc([64, TT], BF16)
    rcp1 = a.alloc([1, TT], F32)
    rcp1b = a.alloc([1, TT], BF16)
    btot = a.alloc([64, NCHT], F32)
    decay = a.alloc([64, NCHT], F32)
    tmpE = a.alloc([64, NCHT], F32)
    A_sb = a.alloc([64, 64], BF16)
    kdT = a.alloc([64, 64], BF16)
    oL_g = a.alloc([128, DI], BF16)

    pg = []
    def palloc(name, shape, dtype):
        g = nc.psum_tensor(prog.ns + name, shape, dtype)
        t = g.__enter__(); pg.append(g)
        return t[:]
    pA = palloc("pA", [128, TT], F32)
    pB = palloc("pB", [128, TT], F32)
    pTR = palloc("pTR", [128, 1024], BF16)
    pSH = palloc("pSH", [64, 512], F32)
    pO = palloc("pO", [64, 1024], F32)
    pS = palloc("pS", [128, 1024], F32)

    for nm, dst, src in [
        ("c_convw", convw.rearrange("p a b -> p (a b)"), d_convw),
        ("c_convb", convb, d_convb), ("c_igb2", igb2, d_igb2),
        ("c_rw", rw, d_rw), ("c_gb", gb, d_gb), ("c_tb", tb, d_tb),
        ("c_id", ident, d_ident), ("c_id64", id64f, d_id64f),
        ("c_mask", maskLT, d_mask), ("c_on", ones64, d_ones64),
        ("c_on1", ones1x, d_ones1x), ("c_repm32", repm32, d_repm32),
        ("c_repmb", repmb, d_repmb),
    ]:
        op("q", lambda e, dst=dst, src=src: e.dma_start(out=dst, in_=src), sig=nm)

    ev_up = emit_cast_weight(prog, "wup", d_upT, upT, H, DI, wstage)
    ev_qk = emit_cast_weight(prog, "wqk", d_qkT, qkT, DI, 128, wstage,
                             prev=ev_up)
    ev_gt = emit_cast_weight(prog, "wgt", d_gtT, gtT, DI, 128, wstage,
                             prev=ev_qk)
    ev_ig1 = emit_cast_weight(prog, "wig1", d_ig1T, ig1T, DI, R, wstage,
                              prev=ev_gt)
    op("q", lambda e: e.dma_start(out=wstage[0:16, 0:DI], in_=d_ig2T),
       waits=(ev_ig1,), sig="wig2_ld")
    ev_ig2 = op("s", lambda e: e.copy(ig2T.rearrange("p c f -> p (c f)"),
                                      wstage[0:16, 0:DI]),
                waits=("wig2_ld",), sig="wig2")

    op("v", lambda e: e.memset(S, 0.0), sig="S_init")
    op("v", lambda e: e.memset(Sb, 0.0), sig="Sb_init")
    op("v", lambda e: e.memset(Bacc[:, 0:1], 0.0), sig="Bacc_init")
    op("v", lambda e: e.memset(rep64, 1.0))
    op("v", lambda e: e.memset(
        rep64.rearrange("p (c t) -> p c t", t=CH)[:, :, 0:1], 0.0),
       sig="rep64_i")

    op("q", lambda e: e.dma_start(out=xg[0:3], in_=d_x[0:3, :]), sig="halo_ld")
    ev = emit_ln_group(prog, "halo_ln", xg, xng, ln_m, ln_v, ln_tr, 3,
                       ("halo_ld",))
    for j in range(8):
        op("t", lambda e, j=j: e.transpose(pTR[:, j * 128:j * 128 + 3],
                                           xng[0:3, 128 * j:128 * (j + 1)],
                                           ident[0:3, 0:3]),
           waits=(ev, "c_id") if j == 0 else (),
           sig="halo_tr" if j == 7 else None)
    op("v", lambda e: e.tensor_copy(
        xnT_t[:, :, 0:3],
        pTR.rearrange("p (c t) -> p c t", t=128)[:, :, 0:3]),
       waits=("halo_tr",), sig="halo_xnT")
    for d in range(16):
        wdd = []
        if d == 0:
            wdd = ["halo_xnT", ev_up]
        else:
            wdd = [f"halo_up{d-1}"]
        for c in range(8):
            op("t", lambda e, d=d, c=c: e.matmul(
                pA[:, 0:3], upT[:, c, 128 * d:128 * (d + 1)], xnT_t[:, c, 0:3],
                start=(c == 0), stop=(c == 7)),
               waits=tuple(wdd) if c == 0 else (),
               sig=f"halo_upm{d}" if c == 7 else None)
        op("s", lambda e, d=d: e.copy(xup_halo[:, d, :], pA[:, 0:3]),
           waits=(f"halo_upm{d}",), sig=f"halo_up{d}")

    for ti in range(ntile):
        t0 = ti * TT
        P = f"_t{ti}"
        Pm = f"_t{ti-1}"
        for g in range(4):
            gt0 = t0 + g * GRP
            if g > 0:
                wl = [f"ln{P}_g{g-1}_sq", f"xnT_tr{P}_g{g-1}",
                      f"xn_st{P}_g{g-1}"]
            elif ti > 0:
                wl = [f"ln{Pm}_g3_sq", f"xnT_tr{Pm}_g3", f"xn_st{Pm}_g3"]
            else:
                wl = ["halo_ln_sq", "halo_xnT"]
            op("q", lambda e, gt0=gt0: e.dma_start(
                out=xg, in_=d_x[3 + gt0:3 + gt0 + GRP, :]),
               waits=tuple(wl), sig=f"x_ld{P}_g{g}")
            ev = emit_ln_group(prog, f"ln{P}_g{g}", xg, xng, ln_m, ln_v, ln_tr,
                               128, (f"x_ld{P}_g{g}",))
            op("q", lambda e, gt0=gt0: e.dma_start(out=o_xn[gt0:gt0 + GRP, :],
                                                   in_=xg),
               waits=(ev + "_f",), sig=f"xn_st{P}_g{g}")

            wtr = [ev]
            if g > 0:
                wtr.append(f"xnT_ev{P}_g{g-1}")
            elif ti > 0:
                wtr.append(f"vt_ev{Pm}_c{NCHT-1}_h1")
            else:
                wtr.append("halo_xnT")
            for j in range(8):
                op("t", lambda e, j=j: e.transpose(
                    pTR[:, j * 128:(j + 1) * 128], xng[:, 128 * j:128 * (j + 1)],
                    ident),
                   waits=tuple(wtr) if j == 0 else (),
                   sig=f"xnT_tr{P}_g{g}" if j == 7 else None)
            wev = [f"xnT_tr{P}_g{g}"]
            if g == 0:
                if ti > 0:
                    wev += [f"up_mm{Pm}_d15", f"xnT_st{Pm}"]
                else:
                    wev.append("halo_up15")
            op("v", lambda e, g=g: e.tensor_copy(
                xnT_t[:, :, g * GRP:(g + 1) * GRP],
                pTR.rearrange("p (c t) -> p c t", t=128)),
               waits=tuple(wev), sig=f"xnT_ev{P}_g{g}")
        op("q", lambda e, t0=t0: e.dma_start(
            out=o_xnT.rearrange("(c p) t -> p c t", p=128)[:, :, t0:t0 + TT],
            in_=xnT_t),
           waits=(f"xnT_ev{P}_g3",), sig=f"xnT_st{P}")

        if ti == 0:
            op("v", lambda e: e.tensor_copy(
                xup[:, :, 0:3], xup_halo),
               waits=("halo_up15",), sig=f"halo_cp{P}")
        else:
            op("v", lambda e: e.tensor_copy(
                xup[:, :, 0:3], xup_halo),
               waits=(f"halo_sv{Pm}",), sig=f"halo_cp{P}")

        for d in range(16):
            pp = [pA, pB][d % 2]
            wgd = []
            if d >= 2:
                wgd.append(f"up_ev{P}_d{d-2}")
            if d == 0:
                wgd += [f"xnT_ev{P}_g3", f"halo_cp{P}"]
                if ti > 0:
                    wgd += [f"vmul{Pm}_d15", f"vt_ev{Pm}_c{NCHT-1}_h1",
                            f"kf_ev{Pm}", f"u_ev{Pm}"]
                else:
                    wgd.append("halo_up15")
            if d == 1 and ti > 0:
                wgd += [f"tau_ev{Pm}"]
            for c in range(8):
                op("t", lambda e, d=d, c=c, pp=pp: e.matmul(
                    pp, upT[:, c, 128 * d:128 * (d + 1)], xnT_t[:, c, :],
                    start=(c == 0), stop=(c == 7)),
                   waits=tuple(wgd) if c == 0 else (),
                   sig=f"up_mm{P}_d{d}" if c == 7 else None)
            op("s", lambda e, d=d, pp=pp: e.copy(xup[:, d, 3:TT + 3], pp),
               waits=(f"up_mm{P}_d{d}",), sig=f"up_ev{P}_d{d}")

        for d in range(16):
            wc = [f"up_ev{P}_d{d}"]
            if ti == 0 and d == 0:
                wc += ["c_convw", "c_convb"]
            if ti > 0 and d == 0:
                wc += [f"srw_st{Pm}_g3", f"srwtr{Pm}_g3_h1",
                       f"zsig{Pm}_d15"]
            op("v", lambda e, d=d: e.tensor_single_scalar(
                sh[:, d, :], xup[:, d, 0:TT], convw[:, d, 0:1], ALU.mult),
               waits=tuple(wc))
            op("v", lambda e, d=d: e.tensor_single_scalar(
                sh[:, d, :], sh[:, d, :], convb[:, d:d + 1], ALU.add))
            for j in range(1, 4):
                op("v", lambda e, d=d, j=j: e.scalar_tensor_tensor(
                    sh[:, d, :], xup[:, d, j:TT + j], convw[:, d, j:j + 1],
                    sh[:, d, :], ALU.mult, ALU.add),
                   sig=f"convacc{P}_d{d}" if j == 3 else None)
        op("s", lambda e: e.activation(sh.rearrange("p c t -> p (c t)"),
                                       sh.rearrange("p c t -> p (c t)"),
                                       AF.Silu),
           waits=(f"convacc{P}_d15",), sig=f"conv{P}")

        op("v", lambda e: e.tensor_copy(
            xup_halo, xup[:, :, TT:TT + 3]),
           waits=(f"convacc{P}_d15",), sig=f"halo_sv{P}")

        if debug and ti == 0:
            op("q", lambda e: e.dma_start(out=o_dbg[:, 0:TT],
                                          in_=xup[:, 0, 3:TT + 3]),
               waits=(f"up_ev{P}_d0",), sig="dbg3")
            op("q", lambda e: e.dma_start(out=o_dbg[:, TT:2 * TT],
                                          in_=sh[:, 0, :]),
               waits=(f"conv{P}",), sig="dbg4")
        wqk = [f"conv{P}"]
        if ti == 0:
            wqk.append(ev_qk)
        else:
            wqk.append(f"kf_ev{Pm}")
        for c in range(16):
            op("t", lambda e, c=c: e.matmul(pA, qkT[:, c, :], sh[:, c, :],
                                            start=(c == 0), stop=(c == 15)),
               waits=tuple(wqk) if c == 0 else (),
               sig=f"qk_mm{P}" if c == 15 else None)
        wgt = []
        if ti == 0:
            wgt.append(ev_gt)
        else:
            wgt.append(f"tau_ev{Pm}")
        for c in range(16):
            op("t", lambda e, c=c: e.matmul(pB, gtT[:, c, :], sh[:, c, :],
                                            start=(c == 0), stop=(c == 15)),
               waits=tuple(wgt) if c == 0 else (),
               sig=f"gt_mm{P}" if c == 15 else None)
        op("v", lambda e: e.tensor_copy(qf, pA[0:64]), waits=(f"qk_mm{P}",))
        op("v", lambda e: e.tensor_copy(kf, pA[64:128]), sig=f"kf_ev{P}")

        wpa = [f"gt_mm{P}"]
        if ti == 0:
            wpa.append("c_gb")
        op("s", lambda e: e.activation(tmp1, pB[0:64], AF.Abs, bias=gb),
           waits=tuple(wpa), sig=f"pa{P}")
        op("s", lambda e: e.activation(tmp2, tmp1, AF.Exp, scale=-1.0))
        op("s", lambda e: e.activation(tmp2, tmp2, AF.Ln, bias=1.0),
           sig=f"pl{P}")
        op("s", lambda e: e.activation(tmp1, pB[0:64], AF.Relu, bias=gb),
           sig=f"pr{P}")
        wtb = [] if ti > 0 else ["c_tb"]
        op("s", lambda e: e.activation(tau, pB[64:128], AF.Sigmoid, bias=tb),
           waits=tuple(wtb), sig=f"tau_ev{P}")
        op("v", lambda e: e.tensor_tensor(gg, tmp1, tmp2, ALU.add),
           waits=(f"pl{P}", f"pr{P}"), sig=f"gg{P}")
        op("s", lambda e: e.activation(tmp1, gg, AF.Ln), waits=(f"gg{P}",),
           sig=f"plng{P}")
        op("v", lambda e: e.tensor_tensor(tmp2, tau, tmp1, ALU.mult),
           waits=(f"plng{P}", f"tau_ev{P}"), sig=f"pt{P}")
        op("s", lambda e: e.activation(it, tmp2, AF.Exp), waits=(f"pt{P}",),
           sig=f"it{P}")
        op("v", lambda e: e.tensor_tensor(rt, gg, tau, ALU.mult),
           waits=(f"tau_ev{P}",), sig=f"rt{P}")

        for c in range(NCHT):
            sl = slice(c * CH, (c + 1) * CH)
            op("v", lambda e, sl=sl: e.tensor_tensor_scan(
                bpos[:, sl], ones64[:, 0:CH], rt[:, sl], 0.0, ALU.mult, ALU.add),
               waits=(f"rt{P}", "c_on") if c == 0 else (),
               sig=f"bpos{P}" if c == NCHT - 1 else None)
        op("v", lambda e: e.tensor_copy(
            btot.unsqueeze(2),
            bpos.rearrange("p (c t) -> p c t", t=CH)[:, :, CH - 1:CH]),
           waits=(f"bpos{P}",), sig=f"btot{P}")
        op("s", lambda e: e.activation(decay, btot, AF.Exp, scale=-1.0),
           waits=(f"btot{P}",), sig=f"decay{P}")
        wtm = ["Bacc_init"] if ti == 0 else [f"E{Pm}", f"bacc{Pm}"]
        op("v", lambda e: e.tensor_copy(tmpE[:, 0:1], Bacc[:, 0:1]),
           waits=tuple(wtm), sig=f"tE{P}_0")
        for cc in range(1, NCHT):
            op("v", lambda e, cc=cc: e.tensor_tensor(
                tmpE[:, cc:cc + 1], tmpE[:, cc - 1:cc], btot[:, cc - 1:cc],
                ALU.add), waits=(f"tE{P}_{cc-1}", f"btot{P}"),
               sig=f"tE{P}_{cc}")
        op("v", lambda e: e.tensor_single_scalar(tmpE, tmpE, 1.0, ALU.mult),
           waits=(f"tE{P}_{NCHT-1}",), sig=f"tmpE{P}")
        op("s", lambda e, ti=ti: e.activation(E_all[:, ti * NCHT:(ti + 1) * NCHT],
                                              tmpE, AF.Exp, scale=-1.0),
           waits=(f"tmpE{P}",), sig=f"E{P}")
        op("v", lambda e: e.tensor_tensor(
            Bacc[:, 0:1], tmpE[:, NCHT - 1:NCHT], btot[:, NCHT - 1:NCHT],
            ALU.add), waits=(f"tmpE{P}", f"E{P}"), sig=f"bacc{P}")
        if debug and ti == 0:
            op("q", lambda e: e.dma_start(out=o_dbgf[:, 0:NCHT], in_=btot),
               waits=(f"btot{P}",), sig="dbgf0")
            op("q", lambda e: e.dma_start(out=o_dbgf[:, 16:16 + NCHT], in_=tmpE),
               waits=(f"tmpE{P}",), sig="dbgf2")

        op("v", lambda e: e.tensor_tensor(ksq, kf, kf, ALU.mult),
           sig=f"ksq{P}")
        wkn = [f"ksq{P}"]
        if ti > 0:
            wkn.append(f"kdT_ev{Pm}_c{NCHT-1}")
        op("t", lambda e: e.matmul(pSH[0:1, :], ones64[:, 0:1], ksq,
                                   start=True, stop=True),
           waits=tuple(wkn), sig=f"nsq_mm{P}")
        op("s", lambda e: e.activation(rcp1, pSH[0:1, :], AF.Sqrt),
           waits=(f"nsq_mm{P}",), sig=f"nsqrt{P}")
        op("v", lambda e: e.tensor_scalar(rcp1, rcp1, 1.0, 1e-12,
                                          ALU.mult, ALU.max),
           waits=(f"nsqrt{P}",), sig=f"rmax{P}")
        op("v", lambda e: e.reciprocal(rcp1, rcp1), waits=(f"rmax{P}",),
           sig=f"rcpr{P}")
        op("v", lambda e: e.tensor_copy(rcp1b, rcp1), waits=(f"rcpr{P}",),
           sig=f"rcp1b{P}")
        wbc = [f"rcp1b{P}"]
        if ti == 0:
            wbc.append("c_on1")
        op("t", lambda e: e.matmul(pSH, ones1x, rcp1b, start=True, stop=True),
           waits=tuple(wbc), sig=f"bc_mm{P}")
        op("v", lambda e: e.tensor_tensor(kn, kf, pSH, ALU.mult),
           waits=(f"bc_mm{P}",))
        op("v", lambda e: e.tensor_tensor(kn, kn, it, ALU.mult),
           waits=(f"it{P}",), sig=f"kn{P}")

        op("s", lambda e: e.activation(tmp1, bpos, AF.Exp, scale=-1.0),
           waits=(f"bpos{P}", f"pt{P}"), sig=f"expb{P}")
        wqe = [f"expb{P}"]
        if ti > 0:
            wqe.append(f"Amm{Pm}_c{NCHT-1}")
        op("v", lambda e, t0=t0: e.scalar_tensor_tensor(
            qeT_all[:, t0:t0 + TT], qf, SCALE, tmp1, ALU.mult, ALU.mult),
           waits=tuple(wqe), sig=f"qe{P}")
        op("s", lambda e: e.activation(tmp2, bpos, AF.Exp), sig=f"expp{P}")
        op("v", lambda e: e.tensor_tensor(ke, kn, tmp2, ALU.mult),
           waits=(f"expp{P}", f"kn{P}"), sig=f"ke{P}")
        op("v", lambda e: e.memset(d1b, 0.0), waits=(f"ke{P}",))
        op("v", lambda e: e.tensor_copy(
            d1b.rearrange("p (c t) -> p c t", t=CH)[:, :, 0:1],
            btot.unsqueeze(2)), waits=(f"btot{P}",), sig=f"d1b{P}")
        op("v", lambda e: e.tensor_tensor_scan(btr, rep64, d1b, 0.0, ALU.mult,
                                               ALU.add),
           waits=(f"d1b{P}", "rep64_i"), sig=f"btr{P}")
        op("v", lambda e: e.tensor_tensor(tmp3, bpos, btr, ALU.subtract),
           waits=(f"btr{P}", f"nsq_mm{P}"), sig=f"sub{P}")
        op("s", lambda e: e.activation(tmp3, tmp3, AF.Exp), waits=(f"sub{P}",),
           sig=f"expd{P}")
        op("v", lambda e: e.tensor_tensor(kd, kn, tmp3, ALU.mult),
           waits=(f"expd{P}",), sig=f"kd{P}")

        for d in range(16):
            wsr = []
            if d == 0:
                wsr = [f"conv{P}", f"qk_mm{P}", f"gt_mm{P}"]
                if debug and ti == 0:
                    wsr.append("dbg4")
                if ti == 0:
                    wsr.append("c_rw")
            op("v", lambda e, d=d: e.tensor_scalar(
                sh[:, d, :], sh[:, d, :], rw[:, d:d + 1], None, ALU.mult),
               waits=tuple(wsr), sig=f"srw{P}" if d == 15 else None)
        for g in range(4):
            for hh in range(2):
                for jj in range(8):
                    j = hh * 8 + jj
                    wtr = []
                    if jj == 0:
                        wtr = [f"srw{P}"]
                        if hh == 1:
                            wtr.append(f"srw_ev{P}_g{g}_h0")
                        elif g > 0:
                            wtr.append(f"srw_ev{P}_g{g-1}_h1")
                        else:
                            wtr.append(f"xnT_ev{P}_g3")
                    op("t", lambda e, g=g, j=j, jj=jj: e.transpose(
                        pTR[:, jj * 128:(jj + 1) * 128],
                        sh[:, j, g * GRP:(g + 1) * GRP], ident),
                       waits=tuple(wtr),
                       sig=f"srwtr{P}_g{g}_h{hh}" if jj == 7 else None)
                we = [f"srwtr{P}_g{g}_h{hh}"]
                if hh == 0 and g > 0:
                    we.append(f"srw_st{P}_g{g-1}")
                op("v", lambda e, hh=hh: e.tensor_copy(
                    srw_g[:, hh * 1024:(hh + 1) * 1024], pTR),
                   waits=tuple(we), sig=f"srw_ev{P}_g{g}_h{hh}")
            op("q", lambda e, g=g, t0=t0: e.dma_start(
                out=o_srwT[t0 + g * GRP:t0 + (g + 1) * GRP, :], in_=srw_g),
               waits=(f"srw_ev{P}_g{g}_h1",), sig=f"srw_st{P}_g{g}")
        op("s", lambda e: e.copy(rcp1b[0:1, 0:1], rcp1b[0:1, 0:1]),
           waits=(f"srwtr{P}_g3_h1",), sig=f"srwtr_done{P}")

        wig = [f"conv{P}", f"tau_ev{P}"]
        if ti == 0:
            wig.append(ev_ig2)
        else:
            wig.append(f"u_ev{Pm}")
        for c in range(16):
            op("t", lambda e, c=c: e.matmul(pB[0:16, 0:TT], ig1T[:, c, :],
                                            xup[:, c, 3:TT + 3],
                                            start=(c == 0), stop=(c == 15)),
               waits=tuple(wig) if c == 0 else (),
               sig=f"u_mm{P}" if c == 15 else None)
        op("v", lambda e: e.tensor_copy(u16, pB[0:16, 0:TT]),
           waits=(f"u_mm{P}",), sig=f"u_ev{P}")
        for d in range(16):
            pp = [pA, pB][d % 2]
            wz = []
            if d >= 2:
                wz.append(f"zsig{P}_d{d-2}")
            if d == 0:
                wz += [f"u_ev{P}", f"kf_ev{P}"]
            if d == 1:
                wz += [f"u_ev{P}", f"tau_ev{P}"]
            op("t", lambda e, d=d, pp=pp: e.matmul(
                pp[:, 0:TT], ig2T[:, d, :], u16, start=True, stop=True),
               waits=tuple(wz), sig=f"z_mm{P}_d{d}")
            wzs = [f"z_mm{P}_d{d}", f"srwtr_done{P}"]
            if ti == 0 and d == 0:
                wzs.append("c_igb2")
            op("s", lambda e, d=d, pp=pp: e.activation(
                sh[:, d, :], pp[:, 0:TT], AF.Sigmoid, bias=igb2[:, d:d + 1]),
               waits=tuple(wzs), sig=f"zsig{P}_d{d}")
            wvm = [f"zsig{P}_d{d}", f"halo_sv{P}"]
            if debug and ti == 0 and d == 0:
                wvm.append("dbg3")
            op("v", lambda e, d=d: e.tensor_tensor(
                xup[:, d, 3:TT + 3], sh[:, d, :], xup[:, d, 3:TT + 3],
                ALU.mult),
               waits=tuple(wvm),
               sig=f"vmul{P}_d{d}")

        for c in range(NCHT):
            for hh in range(2):
                for dd in range(8):
                    d = hh * 8 + dd
                    wv = []
                    if dd == 0:
                        wv = [f"vmul{P}_d15"]
                        if not (c == 0 and hh == 0):
                            prevc, prevh = (c, 0) if hh == 1 else (c - 1, 1)
                            wv.append(f"vt_ev{P}_c{prevc}_h{prevh}")
                        else:
                            wv.append(f"srw_ev{P}_g3_h1")
                    op("t", lambda e, c=c, d=d, dd=dd: e.transpose(
                        pTR[0:64, dd * 128:(dd + 1) * 128],
                        xup[:, d, 3 + c * CH:3 + (c + 1) * CH], ident),
                       waits=tuple(wv),
                       sig=f"vtr{P}_c{c}_h{hh}" if dd == 7 else None)
                op("v", lambda e, c=c, hh=hh: e.tensor_copy(
                    vt[:, c, hh * 1024:(hh + 1) * 1024], pTR[0:64]),
                   waits=(f"vtr{P}_c{c}_h{hh}",), sig=f"vt_ev{P}_c{c}_h{hh}")

        for c in range(NCHT):
            gl = f"{P}_c{c}"
            glm = f"{P}_c{c-1}"
            lsl = slice(c * CH, (c + 1) * CH)
            qsl = slice(t0 + c * CH, t0 + (c + 1) * CH)
            wA = [f"ke{P}", f"qe{P}"]
            if c == 0:
                wA.append(f"kn{P}")
            else:
                wA += [f"Amask{glm}", f"kdT_ev{glm}"]
            op("t", lambda e, lsl=lsl, qsl=qsl: e.matmul(
                pSH[:, 0:64], ke[:, lsl], qeT_all[:, qsl],
                start=True, stop=True),
               waits=tuple(wA), sig=f"Amm{gl}")
            wK = [f"kd{P}"]
            if ti == 0 and c == 0:
                wK.append("c_id64")
            op("t", lambda e, lsl=lsl: e.transpose(
                pSH[:, 64:128], kd[:, lsl], id64f),
               waits=tuple(wK), sig=f"kdTmm{gl}")
            wM = [f"kdTmm{gl}"]
            if ti == 0 and c == 0:
                wM.append("c_mask")
            op("v", lambda e: e.tensor_tensor(A_sb, pSH[:, 0:64], maskLT,
                                              ALU.mult),
               waits=tuple(wM), sig=f"Amask{gl}")
            op("v", lambda e: e.tensor_copy(kdT, pSH[:, 64:128]),
               sig=f"kdT_ev{gl}")

            for h in range(2):
                hsl = slice(h * 1024, (h + 1) * 1024)
                wo = [f"vt_ev{P}_c{c}_h{h}", f"Amask{gl}"]
                if ti == 0 and c == 0 and h == 0:
                    wo.append("Sb_init")
                if c > 0:
                    wo.append(f"SbU{glm}_h{h}")
                elif ti > 0:
                    wo.append(f"SbU{Pm}_c{NCHT-1}_h{h}")
                if h == 1:
                    wo.append(f"oev{gl}_h0")
                elif c > 0:
                    wo.append(f"oev{glm}_h1")
                elif ti > 0:
                    wo.append(f"oev{Pm}_c{NCHT-1}_h1")
                for s2 in range(2):
                    op("t", lambda e, qsl=qsl, h=h, s2=s2: e.matmul(
                        pO[:, s2 * 512:(s2 + 1) * 512], qeT_all[:, qsl],
                        Sb[:, h * 1024 + s2 * 512:h * 1024 + (s2 + 1) * 512],
                        start=True, stop=False),
                       waits=tuple(wo) if s2 == 0 else ())
                for s2 in range(2):
                    op("t", lambda e, c=c, h=h, s2=s2: e.matmul(
                        pO[:, s2 * 512:(s2 + 1) * 512], A_sb,
                        vt[:, c, h * 1024 + s2 * 512:h * 1024 + (s2 + 1) * 512],
                        start=False, stop=True),
                       sig=f"omm{gl}_h{h}" if s2 == 1 else None)
                row = 64 * (c % 2)
                woe = [f"omm{gl}_h{h}"]
                if c % 2 == 0 and h == 0:
                    if c > 1:
                        woe.append(f"oL_st{P}_c{c-1}")
                    elif ti > 0:
                        woe.append(f"oL_st{Pm}_c{NCHT-1}")
                op("s", lambda e, row=row, hsl=hsl: e.copy(
                    oL_g[row:row + 64, hsl], pO),
                   waits=tuple(woe), sig=f"oev{gl}_h{h}")
                wS = [f"kdT_ev{gl}", f"vt_ev{P}_c{c}_h{h}"]
                if c > 0:
                    wS.append(f"SU{glm}_h{h}")
                elif ti > 0:
                    wS.append(f"SU{Pm}_c{NCHT-1}_h{h}")
                if h == 1:
                    wS.append(f"SU{gl}_h0")
                for s2 in range(2):
                    op("t", lambda e, c=c, h=h, s2=s2: e.matmul(
                        pS[0:64, s2 * 512:(s2 + 1) * 512], kdT,
                        vt[:, c, h * 1024 + s2 * 512:h * 1024 + (s2 + 1) * 512],
                        start=True, stop=True),
                       waits=tuple(wS) if s2 == 0 else (),
                       sig=f"Smm{gl}_h{h}" if s2 == 1 else None)
                wSU = [f"Smm{gl}_h{h}", f"decay{P}"]
                if ti == 0 and c == 0 and h == 0:
                    wSU.append("S_init")
                op("v", lambda e, c=c, hsl=hsl: e.scalar_tensor_tensor(
                    S[:, hsl], S[:, hsl], decay[:, c:c + 1], pS[0:64],
                    ALU.mult, ALU.add),
                   waits=tuple(wSU), sig=f"SU{gl}_h{h}")
                op("s", lambda e, hsl=hsl: e.copy(Sb[:, hsl], S[:, hsl]),
                   waits=(f"SU{gl}_h{h}",), sig=f"SbU{gl}_h{h}")
            if c % 2 == 1:
                op("q", lambda e, t0=t0, c=c: e.dma_start(
                    out=o_oL[t0 + (c - 1) * CH:t0 + (c + 1) * CH, :], in_=oL_g),
                   waits=(f"oev{gl}_h1", f"oev{glm}_h1"),
                   sig=f"oL_st{gl}")

    lastP = f"_t{ntile-1}"
    op("q", lambda e: e.dma_start(out=o_Send, in_=S),
       waits=(f"SU{lastP}_c{NCHT-1}_h1",), sig="S_st")
    op("q", lambda e: e.dma_start(out=o_qeT, in_=qeT_all),
       waits=(f"qe{lastP}",), sig="qeT_st")
    op("q", lambda e: e.dma_start(out=o_E, in_=E_all),
       waits=(f"E{lastP}",), sig="E_st")

    a.reset()
    a.mark()
    wstage = a.alloc([128, I_FFN], F32)
    wq_out = a.alloc([128, I_FFN], BF16)
    qs = emit_quant_scratch(a)
    last_gla = f"oL_st{lastP}_c{NCHT-1}"
    prev_st = None
    for wi, (dsrc, dout, rows, cols) in enumerate([
            (d_wg_sh, o_wgq, 352, H), (d_wu_sh, o_wuq, 352, H),
            (d_wd_sh, o_wdq, 128, I_FFN)]):
        nch = (rows + 127) // 128
        for cidx in range(nch):
            pr = min(128, rows - cidx * 128)
            nm = f"wq{wi}_{cidx}"
            wl = [last_gla, f"srw_st{lastP}_g3", f"xnT_st{lastP}",
                  f"vt_ev{lastP}_c{NCHT-1}_h1", f"vmul{lastP}_d15",
                  f"kd{lastP}", f"bacc{lastP}"] + \
                [f"xn_st{lastP}_g{g}" for g in range(4)]
            if prev_st:
                wl.append(prev_st)
            op("q", lambda e, dsrc=dsrc, cidx=cidx, pr=pr, cols=cols:
               e.dma_start(out=wstage[0:pr, 0:cols],
                           in_=dsrc[cidx * 128:cidx * 128 + pr, :]),
               waits=tuple(wl), sig=nm + "_ld")
            evs = []
            npiece = 2 if cols == H else 4
            pw = cols // npiece
            for hw in range(npiece):
                w0 = hw * pw
                evs.append(emit_quant_tile(
                    prog, qs, nm + f"_q{hw}", wstage[0:pr, w0:w0 + pw],
                    wq_out[0:pr, w0:w0 + pw], pr, pw, repm32, repmb,
                    (nm + "_ld", "c_repm32", "c_repmb"),
                    fresh=(wi == 0 and cidx == 0 and hw == 0)))
            prev_st = op("q", lambda e, dout=dout, cidx=cidx, pr=pr, cols=cols:
                         e.dma_start(out=dout[cidx * 128:cidx * 128 + pr, :],
                                     in_=wq_out[0:pr, 0:cols]),
                         waits=tuple(evs), sig=nm + "_st")

    a.reset()
    for g in reversed(pg):
        g.__exit__(None, None, None)
    ends = dict(
        S_st=prog._n("S_st"), qeT_st=prog._n("qeT_st"), E_st=prog._n("E_st"),
        wq_st=prog._n(prev_st) if not prev_st.startswith(prog.ns) else prev_st,
        oL_st=prog._n(f"oL_st{lastP}_c{NCHT-1}"),
        xn_st=prog._n(f"xn_st{lastP}_g3"), xnT_st=prog._n(f"xnT_st{lastP}"),
        srw_st=prog._n(f"srw_st{lastP}_g3"),
        SbU=prog._n(f"SbU{lastP}_c{NCHT-1}_h1"),
        E=prog._n(f"E{lastP}"), bacc=prog._n(f"bacc{lastP}"),
        wg0=prog._n("wgq_sh_nm") if False else None,
    )
    ends = {k: v for k, v in ends.items() if v}
    return ends


def build_phase_a(ntile=NTILE, debug=False):
    nc = bass.Bass("TRN2", target_bir_lowering=False)
    prog = Prog()
    a = Arena(nc, "arena", 102 * 1024)
    mk = make_mk(nc, {}, False)
    ends = emit_phase_a(nc, prog, a, mk, ntile=ntile, debug=debug)
    prog.op("p", lambda e: e.nop(), waits=tuple(ends.values()))
    with nc.allow_low_precision(reason="bf16 pipeline per problem statement"):
        prog.emit(nc)
    a.close()
    return nc


def emit_phase_b(nc, prog, a, mk, ntile=NTILE, fused=False, qeT_buf=None,
                 E_buf=None, d_Sgather=None, cc_ev=None):
    d_xnT = mk("xnT", [H, TC], BF16, "ExternalInput")
    d_oL = mk("oL", [TC, DI], BF16, "ExternalInput")
    d_srwT = mk("srwT", [TC, DI], BF16, "ExternalInput")
    if not fused:
        d_qeT = mk("qeT", [64, TC], BF16, "ExternalInput")
        d_E = mk("Ecols", [64, 64], F32, "ExternalInput")
        d_Sin = mk("Sin", [64, DI], F32, "ExternalInput")
    d_sel = mk("sel", [64, 8], F32, "ExternalInput") if fused else None
    d_gateT = mk("gateT", [H, DI], F32, "ExternalInput")
    d_downT = mk("downT", [DI, H], F32, "ExternalInput")
    d_nw = mk("nw", [128, 16], F32, "ExternalInput")
    d_ident = mk("ident", [128, 128], BF16, "ExternalInput")
    o_s = mk("s_out", [TC, H], F32, "ExternalOutput")

    op = prog.op

    gateT = a.alloc([128, 8, DI], BF16)
    downT = a.alloc([128, 16, H], BF16)
    nw = a.alloc([128, 16], F32)
    ident = a.alloc([128, 128], BF16)
    SinB = a.alloc([64, DI], BF16)
    qeT_all = qeT_buf if qeT_buf is not None else a.alloc([64, TC], BF16)
    E_all = E_buf if E_buf is not None else a.alloc([64, 64], F32)
    qeC = a.alloc([64, 64], BF16)
    SinF = a.alloc([64, DI], F32)
    selb = a.alloc([64, 8], F32)
    wstage = a.alloc([128, DI], F32)
    Gbuf = wstage[0:64, 0:DI]
    xnT_t = a.alloc([128, 8, TT], BF16)
    oL_g = a.alloc([128, DI], BF16)
    srw_g = a.alloc([128, DI], BF16)
    o2a = a.alloc([128, DI], BF16)
    o2af = a.alloc([128, DI], F32)
    sfg = a.alloc([128, DI], F32)
    o3 = a.alloc([128, DI], F32)
    y = a.alloc([128, DI], BF16)
    yT = a.alloc([128, 16, 128], BF16)
    s_g = a.alloc([128, H], F32)
    ssq = a.alloc([128, 1], F32)

    pg = []
    def palloc(name, shape, dtype):
        g = nc.psum_tensor(prog.ns + name, shape, dtype)
        t = g.__enter__(); pg.append(g)
        return t[:]
    pA = palloc("pA", [128, 512], F32)
    pB = palloc("pB", [128, 512], F32)
    pC = palloc("pC", [64, 1024], F32)
    pS = palloc("pS", [128, 1024], F32)
    pTR = palloc("pTR", [128, 1024], BF16)

    op("q", lambda e: e.dma_start(out=ident, in_=d_ident), sig="c_id")
    if not fused:
        op("q", lambda e: e.dma_start(out=E_all, in_=d_E), sig="c_E")
        op("q", lambda e: e.dma_start(out=qeT_all, in_=d_qeT), sig="c_qeT")
        op("q", lambda e: e.dma_start(out=wstage[0:64, 0:DI], in_=d_Sin),
           sig="sin_ld")
        op("s", lambda e: e.copy(SinB, wstage[0:64, 0:DI]), waits=("sin_ld",),
           sig="sinb")
    else:
        op("q", lambda e: e.dma_start(out=selb, in_=d_sel), sig="sel_ld")
        op("v", lambda e: e.memset(SinF, 0.0), sig="sin_z")
        for j in range(8):
            op("q", lambda e, j=j: e.dma_start(
                out=Gbuf, in_=d_Sgather[64 * j:64 * (j + 1), :]),
               waits=(cc_ev,) if j == 0 else (f"sacc_{j-1}",),
               sig=f"sg_ld_{j}")
            op("v", lambda e, j=j: e.scalar_tensor_tensor(
                SinF, Gbuf, selb[:, j:j + 1], SinF, ALU.mult, ALU.add),
               waits=(f"sg_ld_{j}", "sel_ld", "sin_z"), sig=f"sacc_{j}")
        op("s", lambda e: e.copy(SinB, SinF), waits=("sacc_7",), sig="sinb")
    op("q", lambda e: e.dma_start(out=nw, in_=d_nw), sig="c_nw")
    ev_gw = emit_cast_weight(prog, "wgate", d_gateT, gateT, H, DI, wstage,
                             prev="sinb")
    ev_dw = emit_cast_weight(prog, "wdown", d_downT, downT, DI, H, wstage,
                             prev=ev_gw)
    for d in range(16):
        op("v", lambda e, d=d: e.tensor_scalar(downT[:, d, :], downT[:, d, :],
                                               nw[:, d:d + 1], None, ALU.mult),
           waits=(ev_dw, "c_nw") if d == 0 else (),
           sig="nwfold" if d == 15 else None)

    for ti in range(ntile):
        t0 = ti * TT
        P = f"_t{ti}"
        Pm = f"_t{ti-1}"
        wl = [f"fgmm{Pm}_g3_n3"] if ti > 0 else []
        op("q", lambda e, t0=t0: e.dma_start(
            out=xnT_t,
            in_=d_xnT.rearrange("(c p) t -> p c t", p=128)[:, :, t0:t0 + TT]),
           waits=tuple(wl), sig=f"xnT_ld{P}")
        for g in range(4):
            gl = f"{P}_g{g}"
            glm = f"{P}_g{g-1}" if g > 0 else f"{Pm}_g3"
            gt0 = t0 + g * GRP
            wld = [f"o2a{glm}"] if (g > 0 or ti > 0) else []
            op("q", lambda e, gt0=gt0: e.dma_start(
                out=oL_g, in_=d_oL[gt0:gt0 + GRP, :]),
               waits=tuple(wld), sig=f"oL_ld{gl}")
            op("q", lambda e, gt0=gt0: e.dma_start(
                out=srw_g, in_=d_srwT[gt0:gt0 + GRP, :]),
               sig=f"srw_ld{gl}")
            wo2 = [f"oL_ld{gl}", f"srw_ld{gl}"]
            if g > 0 or ti > 0:
                wo2.append(f"corr{glm}_c1_h1")
            op("v", lambda e: e.tensor_tensor(o2a, oL_g, srw_g, ALU.add),
               waits=tuple(wo2), sig=f"o2ab{gl}")
            op("s", lambda e: e.copy(o2af, o2a), waits=(f"o2ab{gl}",),
               sig=f"o2a{gl}")
            for ns in range(4):
                pp = [pA, pB][ns % 2]
                wf = [f"xnT_ld{P}"]
                if ns >= 2:
                    wf.append(f"sfg{gl}_n{ns-2}")
                elif g > 0 or ti > 0:
                    wf.append(f"sfg{glm}_n{ns + 2}")
                if ti == 0 and g == 0 and ns == 0:
                    wf.append(ev_gw)
                for c in range(8):
                    op("t", lambda e, g=g, ns=ns, c=c, pp=pp: e.matmul(
                        pp, xnT_t[:, c, g * GRP:(g + 1) * GRP],
                        gateT[:, c, ns * 512:(ns + 1) * 512],
                        start=(c == 0), stop=(c == 7)),
                       waits=tuple(wf) if c == 0 else (),
                       sig=f"fgmm{gl}_n{ns}" if c == 7 else None)
                wsf = [f"fgmm{gl}_n{ns}"]
                if g > 0 or ti > 0:
                    wsf.append(f"o3{glm}")
                    wsf.append(f"sq{glm}")
                op("s", lambda e, ns=ns, pp=pp: e.activation(
                    sfg[:, ns * 512:(ns + 1) * 512], pp, AF.Silu),
                   waits=tuple(wsf), sig=f"sfg{gl}_n{ns}")
            for c2 in range(2):
                Cg = (t0 + g * GRP) // CH + c2
                wq = [] if fused else ["c_qeT", "c_E"]
                if not (ti == 0 and g == 0 and c2 == 0):
                    wq.append(f"cmm{gl}_c0_h1" if c2 == 1 else
                              f"cmm{glm}_c1_h1")
                op("v", lambda e, Cg=Cg: e.tensor_scalar(
                    qeC, qeT_all[:, Cg * CH:(Cg + 1) * CH],
                    E_all[:, Cg:Cg + 1], None, ALU.mult),
                   waits=tuple(wq), sig=f"qeC{gl}_c{c2}")
                for h in range(2):
                    wc = [f"qeC{gl}_c{c2}", "sinb"]
                    if h == 1:
                        wc.append(f"corrc{gl}_c{c2}_h0")
                    elif not (ti == 0 and g == 0 and c2 == 0):
                        wc.append(f"corrc{gl}_c0_h1" if c2 == 1 else
                                  f"corrc{glm}_c1_h1")
                    for s2 in range(2):
                        op("t", lambda e, h=h, s2=s2: e.matmul(
                            pC[:, s2 * 512:(s2 + 1) * 512], qeC,
                            SinB[:, h * 1024 + s2 * 512:
                                 h * 1024 + (s2 + 1) * 512],
                            start=True, stop=True),
                           waits=tuple(wc) if s2 == 0 else (),
                           sig=f"cmm{gl}_c{c2}_h{h}" if s2 == 1 else None)
                    op("s", lambda e, c2=c2, h=h: e.copy(
                        o3[64 * c2:64 * c2 + 64][:, h * 1024:(h + 1) * 1024],
                        pC),
                       waits=(f"cmm{gl}_c{c2}_h{h}",),
                       sig=f"corrc{gl}_c{c2}_h{h}")
                    op("v", lambda e, c2=c2, h=h: e.tensor_tensor(
                        o3[64 * c2:64 * c2 + 64][:, h * 1024:(h + 1) * 1024],
                        o3[64 * c2:64 * c2 + 64][:, h * 1024:(h + 1) * 1024],
                        o2af[64 * c2:64 * c2 + 64][:, h * 1024:(h + 1) * 1024],
                        ALU.add),
                       waits=(f"corrc{gl}_c{c2}_h{h}", f"o2a{gl}"),
                       sig=f"corr{gl}_c{c2}_h{h}")
            op("v", lambda e: e.memset(ssq, 0.0))
            op("v", lambda e: e.tensor_tensor(o3, o3, sfg, ALU.mult),
               waits=(f"corr{gl}_c1_h1", f"corr{gl}_c0_h1", f"sfg{gl}_n3"),
               sig=f"o3{gl}")
            op("s", lambda e: e.activation(sfg, o3, AF.Square, accum_out=ssq),
               waits=(f"o3{gl}",), sig=f"sq{gl}")
            op("v", lambda e: e.tensor_scalar(ssq, ssq, 1.0 / DI, NORM_EPS,
                                              ALU.mult, ALU.add),
               waits=(f"sq{gl}",), sig=f"sqs{gl}")
            op("v", lambda e: e.reciprocal(ssq, ssq), waits=(f"sqs{gl}",),
               sig=f"rr{gl}")
            op("s", lambda e: e.activation(ssq, ssq, AF.Sqrt),
               waits=(f"rr{gl}",), sig=f"rs{gl}")
            wy = [f"rs{gl}"]
            if g > 0 or ti > 0:
                wy.append(f"yT_ev{glm}_h1")
            op("v", lambda e: e.tensor_scalar(y, o3, ssq[:, 0:1], None,
                                              ALU.mult),
               waits=tuple(wy), sig=f"y{gl}")
            for hh in range(2):
                for jj in range(8):
                    j = hh * 8 + jj
                    wtr = []
                    if jj == 0:
                        wtr = [f"y{gl}"]
                        if hh == 1:
                            wtr.append(f"yT_ev{gl}_h0")
                        elif g > 0 or ti > 0:
                            wtr.append(f"yT_ev{glm}_h1")
                        if ti == 0 and g == 0 and j == 0:
                            wtr.append("c_id")
                    op("t", lambda e, j=j, jj=jj: e.transpose(
                        pTR[:, jj * 128:(jj + 1) * 128],
                        y[:, j * 128:(j + 1) * 128], ident),
                       waits=tuple(wtr),
                       sig=f"yTtr{gl}_h{hh}" if jj == 7 else None)
                whh = [f"yTtr{gl}_h{hh}"]
                if hh == 0 and (g > 0 or ti > 0):
                    whh.append(f"dmm{glm}_n1")
                op("v", lambda e, hh=hh: e.tensor_copy(
                    yT[:, hh * 8:(hh + 1) * 8, :],
                    pTR.rearrange("p (c t) -> p c t", t=128)),
                   waits=tuple(whh), sig=f"yT_ev{gl}_h{hh}")
            for ns in range(2):
                wd = [f"yT_ev{gl}_h1", "nwfold"]
                if g > 0 or ti > 0:
                    wd.append(f"s_ev{glm}")
                for c in range(16):
                    op("t", lambda e, ns=ns, c=c: e.matmul(
                        pS[:, ns * 512:(ns + 1) * 512], yT[:, c, :],
                        downT[:, c, ns * 512:(ns + 1) * 512],
                        start=(c == 0), stop=(c == 15)),
                       waits=tuple(wd) if (c == 0 and ns == 0) else (),
                       sig=f"dmm{gl}_n{ns}" if c == 15 else None)
            wse = [f"dmm{gl}_n0", f"dmm{gl}_n1"]
            if g > 0 or ti > 0:
                wse.append(f"s_st{glm}")
            op("s", lambda e: e.copy(s_g, pS), waits=tuple(wse),
               sig=f"s_ev{gl}")
            op("q", lambda e, gt0=gt0: e.dma_start(
                out=o_s[gt0:gt0 + GRP, :], in_=s_g),
               waits=(f"s_ev{gl}",), sig=f"s_st{gl}")

    for g in reversed(pg):
        g.__exit__(None, None, None)
    lb = f"_t{ntile-1}_g3"
    return dict(s_st=prog._n(f"s_st{lb}"), s_ev=prog._n(f"s_ev{lb}"),
                yT=prog._n(f"yT_ev{lb}_h1"), sq=prog._n(f"sq{lb}"),
                o3=prog._n(f"o3{lb}"))


def build_phase_b(ntile=NTILE):
    nc = bass.Bass("TRN2", target_bir_lowering=False)
    prog = Prog()
    a = Arena(nc, "arena", 100 * 1024)
    mk = make_mk(nc, {}, False)
    ends = emit_phase_b(nc, prog, a, mk, ntile=ntile)
    prog.op("p", lambda e: e.nop(), waits=tuple(ends.values()))
    with nc.allow_low_precision(reason="bf16 pipeline per problem statement"):
        prog.emit(nc)
    a.close()
    return nc


def emit_phase_c(nc, prog, a, mk, ngrp=TC // GRP, fused=False, wq_evs=()):
    d_xn = mk("xn_tm", [TC, H], F32, "ExternalInput")
    d_s = mk("s_out" if fused else "s_in", [TC, H], F32, "ExternalInput")
    d_x = mk("x", [TC + 3, H], F32, "ExternalInput")
    if fused:
        d_wgq = mk("wgq_g", [I_FFN, H], BF16, "ExternalInput")
        d_wuq = mk("wuq_g", [I_FFN, H], BF16, "ExternalInput")
        d_wdq = mk("wdq_g", [H, I_FFN], BF16, "ExternalInput")
    else:
        d_wgq = mk("wgq", [I_FFN, H], BF16, "ExternalInput")
        d_wuq = mk("wuq", [I_FFN, H], BF16, "ExternalInput")
        d_wdq = mk("wdq", [H, I_FFN], BF16, "ExternalInput")
    d_repm32 = mk("repm32c", [128, 704], F32, "ExternalInput")
    d_repmb = mk("repmb", [128, QW_MAX], BF16, "ExternalInput")
    d_ident = mk("ident", [128, 128], BF16, "ExternalInput")
    o_out = mk("out", [TC, H], F32, "ExternalOutput")

    op = prog.op

    wgqT = a.alloc([128, 8, I_FFN], BF16)
    wuqT = a.alloc([128, 8, I_FFN], BF16)
    wdqT = a.alloc([128, 22, H], BF16)
    repm32 = a.alloc([128, 704], F32)
    repmb = a.alloc([128, QW_MAX], BF16)
    ident = a.alloc([128, 128], BF16)
    xn_g = a.alloc([128, H], F32)
    x_g = a.alloc([128, H], F32)
    s_gc = a.alloc([128, H], F32)
    xq = a.alloc([128, H], BF16)
    hbuf = a.alloc([128, I_FFN], F32)
    hq = a.alloc([128, I_FFN], BF16)
    hqT = a.alloc([128, 22, 128], BF16)
    xqT = hqT[:, 0:8, :]
    g_g = a.alloc([128, H], F32)
    qs = emit_quant_scratch(a)
    sg1 = g_g[:, 0:512]
    wld = hbuf.bitcast(BF16)[:, 0:I_FFN]
    t1 = xn_g

    pg = []
    def palloc(name, shape, dtype):
        g = nc.psum_tensor(prog.ns + name, shape, dtype)
        t = g.__enter__(); pg.append(g)
        return t[:]
    pA = palloc("pA", [128, 512], F32)
    pB = palloc("pB", [128, 512], F32)
    pS = palloc("pS", [128, 1024], F32)
    pTR = palloc("pTR", [128, 1024], BF16)

    op("q", lambda e: e.dma_start(out=repm32, in_=d_repm32), sig="c_repm")
    op("q", lambda e: e.dma_start(out=repmb, in_=d_repmb), sig="c_repmb")
    op("q", lambda e: e.dma_start(out=ident, in_=d_ident), sig="c_id")

    def wtrans(nm, dram, dst, rows, cols, prev_ev):
        nrc = rows // 128
        ncc = cols // 128
        last_evac = prev_ev
        for rc in range(nrc):
            wlw = [f"{nm}_r{rc-1}_done"] if rc > 0 else (
                ([prev_ev] if prev_ev else []) + list(wq_evs))
            op("q", lambda e, rc=rc, cols=cols: e.dma_start(
                out=wld[:, 0:cols], in_=dram[rc * 128:(rc + 1) * 128, :]),
               waits=tuple(wlw), sig=f"{nm}_r{rc}_ld")
            evs = []
            for cc in range(ncc):
                jj = cc % 8
                wt = []
                if jj == 0:
                    wt = [f"{nm}_r{rc}_ld", "c_id"]
                    if cc >= 8:
                        wt.append(f"{nm}_r{rc}_e{cc - 8}")
                    elif last_evac:
                        wt.append(last_evac)
                op("t", lambda e, cc=cc, jj=jj: e.transpose(
                    pTR[:, jj * 128:(jj + 1) * 128],
                    wld[:, cc * 128:(cc + 1) * 128], ident),
                   waits=tuple(wt),
                   sig=f"{nm}_r{rc}_t{cc}" if (jj == 7 or cc == ncc - 1)
                   else None)
                if jj == 7 or cc == ncc - 1:
                    c0 = (cc // 8) * 8
                    nn = cc - c0 + 1
                    ev = op("v", lambda e, rc=rc, c0=c0, nn=nn:
                            e.tensor_copy(
                                dst[:, c0:c0 + nn, rc * 128:(rc + 1) * 128],
                                pTR[:, 0:nn * 128].rearrange(
                                    "p (c t) -> p c t", t=128)),
                            waits=(f"{nm}_r{rc}_t{cc}",),
                            sig=f"{nm}_r{rc}_e{c0}")
                    evs.append(ev)
                    last_evac = ev
            op("v", lambda e: e.tensor_copy(xn_g[0:1, 0:1], wld[0:1, 0:1]),
               waits=tuple(evs), sig=f"{nm}_r{rc}_done")
        return last_evac, f"{nm}_r{nrc-1}_done"

    ev_wg_e, ev_wg = wtrans("twg", d_wgq, wgqT, I_FFN, H,
                            wq_evs[0] if wq_evs else None)
    ev_wu_e, ev_wu = wtrans("twu", d_wuq, wuqT, I_FFN, H, ev_wg)
    ev_wd_e, ev_wd = wtrans("twd", d_wdq, wdqT, H, I_FFN, ev_wu)

    ISL = [(0, 512), (512, 1024), (1024, 1536), (1536, 2048), (2048, 2560),
           (2560, 2816)]

    for gi in range(ngrp):
        gl = f"_g{gi}"
        glm = f"_g{gi-1}"
        gt0 = gi * GRP
        wxn = [f"xq{glm}_h1_f"] if gi > 0 else [ev_wd]
        op("q", lambda e, gt0=gt0: e.dma_start(out=xn_g,
                                               in_=d_xn[gt0:gt0 + GRP, :]),
           waits=tuple(wxn), sig=f"xn_ld{gl}")
        wx = [f"out_st{glm}"] if gi > 0 else []
        op("q", lambda e, gt0=gt0: e.dma_start(
            out=x_g, in_=d_x[3 + gt0:3 + gt0 + GRP, :]),
           waits=tuple(wx), sig=f"x_ld{gl}")
        ws = [f"t1{glm}"] if gi > 0 else []
        op("q", lambda e, gt0=gt0: e.dma_start(out=s_gc,
                                               in_=d_s[gt0:gt0 + GRP, :]),
           waits=tuple(ws), sig=f"s_ld{gl}")
        for hw in range(2):
            w0 = hw * 512
            ev_xq = emit_quant_tile(
                prog, qs, f"xq{gl}_h{hw}_f", xn_g[:, w0:w0 + 512],
                xq[:, w0:w0 + 512], 128, 512, repm32, repmb,
                (f"xn_ld{gl}", "c_repm", "c_repmb"),
                fresh=(gi == 0 and hw == 0))
        for j in range(8):
            wt = []
            if j == 0:
                wt = [ev_xq]
                wt.append(f"hqT_ev{glm}_b2" if gi > 0 else ev_wd_e)
            op("t", lambda e, j=j: e.transpose(
                pTR[:, j * 128:(j + 1) * 128], xq[:, j * 128:(j + 1) * 128],
                ident),
               waits=tuple(wt), sig=f"xqtr{gl}" if j == 7 else None)
        wxv = [f"xqtr{gl}"]
        if gi > 0:
            wxv += [f"gumm{glm}_s5", f"ffnmm{glm}_n1"]
        op("v", lambda e: e.tensor_copy(
            xqT, pTR.rearrange("p (c t) -> p c t", t=128)),
           waits=tuple(wxv), sig=f"xqT_ev{gl}")
        for si, (i0, i1) in enumerate(ISL):
            wm = [f"xqT_ev{gl}"]
            if si == 0:
                if gi > 0:
                    wm.append(f"h_ev{glm}_s5")
            else:
                wm.append(f"h_ev{gl}_s{si-1}")
            for c in range(8):
                op("t", lambda e, c=c, i0=i0, i1=i1: e.matmul(
                    pA[:, 0:i1 - i0], xqT[:, c, :], wgqT[:, c, i0:i1],
                    start=(c == 0), stop=(c == 7)),
                   waits=tuple(wm) if c == 0 else ())
            for c in range(8):
                op("t", lambda e, c=c, i0=i0, i1=i1: e.matmul(
                    pB[:, 0:i1 - i0], xqT[:, c, :], wuqT[:, c, i0:i1],
                    start=(c == 0), stop=(c == 7)),
                   sig=f"gumm{gl}_s{si}" if c == 7 else None)
            wsg = [f"gumm{gl}_s{si}"]
            op("s", lambda e, i0=i0, i1=i1: e.activation(
                sg1[:, 0:i1 - i0], pA[:, 0:i1 - i0], AF.Silu),
               waits=tuple(wsg), sig=f"sg{gl}_s{si}")
            op("v", lambda e, i0=i0, i1=i1: e.tensor_tensor(
                hbuf[:, i0:i1], sg1[:, 0:i1 - i0], pB[:, 0:i1 - i0], ALU.mult),
               waits=(f"sg{gl}_s{si}",), sig=f"h_ev{gl}_s{si}")
        ev_hq = []
        for hw in range(4):
            w0 = hw * (I_FFN // 4)
            ev_hq.append(emit_quant_tile(
                prog, qs, f"hq{gl}_f{hw}", hbuf[:, w0:w0 + I_FFN // 4],
                hq[:, w0:w0 + I_FFN // 4], 128, I_FFN // 4, repm32, repmb,
                (f"h_ev{gl}_s5",)))
        for j in range(22):
            jj = j % 8
            wt = []
            if jj == 0:
                wt = list(ev_hq)
                if j > 0:
                    wt.append(f"hqT_ev{gl}_b{j // 8 - 1}")
                else:
                    wt.append(f"xqT_ev{gl}")
            op("t", lambda e, j=j, jj=jj: e.transpose(
                pTR[:, jj * 128:(jj + 1) * 128], hq[:, j * 128:(j + 1) * 128],
                ident),
               waits=tuple(wt),
               sig=f"hqtr{gl}_b{j // 8}" if (jj == 7 or j == 21) else None)
            if jj == 7 or j == 21:
                c0 = (j // 8) * 8
                nn = j - c0 + 1
                whv = [f"hqtr{gl}_b{j // 8}"]
                if j // 8 == 0 and gi > 0:
                    whv.append(f"ffnmm{glm}_n1")
                op("v", lambda e, c0=c0, nn=nn: e.tensor_copy(
                    hqT[:, c0:c0 + nn, :],
                    pTR[:, 0:nn * 128].rearrange("p (c t) -> p c t", t=128)),
                   waits=tuple(whv), sig=f"hqT_ev{gl}_b{j // 8}")
        for ns in range(2):
            wd = [f"hqT_ev{gl}_b2"]
            if gi > 0 and ns == 0:
                wd.append(f"g_ev{glm}")
            for c in range(22):
                op("t", lambda e, ns=ns, c=c: e.matmul(
                    pS[:, ns * 512:(ns + 1) * 512], hqT[:, c, :],
                    wdqT[:, c, ns * 512:(ns + 1) * 512],
                    start=(c == 0), stop=(c == 21)),
                   waits=tuple(wd) if c == 0 else (),
                   sig=f"ffnmm{gl}_n{ns}" if c == 21 else None)
        wgv = [f"ffnmm{gl}_n0", f"ffnmm{gl}_n1"]
        if gi > 0:
            wgv.append(f"t1{glm}")
        op("s", lambda e: e.activation(g_g, pS, AF.Silu), waits=tuple(wgv),
           sig=f"g_ev{gl}")
        op("v", lambda e: e.tensor_tensor(t1, s_gc, g_g, ALU.mult),
           waits=(f"g_ev{gl}", f"s_ld{gl}"), sig=f"t1{gl}")
        op("v", lambda e: e.tensor_tensor(x_g, t1, x_g, ALU.add),
           waits=(f"x_ld{gl}",), sig=f"out{gl}")
        op("q", lambda e, gt0=gt0: e.dma_start(out=o_out[gt0:gt0 + GRP, :],
                                               in_=x_g),
           waits=(f"out{gl}",), sig=f"out_st{gl}")

    for g in reversed(pg):
        g.__exit__(None, None, None)
    return dict(out_st=prog._n(f"out_st_g{ngrp-1}"))


def build_phase_c(ngrp=TC // GRP):
    nc = bass.Bass("TRN2", target_bir_lowering=False)
    prog = Prog()
    a = Arena(nc, "arena", 102 * 1024)
    mk = make_mk(nc, {}, False)
    ends = emit_phase_c(nc, prog, a, mk, ngrp=ngrp)
    prog.op("p", lambda e: e.nop(), waits=tuple(ends.values()))
    with nc.allow_low_precision(reason="bf16 pipeline per problem statement"):
        prog.emit(nc)
    a.close()
    return nc


def build_fused(ntile=NTILE, ngrp=TC // GRP):
    nc = bass.Bass("TRN2", target_bir_lowering=False)
    prog = Prog()
    a = Arena(nc, "arena", 102 * 1024)
    cache = {}
    mk = make_mk(nc, cache, True)

    qeT_all = a.alloc([64, TC], BF16)
    E_all = a.alloc([64, 64], F32)
    a.mark()

    prog.ns = "A."
    ends_a = emit_phase_a(nc, prog, a, mk, ntile=ntile, fused=True,
                          qeT_buf=qeT_all, E_buf=E_all)
    prog.ns = ""

    d_Send = cache["Send"]
    d_Sg = mk("Sgather", [64 * NCORE, DI], F32, "Internal")
    d_wgg = mk("wgq_g", [I_FFN, H], BF16, "Internal")
    d_wug = mk("wuq_g", [I_FFN, H], BF16, "Internal")
    d_wdg = mk("wdq_g", [H, I_FFN], BF16, "Internal")
    rg = [list(range(NCORE))]
    prog.op("pc", lambda e: e.collective_compute(
        "AllGather", ALU.bypass, replica_groups=rg, ins=[d_Send.opt()],
        outs=[d_Sg.opt()]), waits=(ends_a["S_st"],), sig="@cc_S")
    prog.op("pc", lambda e: e.collective_compute(
        "AllGather", ALU.bypass, replica_groups=rg,
        ins=[cache["wgq_sh"].opt()],
        outs=[d_wgg.opt()]), waits=(ends_a["wq_st"],), sig="@cc_wg")
    prog.op("pc", lambda e: e.collective_compute(
        "AllGather", ALU.bypass, replica_groups=rg,
        ins=[cache["wuq_sh"].opt()],
        outs=[d_wug.opt()]), sig="@cc_wu")
    prog.op("pc", lambda e: e.collective_compute(
        "AllGather", ALU.bypass, replica_groups=rg,
        ins=[cache["wdq_sh"].opt()],
        outs=[d_wdg.opt()]), sig="@cc_wd")

    prog.barrier("ab", list(ends_a.values()))
    a.reset()
    a.mark()

    prog.ns = "B."
    ends_b = emit_phase_b(nc, prog, a, mk, ntile=ntile, fused=True,
                          qeT_buf=qeT_all, E_buf=E_all, d_Sgather=d_Sg,
                          cc_ev="@cc_S")
    prog.ns = ""

    prog.barrier("bc", list(ends_b.values()))
    a.reset()
    a.clear()

    prog.ns = "C."
    ends_c = emit_phase_c(nc, prog, a, mk, ngrp=ngrp, fused=True,
                          wq_evs=("@cc_wg", "@cc_wu", "@cc_wd"))
    prog.ns = ""

    prog.op("p", lambda e: e.nop(), waits=tuple(ends_c.values()))
    with nc.allow_low_precision(reason="bf16 pipeline per problem statement"):
        prog.emit(nc)
    a.close()
    return nc


def _host_prep(inputs):
    f = lambda k: np.ascontiguousarray(inputs[k], dtype=np.float32)
    up_w = f("L0_attn_up_w"); gate_w = f("L0_attn_gate_w")
    conv_w = f("L0_attn_conv_w"); conv_b = f("L0_attn_conv_b")
    q_w = f("L0_attn_q_w"); k_w = f("L0_attn_k_w")
    g_w = f("L0_attn_g_w"); g_b = f("L0_attn_g_b")
    tau_w = f("L0_attn_tau_w"); tau_b = f("L0_attn_tau_b")
    ig_w1 = f("L0_attn_ig_w1"); ig_w2 = f("L0_attn_ig_w2")
    ig_b2 = f("L0_attn_ig_b2"); rw_ = f("L0_attn_residual_weight")
    nw_ = f("L0_attn_norm_weight"); down_w = f("L0_attn_down_w")

    c = {}
    c["upT"] = np.ascontiguousarray(up_w.T)
    c["qkT"] = np.ascontiguousarray(np.concatenate([q_w.T, k_w.T], axis=1))
    c["gtT"] = np.ascontiguousarray(np.concatenate([g_w.T, tau_w.T], axis=1))
    c["ig1T"] = np.ascontiguousarray(ig_w1.T)
    c["ig2T"] = np.ascontiguousarray(ig_w2.T)
    c["convw"] = np.ascontiguousarray(
        conv_w.reshape(16, 128, 4).transpose(1, 0, 2).reshape(128, 64))
    c["convb"] = np.ascontiguousarray(conv_b.reshape(16, 128).T)
    c["igb2"] = np.ascontiguousarray(ig_b2.reshape(16, 128).T)
    c["rw"] = np.ascontiguousarray(rw_.reshape(16, 128).T)
    c["nw"] = np.ascontiguousarray(nw_.reshape(16, 128).T)
    c["gb"] = np.ascontiguousarray(g_b.reshape(64, 1))
    c["tb"] = np.ascontiguousarray(tau_b.reshape(64, 1))
    c["ident"] = np.eye(128, dtype=BF)
    c["id64f"] = np.eye(64, dtype=np.float32)
    jj, ii = np.meshgrid(np.arange(64), np.arange(64), indexing="ij")
    c["maskLT"] = (ii >= jj).astype(np.float32)
    c["ones64"] = np.ones((64, 64), np.float32)
    c["ones1x"] = np.ones((1, 64), dtype=BF)
    rm = np.ones((128, I_FFN), dtype=BF)
    rm[:, 0::16] = 0
    c["repm"] = rm
    c["repmb"] = np.ascontiguousarray(rm[:, 0:QW_MAX])
    rm32 = np.ones((128, 1024), dtype=np.float32)
    rm32[:, 0::16] = 0
    c["repm32"] = rm32
    c["gateT"] = np.ascontiguousarray(gate_w.T)
    c["downT"] = np.ascontiguousarray(down_w.T)
    c["wg"] = f("L0_ffn_w_gate"); c["wu"] = f("L0_ffn_w_up")
    c["wd"] = f("L0_ffn_w_down")
    c["x"] = f("L0_attn_hidden_states")
    return c


def _shard_x(x):
    out = []
    for cid in range(NCORE):
        b = cid // 2
        t0 = (cid % 2) * TC
        if t0 == 0:
            sl = np.concatenate([np.zeros((3, H), np.float32), x[b, 0:TC]], 0)
        else:
            sl = x[b, t0 - 3:t0 + TC]
        out.append(np.ascontiguousarray(sl, dtype=np.float32))
    return out


def kernel(**inputs):
    c = _host_prep(inputs)
    xs = _shard_x(c["x"])

    nc = build_fused()
    in_maps = []
    for cid in range(NCORE):
        r0 = cid * 352
        rd0 = cid * 128
        sel = np.zeros((64, 8), np.float32)
        if cid % 2 == 1:
            sel[:, cid - 1] = 1.0
        in_maps.append(dict(
            x=xs[cid], upT=c["upT"], qkT=c["qkT"], gtT=c["gtT"],
            ig1T=c["ig1T"], ig2T=c["ig2T"], convw=c["convw"], convb=c["convb"],
            igb2=c["igb2"], rw=c["rw"], gb=c["gb"], tb=c["tb"],
            ident=c["ident"], id64f=c["id64f"], maskLT=c["maskLT"],
            ones64=c["ones64"], ones1x=c["ones1x"], repm32=c["repm32"],
            wg_sh=np.ascontiguousarray(c["wg"][r0:r0 + 352]),
            wu_sh=np.ascontiguousarray(c["wu"][r0:r0 + 352]),
            wd_sh=np.ascontiguousarray(c["wd"][rd0:rd0 + 128]),
            sel=sel, gateT=c["gateT"], downT=c["downT"], nw=c["nw"],
            repm32c=c["repm32"][:, 0:704].copy(), repmb=c["repmb"],
        ))
    rc = run_bass_kernel_spmd(nc, in_maps, core_ids=list(range(NCORE))).results

    out = np.empty((B, T, H), np.float32)
    for cid in range(NCORE):
        b = cid // 2
        t0 = (cid % 2) * TC
        out[b, t0:t0 + TC] = rc[cid]["out"]
    return out


def kernel_unfused(**inputs):
    c = _host_prep(inputs)
    xs = _shard_x(c["x"])

    nc_a = build_phase_a()
    in_maps = []
    for cid in range(NCORE):
        r0 = cid * 352
        rd0 = cid * 128
        in_maps.append(dict(
            x=xs[cid], upT=c["upT"], qkT=c["qkT"], gtT=c["gtT"],
            ig1T=c["ig1T"], ig2T=c["ig2T"], convw=c["convw"], convb=c["convb"],
            igb2=c["igb2"], rw=c["rw"], gb=c["gb"], tb=c["tb"],
            ident=c["ident"], id64f=c["id64f"], maskLT=c["maskLT"],
            ones64=c["ones64"], ones1x=c["ones1x"],
            repm32=c["repm32"], repmb=c["repmb"],
            wg_sh=np.ascontiguousarray(c["wg"][r0:r0 + 352]),
            wu_sh=np.ascontiguousarray(c["wu"][r0:r0 + 352]),
            wd_sh=np.ascontiguousarray(c["wd"][rd0:rd0 + 128]),
        ))
    ra = run_bass_kernel_spmd(nc_a, in_maps, core_ids=list(range(NCORE))).results

    wgq = np.ascontiguousarray(
        np.concatenate([ra[i]["wgq_sh"] for i in range(NCORE)], 0))
    wuq = np.ascontiguousarray(
        np.concatenate([ra[i]["wuq_sh"] for i in range(NCORE)], 0))
    wdq = np.ascontiguousarray(
        np.concatenate([ra[i]["wdq_sh"] for i in range(NCORE)], 0))

    nc_b = build_phase_b()
    in_maps = []
    for cid in range(NCORE):
        sin = (np.zeros((64, DI), np.float32) if cid % 2 == 0
               else np.ascontiguousarray(ra[cid - 1]["Send"], np.float32))
        in_maps.append(dict(
            xnT=ra[cid]["xnT"], oL=ra[cid]["oL"], srwT=ra[cid]["srwT"],
            qeT=ra[cid]["qeT"], Ecols=ra[cid]["Ecols"], Sin=sin,
            gateT=c["gateT"], downT=c["downT"], nw=c["nw"], ident=c["ident"]))
    rb = run_bass_kernel_spmd(nc_b, in_maps, core_ids=list(range(NCORE))).results

    nc_c = build_phase_c()
    in_maps = []
    for cid in range(NCORE):
        in_maps.append(dict(
            xn_tm=ra[cid]["xn_tm"], s_in=rb[cid]["s_out"], x=xs[cid],
            wgq=wgq, wuq=wuq, wdq=wdq, repm32c=c["repm32"][:, 0:704].copy(),
            repmb=c["repmb"], ident=c["ident"]))
    rc = run_bass_kernel_spmd(nc_c, in_maps, core_ids=list(range(NCORE))).results

    out = np.empty((B, T, H), np.float32)
    for cid in range(NCORE):
        b = cid // 2
        t0 = (cid % 2) * TC
        out[b, t0:t0 + TC] = rc[cid]["out"]
    return out


# version 09
# speedup vs baseline: 1.1874x; 1.0096x over previous
import numpy as np
import ml_dtypes
import concourse.bass as bass
import concourse.mybir as mybir
from concourse.bass_utils import run_bass_kernel_spmd

F32 = mybir.dt.float32
BF16 = mybir.dt.bfloat16
I16 = mybir.dt.int16
AF = mybir.ActivationFunctionType
ALU = mybir.AluOpType
AX = mybir.AxisListType
BF = ml_dtypes.bfloat16

B, T, H = 4, 8192, 1024
DI = 2048
R = 16
CH = 64
NCORE = 8
TC = T // 2
I_FFN = 2816
TT = 512
NTILE = TC // TT
NCHT = TT // CH
GRP = 128
SCALE = 0.125
LN_EPS = 1e-5
NORM_EPS = 1e-5


class Prog:

    def __init__(self):
        self.ops = []
        self.events = {}
        self.counts = {"q": 0, "t": 0, "v": 0, "s": 0, "p": 0, "pc": 0}
        self.ns = ""

    def _n(self, name):
        if name.startswith("@"):
            return name
        if self.ns and name.startswith(self.ns):
            return name
        return self.ns + name

    def op(self, eng, fn, waits=(), sig=None):
        if isinstance(waits, str):
            waits = (waits,)
        waits = tuple(self._n(w) for w in waits if w is not None)
        if sig is not None:
            sig = self._n(sig)
        for w in waits:
            assert w in self.events, f"wait on unplanned event {w}"
        self.ops.append((eng, fn, waits, sig))
        if sig is not None:
            self.counts[eng] += 16 if eng == "q" else 1
            assert sig not in self.events, f"dup event {sig}"
            self.events[sig] = (eng, self.counts[eng])
        return sig

    def barrier(self, tag, end_events):
        for eng in ("q", "t", "v", "s", "p"):
            self.op(eng, lambda e: e.nop(), waits=tuple(end_events),
                    sig=f"@bar_{tag}_{eng}")
        return [f"@bar_{tag}_{e}" for e in ("q", "t", "v", "s", "p")]

    def emit(self, nc):
        sems = {}
        ctxs = []
        for key in self.counts:
            g = nc.semaphore(f"sem_{key}")
            sems[key] = g.__enter__()
            ctxs.append(g)
        blk_g = nc.Block()
        block = blk_g.__enter__()
        ctxs.append(blk_g)

        name_map = {"q": "sync", "t": "tensor", "v": "vector", "s": "scalar",
                    "p": "gpsimd", "pc": "gpsimd"}
        streams = {v: [] for v in name_map.values()}
        for o in self.ops:
            streams[name_map[o[0]]].append(o)
        events = self.events

        def run_stream(eng_handle, ops):
            waited = {}
            for eng_key, fn, waits, sig in ops:
                for w in waits:
                    wk, wv = events[w]
                    if waited.get(wk, -1) < wv:
                        eng_handle.wait_ge(sems[wk], wv)
                        waited[wk] = wv
                inst = fn(eng_handle)
                if sig is not None:
                    ek, _ = events[sig]
                    inst.then_inc(sems[ek], 16 if ek == "q" else 1)

        for name in ("sync", "tensor", "vector", "scalar", "gpsimd"):
            ops = streams[name]
            if not ops:
                continue
            def make(oplist):
                def body(eng):
                    run_stream(eng, oplist)
                return body
            getattr(block, name)(make(ops))

        for g in reversed(ctxs):
            g.__exit__(None, None, None)


class Arena:
    def __init__(self, nc, name, cols_bf16):
        self.g = nc.sbuf_tensor(name, [128, cols_bf16], BF16)
        self.t = self.g.__enter__()
        self.cols = cols_bf16
        self.off = 0
        self.marks = []

    def alloc(self, shape, dtype=BF16):
        n = 1
        for d in shape[1:]:
            n *= d
        w = {BF16: 1, I16: 1, F32: 2}[dtype]
        if self.off % 2:
            self.off += 1
        start = self.off
        self.off += n * w
        assert self.off <= self.cols, f"arena overflow {self.off} > {self.cols}"
        view = self.t[:, start:start + n * w]
        if dtype != BF16:
            view = view.bitcast(dtype)
        if shape[0] != 128:
            view = view[0:shape[0]]
        if len(shape) > 2:
            names = " ".join(f"d{i}" for i in range(1, len(shape)))
            kw = {f"d{i}": shape[i] for i in range(2, len(shape))}
            view = view.rearrange(f"p ({names}) -> p {names}", **kw)
        return view

    def alloc_pair(self, shape, dtype=F32):
        assert shape[0] == 64
        full = self.alloc([128] + list(shape[1:]), dtype)
        return full[0:64], full[64:128]

    def clear(self):
        self.off = 0
        self.marks = []

    def mark(self):
        self.marks.append(self.off)

    def reset(self):
        self.off = self.marks.pop()

    def close(self):
        self.g.__exit__(None, None, None)


def emit_ln_group(prog, nm, xg, xng, ln_m, ln_v, ln_tr, P, waits):
    op = prog.op
    op("v", lambda e: e.tensor_reduce(ln_m[0:P], xg[0:P], AX.X, ALU.add),
       waits=waits, sig=nm + "_red")
    op("v", lambda e: e.tensor_scalar(ln_m[0:P], ln_m[0:P], -1.0 / H, None,
                                      ALU.mult), waits=(nm + "_red",),
       sig=nm + "_m")
    op("v", lambda e: e.memset(ln_v[0:P], 0.0))
    op("v", lambda e: e.tensor_scalar(ln_tr[0:P], xg[0:P], ln_m[0:P], None,
                                      ALU.add), waits=(nm + "_m",),
       sig=nm + "_c")
    op("s", lambda e: e.activation(xg[0:P], ln_tr[0:P], AF.Square,
                                   accum_out=ln_v[0:P]),
       waits=(nm + "_c",), sig=nm + "_sq")
    op("v", lambda e: e.tensor_scalar(ln_v[0:P], ln_v[0:P], 1.0 / H, LN_EPS,
                                      ALU.mult, ALU.add), waits=(nm + "_sq",),
       sig=nm + "_t")
    op("v", lambda e: e.reciprocal(ln_v[0:P], ln_v[0:P]),
       waits=(nm + "_t",), sig=nm + "_r")
    op("s", lambda e: e.activation(ln_v[0:P], ln_v[0:P], AF.Sqrt),
       waits=(nm + "_r",), sig=nm + "_rs")
    op("v", lambda e: e.tensor_scalar(xng[0:P], ln_tr[0:P], ln_v[0:P],
                                      None, ALU.mult),
       waits=(nm + "_rs",), sig=nm)
    op("v", lambda e: e.tensor_scalar(xg[0:P], ln_tr[0:P], ln_v[0:P],
                                      None, ALU.mult), sig=nm + "_f")
    return nm


QW_MAX = 704


def emit_quant_scratch(a):
    q = {}
    q["d1"] = a.alloc([128, QW_MAX], F32)
    q["d1b"] = a.alloc([128, QW_MAX], BF16)
    q["az"] = a.alloc([128, QW_MAX], F32)
    q["rcpx"] = a.alloc([128, QW_MAX], F32)
    q["scx0"] = a.alloc([128, QW_MAX], BF16)
    q["scx1"] = a.alloc([128, QW_MAX], BF16)
    q["zs"] = a.alloc([128, QW_MAX], F32)
    q["i1"] = a.alloc([128, QW_MAX], I16)
    q["i2"] = a.alloc([128, QW_MAX], I16)
    q["c1"] = a.alloc([128, QW_MAX], BF16)
    q["c2"] = a.alloc([128, QW_MAX], BF16)
    q["q1"] = a.alloc([128, QW_MAX], BF16)
    q["sgn"] = a.alloc([128, QW_MAX], BF16)
    q["bm1"] = a.alloc([128, QW_MAX // 16], F32)
    q["sc32"] = a.alloc([128, QW_MAX // 16], F32)
    q["rcp32"] = a.alloc([128, QW_MAX // 16], F32)
    q["scb"] = a.alloc([128, QW_MAX // 16], BF16)
    q["_prev"] = {}
    q["_vl"] = None
    return q


def emit_quant_pieces(prog, q, names, xfs, wouts, Ps, Ws, repm32, repmb,
                      waits, fresh=False):
    n = len(names)
    pv = q["_prev"]
    wl = waits if isinstance(waits, list) else [tuple(waits)] * n

    def vop(fn, waits=(), sig=None):
        w = tuple(waits) + ((q["_vl"],) if q["_vl"] else ())
        prog.op("v", fn, waits=w, sig=sig)
        q["_vl"] = sig
        return sig

    def sop(fn, waits=(), sig=None):
        prog.op("s", fn, waits=tuple(waits), sig=sig)
        return sig

    def sl(name, k):
        return q[name][0:Ps[k], 0:Ws[k]]

    def slb(name, k):
        return q[name][0:Ps[k], 0:Ws[k] // 16]

    def head(k):
        nm, P, W = names[k], Ps[k], Ws[k]
        NB = W // 16
        xf, d1, d1b = xfs[k], sl("d1", k), sl("d1b", k)
        bm1, sc32 = slb("bm1", k), slb("sc32", k)
        rcp32, scb = slb("rcp32", k), slb("scb", k)
        scx = sl("scx1" if k % 2 else "scx0", k)
        sop(lambda e: e.activation(sl("az", k), xf, AF.Abs),
            waits=tuple(wl[k]) + pv.get("az", ()), sig=nm + "_az")
        vop(lambda e: e.tensor_reduce(
            bm1.unsqueeze(2), xf.rearrange("p (nb bk) -> p nb bk", bk=16),
            AX.X, ALU.max, apply_absolute_value=True),
            waits=wl[k], sig=nm + "_bm")
        vop(lambda e: e.tensor_scalar(sc32, bm1, 1.0 / 6.0, 1e-12,
                                      ALU.mult, ALU.max), sig=nm + "_s32")
        vop(lambda e: e.reciprocal(rcp32, sc32), waits=(nm + "_s32",),
            sig=nm + "_rc")
        vop(lambda e: e.tensor_copy(scb, sc32), waits=(nm + "_s32",),
            sig=nm + "_sb")
        if fresh and k == 0:
            vop(lambda e: e.memset(q["d1"][:], 0.0), sig=nm + "_m1")
            vop(lambda e: e.memset(q["d1b"][:], 0.0), sig=nm + "_m2")
        vop(lambda e: e.tensor_copy(
            d1.rearrange("p (nb bk) -> p nb bk", bk=16)[:, :, 0:1],
            rcp32.unsqueeze(2)), sig=nm + "_d1")
        vop(lambda e: e.tensor_tensor_scan(sl("rcpx", k), repm32[0:P, 0:W],
                                           d1, 0.0, ALU.mult, ALU.add),
            sig=nm + "_rx")
        vop(lambda e: e.tensor_copy(
            d1b.rearrange("p (nb bk) -> p nb bk", bk=16)[:, :, 0:1],
            scb.unsqueeze(2)), sig=nm + "_d2")
        vop(lambda e: e.tensor_tensor_scan(scx, repmb[0:P, 0:W], d1b, 0.0,
                                           ALU.mult, ALU.add), sig=nm + "_sx")

    def zsop(k):
        nm = names[k]
        vop(lambda e: e.tensor_tensor(sl("zs", k), sl("az", k), sl("rcpx", k),
                                      ALU.mult),
            waits=(nm + "_az", nm + "_rx") + pv.get("zs", ()),
            sig=nm + "_zs")
        pv["az"] = (nm + "_zs",)

    def tail(k):
        nm = names[k]
        xf, wout = xfs[k], wouts[k]
        zs, i1, i2 = sl("zs", k), sl("i1", k), sl("i2", k)
        c1, c2, q1, sgn = sl("c1", k), sl("c2", k), sl("q1", k), sl("sgn", k)
        scx = sl("scx1" if k % 2 else "scx0", k)
        sop(lambda e: e.activation(sgn, xf, AF.Sign),
            waits=pv.get("sgn", ()), sig=nm + "_sg")
        sop(lambda e: e.activation(i1, zs, AF.Copy, scale=2.0),
            waits=(nm + "_zs",) + pv.get("i1", ()), sig=nm + "_n1")
        sop(lambda e: e.activation(i2, zs, AF.Copy, bias=-2.0),
            waits=pv.get("i2", ()), sig=nm + "_n2")
        vop(lambda e: e.tensor_scalar(c1, i1, 0, 4, ALU.max, ALU.min),
            waits=(nm + "_n1",), sig=nm + "_c1")
        vop(lambda e: e.tensor_scalar(c2, i2, 0, 2, ALU.max, ALU.min),
            waits=(nm + "_n2",), sig=nm + "_c2")
        vop(lambda e: e.scalar_tensor_tensor(q1, c1, 0.5, c2,
                                             ALU.mult, ALU.add),
            sig=nm + "_t1")
        sop(lambda e: e.activation(i1, zs, AF.Copy, bias=-4.5),
            waits=(nm + "_c1",), sig=nm + "_n3")
        vop(lambda e: e.tensor_scalar(c2, i1, 0, 1, ALU.max, ALU.min),
            waits=(nm + "_n3",), sig=nm + "_c3")
        vop(lambda e: e.scalar_tensor_tensor(q1, c2, 2.0, q1,
                                             ALU.mult, ALU.add),
            sig=nm + "_t2")
        vop(lambda e: e.tensor_tensor(q1, q1, sgn, ALU.mult),
            waits=(nm + "_sg",), sig=nm + "_sm")
        vop(lambda e: e.tensor_tensor(wout, q1, scx, ALU.mult),
            waits=(nm + "_sx",), sig=nm)
        pv["sgn"] = (nm + "_sm",)
        pv["i1"] = (nm + "_c3",)
        pv["i2"] = (nm + "_c2",)
        pv["zs"] = (nm + "_n3",)

    head(0)
    zsop(0)
    for k in range(1, n):
        head(k)
        tail(k - 1)
        zsop(k)
    tail(n - 1)
    return [nm for nm in names]


def emit_quant_tile(prog, q, nm, xf, wout, P, W, repm32, repmb, waits,
                    fresh=False):
    return emit_quant_pieces(prog, q, [nm], [xf], [wout], [P], [W],
                             repm32, repmb, waits, fresh=fresh)[0]


def emit_cast_weight(prog, nm, dram, sb_dst, rows, cols, wstage, crows=128,
                     prev=None):
    op = prog.op
    nch = (rows + crows - 1) // crows
    last = None
    for c in range(nch):
        pr = min(crows, rows - c * crows)
        w = (f"{nm}_c{c-1}",) if c >= 1 else ((prev,) if prev else ())
        op("q", lambda e, c=c, pr=pr, cols=cols: e.dma_start(
            out=wstage[0:pr, 0:cols], in_=dram[c * crows:c * crows + pr, :]),
            waits=w, sig=f"{nm}_ld{c}")
        last = op("s", lambda e, c=c, pr=pr, cols=cols: e.copy(
            sb_dst[:, c, 0:cols][0:pr], wstage[0:pr, 0:cols]),
            waits=(f"{nm}_ld{c}",), sig=f"{nm}_c{c}")
    return last


FUSED_INTERNAL = {"xn_tm", "xnT", "srwT", "oL", "Send", "qeT", "Ecols",
                  "wgq_sh", "wuq_sh", "wdq_sh", "s_out", "Sgather",
                  "wgq_g", "wuq_g", "wdq_g"}


def make_mk(nc, cache, fused):
    def mk(name, shape, dtype, kind):
        if name in cache:
            return cache[name]
        if fused and name in FUSED_INTERNAL:
            kind = "Internal"
        ap = nc.dram_tensor(name, shape, dtype, kind=kind).ap()
        cache[name] = ap
        return ap
    return mk


def emit_phase_a(nc, prog, a, mk, ntile=NTILE, debug=False, fused=False,
                 qeT_buf=None, E_buf=None):
    d_x = mk("x", [TC + 3, H], F32, "ExternalInput")
    d_upT = mk("upT", [H, DI], F32, "ExternalInput")
    d_qkT = mk("qkT", [DI, 128], F32, "ExternalInput")
    d_gtT = mk("gtT", [DI, 128], F32, "ExternalInput")
    d_ig1T = mk("ig1T", [DI, R], F32, "ExternalInput")
    d_ig2T = mk("ig2T", [R, DI], F32, "ExternalInput")
    d_convw = mk("convw", [128, 64], F32, "ExternalInput")
    d_convb = mk("convb", [128, 16], F32, "ExternalInput")
    d_igb2 = mk("igb2", [128, 16], F32, "ExternalInput")
    d_rw = mk("rw", [128, 16], F32, "ExternalInput")
    d_gb = mk("gb", [64, 1], F32, "ExternalInput")
    d_tb = mk("tb", [64, 1], F32, "ExternalInput")
    d_ident = mk("ident", [128, 128], BF16, "ExternalInput")
    d_id64f = mk("id64f", [64, 64], F32, "ExternalInput")
    d_mask = mk("maskLT", [64, 64], F32, "ExternalInput")
    d_ones64 = mk("ones64", [64, 64], F32, "ExternalInput")
    d_ones1x = mk("ones1x", [1, 64], BF16, "ExternalInput")
    d_repm32 = mk("repm32", [128, 1024], F32, "ExternalInput")
    d_repmb = mk("repmb", [128, QW_MAX], BF16, "ExternalInput")
    d_wg_sh = mk("wg_sh", [352, H], F32, "ExternalInput")
    d_wu_sh = mk("wu_sh", [352, H], F32, "ExternalInput")
    d_wd_sh = mk("wd_sh", [128, I_FFN], F32, "ExternalInput")

    o_xn = mk("xn_tm", [TC, H], F32, "ExternalOutput")
    o_xnT = mk("xnT", [H, TC], BF16, "ExternalOutput")
    o_srwT = mk("srwT", [TC, DI], BF16, "ExternalOutput")
    o_oL = mk("oL", [TC, DI], BF16, "ExternalOutput")
    o_Send = mk("Send", [64, DI], F32, "ExternalOutput")
    o_qeT = mk("qeT", [64, TC], BF16, "ExternalOutput")
    o_E = mk("Ecols", [64, 64], F32, "ExternalOutput")
    o_wgq = mk("wgq_sh", [352, H], BF16, "ExternalOutput")
    o_wuq = mk("wuq_sh", [352, H], BF16, "ExternalOutput")
    o_wdq = mk("wdq_sh", [128, I_FFN], BF16, "ExternalOutput")
    if debug:
        o_dbg = mk("dbg", [128, 4 * H], BF16, "ExternalOutput")
        o_dbgf = mk("dbgf", [64, 64], F32, "ExternalOutput")

    op = prog.op

    upT = a.alloc([128, 8, DI], BF16)
    qkT = a.alloc([128, 16, 128], BF16)
    gtT = a.alloc([128, 16, 128], BF16)
    ig1T = a.alloc([128, 16, R], BF16)
    ig2T = a.alloc([16, 16, 128], BF16)
    convw = a.alloc([128, 16, 4], F32)
    convb = a.alloc([128, 16], F32)
    igb2 = a.alloc([128, 16], F32)
    rw = a.alloc([128, 16], F32)
    gb = a.alloc([64, 1], F32)
    tb = a.alloc([64, 1], F32)
    ident = a.alloc([128, 128], BF16)
    id64f = a.alloc([64, 64], F32)
    maskLT = a.alloc([64, 64], F32)
    ones64 = a.alloc([64, 64], F32)
    ones1x = a.alloc([1, 64], BF16)
    repm32 = a.alloc([128, 1024], F32)
    repmb = a.alloc([128, QW_MAX], BF16)
    S = a.alloc([64, DI], F32)
    Sb = a.alloc([64, DI], BF16)
    qeT_all = qeT_buf if qeT_buf is not None else a.alloc([64, TC], BF16)
    E_all = E_buf if E_buf is not None else a.alloc([64, 64], F32)
    Bacc = a.alloc([64, 16], F32)
    Pincl = a.alloc([64, 16], F32)

    a.mark()
    wstage = a.alloc([128, DI], F32)
    xg = a.alloc([128, H], F32)
    xng = a.alloc([128, H], BF16)
    ln_tr = a.alloc([128, H], F32)
    ln_m = a.alloc([128, 1], F32)
    ln_v = a.alloc([128, 1], F32)
    xnT_t = a.alloc([128, 8, TT], BF16)
    xup = a.alloc([128, 16, TT + 3], BF16)
    xup_halo = a.alloc([128, 16, 3], BF16)
    sh = a.alloc([128, 16, TT], BF16)
    srw_g = a.alloc([128, DI], BF16)
    u16 = a.alloc([16, TT], BF16)
    vt = a.alloc([64, NCHT, DI], BF16)
    qf = a.alloc([64, TT], F32)
    kf = a.alloc([64, TT], F32)
    gg = a.alloc([64, TT], F32)
    tau = a.alloc([64, TT], F32)
    tmp1 = a.alloc([64, TT], F32)
    tmp2 = a.alloc([64, TT], F32)
    tmp3 = a.alloc([64, TT], F32)
    ksq = tmp3
    rt = a.alloc([64, TT], F32)
    bpos = a.alloc([64, TT], F32)
    btr = a.alloc([64, TT], F32)
    d1b = tmp2
    rep64 = a.alloc([64, TT], F32)
    it = a.alloc([64, TT], F32)
    kn = a.alloc([64, TT], F32)
    kd = a.alloc([64, TT], F32)
    ke = a.alloc([64, TT], BF16)
    rcp1 = a.alloc([1, TT], F32)
    rcp1b = a.alloc([1, TT], BF16)
    btot = a.alloc([64, NCHT], F32)
    decay = a.alloc([64, NCHT], F32)
    tmpE = a.alloc([64, NCHT], F32)
    A_sb = a.alloc([64, 64], BF16)
    kdT = a.alloc([64, 64], BF16)
    oL_g = a.alloc([128, DI], BF16)

    pg = []
    def palloc(name, shape, dtype):
        g = nc.psum_tensor(prog.ns + name, shape, dtype)
        t = g.__enter__(); pg.append(g)
        return t[:]
    pA = palloc("pA", [128, TT], F32)
    pB = palloc("pB", [128, TT], F32)
    pTR = palloc("pTR", [128, 1024], BF16)
    pSH = palloc("pSH", [64, 512], F32)
    pO = palloc("pO", [64, 1024], F32)
    pS = palloc("pS", [128, 1024], F32)

    for nm, dst, src in [
        ("c_convw", convw.rearrange("p a b -> p (a b)"), d_convw),
        ("c_convb", convb, d_convb), ("c_igb2", igb2, d_igb2),
        ("c_rw", rw, d_rw), ("c_gb", gb, d_gb), ("c_tb", tb, d_tb),
        ("c_id", ident, d_ident), ("c_id64", id64f, d_id64f),
        ("c_mask", maskLT, d_mask), ("c_on", ones64, d_ones64),
        ("c_on1", ones1x, d_ones1x), ("c_repm32", repm32, d_repm32),
        ("c_repmb", repmb, d_repmb),
    ]:
        op("q", lambda e, dst=dst, src=src: e.dma_start(out=dst, in_=src), sig=nm)

    ev_up = emit_cast_weight(prog, "wup", d_upT, upT, H, DI, wstage)
    ev_qk = emit_cast_weight(prog, "wqk", d_qkT, qkT, DI, 128, wstage,
                             prev=ev_up)
    ev_gt = emit_cast_weight(prog, "wgt", d_gtT, gtT, DI, 128, wstage,
                             prev=ev_qk)
    ev_ig1 = emit_cast_weight(prog, "wig1", d_ig1T, ig1T, DI, R, wstage,
                              prev=ev_gt)
    op("q", lambda e: e.dma_start(out=wstage[0:16, 0:DI], in_=d_ig2T),
       waits=(ev_ig1,), sig="wig2_ld")
    ev_ig2 = op("s", lambda e: e.copy(ig2T.rearrange("p c f -> p (c f)"),
                                      wstage[0:16, 0:DI]),
                waits=("wig2_ld",), sig="wig2")

    op("v", lambda e: e.memset(S, 0.0), sig="S_init")
    op("v", lambda e: e.memset(Sb, 0.0), sig="Sb_init")
    op("v", lambda e: e.memset(Bacc[:, 0:1], 0.0), sig="Bacc_init")
    op("v", lambda e: e.memset(rep64, 1.0))
    op("v", lambda e: e.memset(
        rep64.rearrange("p (c t) -> p c t", t=CH)[:, :, 0:1], 0.0),
       sig="rep64_i")

    op("q", lambda e: e.dma_start(out=xg[0:3], in_=d_x[0:3, :]), sig="halo_ld")
    ev = emit_ln_group(prog, "halo_ln", xg, xng, ln_m, ln_v, ln_tr, 3,
                       ("halo_ld",))
    for j in range(8):
        op("t", lambda e, j=j: e.transpose(pTR[:, j * 128:j * 128 + 3],
                                           xng[0:3, 128 * j:128 * (j + 1)],
                                           ident[0:3, 0:3]),
           waits=(ev, "c_id") if j == 0 else (),
           sig="halo_tr" if j == 7 else None)
    op("v", lambda e: e.tensor_copy(
        xnT_t[:, :, 0:3],
        pTR.rearrange("p (c t) -> p c t", t=128)[:, :, 0:3]),
       waits=("halo_tr",), sig="halo_xnT")
    for d in range(16):
        wdd = []
        if d == 0:
            wdd = ["halo_xnT", ev_up]
        else:
            wdd = [f"halo_up{d-1}"]
        for c in range(8):
            op("t", lambda e, d=d, c=c: e.matmul(
                pA[:, 0:3], upT[:, c, 128 * d:128 * (d + 1)], xnT_t[:, c, 0:3],
                start=(c == 0), stop=(c == 7)),
               waits=tuple(wdd) if c == 0 else (),
               sig=f"halo_upm{d}" if c == 7 else None)
        op("s", lambda e, d=d: e.copy(xup_halo[:, d, :], pA[:, 0:3]),
           waits=(f"halo_upm{d}",), sig=f"halo_up{d}")

    for ti in range(ntile):
        t0 = ti * TT
        P = f"_t{ti}"
        Pm = f"_t{ti-1}"
        for g in range(4):
            gt0 = t0 + g * GRP
            if g > 0:
                wl = [f"ln{P}_g{g-1}_sq", f"xnT_tr{P}_g{g-1}",
                      f"xn_st{P}_g{g-1}"]
            elif ti > 0:
                wl = [f"ln{Pm}_g3_sq", f"xnT_tr{Pm}_g3", f"xn_st{Pm}_g3"]
            else:
                wl = ["halo_ln_sq", "halo_xnT"]
            op("q", lambda e, gt0=gt0: e.dma_start(
                out=xg, in_=d_x[3 + gt0:3 + gt0 + GRP, :]),
               waits=tuple(wl), sig=f"x_ld{P}_g{g}")
            ev = emit_ln_group(prog, f"ln{P}_g{g}", xg, xng, ln_m, ln_v, ln_tr,
                               128, (f"x_ld{P}_g{g}",))
            op("q", lambda e, gt0=gt0: e.dma_start(out=o_xn[gt0:gt0 + GRP, :],
                                                   in_=xg),
               waits=(ev + "_f",), sig=f"xn_st{P}_g{g}")

            wtr = [ev]
            if g > 0:
                wtr.append(f"xnT_ev{P}_g{g-1}")
            elif ti > 0:
                wtr.append(f"vt_ev{Pm}_c{NCHT-1}_h1")
            else:
                wtr.append("halo_xnT")
            for j in range(8):
                op("t", lambda e, j=j: e.transpose(
                    pTR[:, j * 128:(j + 1) * 128], xng[:, 128 * j:128 * (j + 1)],
                    ident),
                   waits=tuple(wtr) if j == 0 else (),
                   sig=f"xnT_tr{P}_g{g}" if j == 7 else None)
            wev = [f"xnT_tr{P}_g{g}"]
            if g == 0:
                if ti > 0:
                    wev += [f"up_mm{Pm}_d15", f"xnT_st{Pm}"]
                else:
                    wev.append("halo_up15")
            op("v", lambda e, g=g: e.tensor_copy(
                xnT_t[:, :, g * GRP:(g + 1) * GRP],
                pTR.rearrange("p (c t) -> p c t", t=128)),
               waits=tuple(wev), sig=f"xnT_ev{P}_g{g}")
        op("q", lambda e, t0=t0: e.dma_start(
            out=o_xnT.rearrange("(c p) t -> p c t", p=128)[:, :, t0:t0 + TT],
            in_=xnT_t),
           waits=(f"xnT_ev{P}_g3",), sig=f"xnT_st{P}")

        if ti == 0:
            op("v", lambda e: e.tensor_copy(
                xup[:, :, 0:3], xup_halo),
               waits=("halo_up15",), sig=f"halo_cp{P}")
        else:
            op("v", lambda e: e.tensor_copy(
                xup[:, :, 0:3], xup_halo),
               waits=(f"halo_sv{Pm}",), sig=f"halo_cp{P}")

        for d in range(16):
            pp = [pA, pB][d % 2]
            wgd = []
            if d >= 2:
                wgd.append(f"up_ev{P}_d{d-2}")
            if d == 0:
                wgd += [f"xnT_ev{P}_g3", f"halo_cp{P}"]
                if ti > 0:
                    wgd += [f"vmul{Pm}_d15", f"vt_ev{Pm}_c{NCHT-1}_h1",
                            f"kf_ev{Pm}", f"u_ev{Pm}"]
                else:
                    wgd.append("halo_up15")
            if d == 1 and ti > 0:
                wgd += [f"tau_ev{Pm}"]
            for c in range(8):
                op("t", lambda e, d=d, c=c, pp=pp: e.matmul(
                    pp, upT[:, c, 128 * d:128 * (d + 1)], xnT_t[:, c, :],
                    start=(c == 0), stop=(c == 7)),
                   waits=tuple(wgd) if c == 0 else (),
                   sig=f"up_mm{P}_d{d}" if c == 7 else None)
            op("s", lambda e, d=d, pp=pp: e.copy(xup[:, d, 3:TT + 3], pp),
               waits=(f"up_mm{P}_d{d}",), sig=f"up_ev{P}_d{d}")

        for d in range(16):
            wc = [f"up_ev{P}_d{d}"]
            if ti == 0 and d == 0:
                wc += ["c_convw", "c_convb"]
            if ti > 0 and d == 0:
                wc += [f"srw_st{Pm}_g3", f"srwtr{Pm}_g3_h1",
                       f"zsig{Pm}_d15"]
            op("v", lambda e, d=d: e.tensor_single_scalar(
                sh[:, d, :], xup[:, d, 0:TT], convw[:, d, 0:1], ALU.mult),
               waits=tuple(wc))
            op("v", lambda e, d=d: e.tensor_single_scalar(
                sh[:, d, :], sh[:, d, :], convb[:, d:d + 1], ALU.add))
            for j in range(1, 4):
                op("v", lambda e, d=d, j=j: e.scalar_tensor_tensor(
                    sh[:, d, :], xup[:, d, j:TT + j], convw[:, d, j:j + 1],
                    sh[:, d, :], ALU.mult, ALU.add),
                   sig=f"convacc{P}_d{d}" if j == 3 else None)
        op("s", lambda e: e.activation(sh.rearrange("p c t -> p (c t)"),
                                       sh.rearrange("p c t -> p (c t)"),
                                       AF.Silu),
           waits=(f"convacc{P}_d15",), sig=f"conv{P}")

        op("v", lambda e: e.tensor_copy(
            xup_halo, xup[:, :, TT:TT + 3]),
           waits=(f"convacc{P}_d15",), sig=f"halo_sv{P}")

        if debug and ti == 0:
            op("q", lambda e: e.dma_start(out=o_dbg[:, 0:TT],
                                          in_=xup[:, 0, 3:TT + 3]),
               waits=(f"up_ev{P}_d0",), sig="dbg3")
            op("q", lambda e: e.dma_start(out=o_dbg[:, TT:2 * TT],
                                          in_=sh[:, 0, :]),
               waits=(f"conv{P}",), sig="dbg4")
        wqk = [f"conv{P}"]
        if ti == 0:
            wqk.append(ev_qk)
        else:
            wqk.append(f"kf_ev{Pm}")
        for c in range(16):
            op("t", lambda e, c=c: e.matmul(pA, qkT[:, c, :], sh[:, c, :],
                                            start=(c == 0), stop=(c == 15)),
               waits=tuple(wqk) if c == 0 else (),
               sig=f"qk_mm{P}" if c == 15 else None)
        wgt = []
        if ti == 0:
            wgt.append(ev_gt)
        else:
            wgt.append(f"tau_ev{Pm}")
        for c in range(16):
            op("t", lambda e, c=c: e.matmul(pB, gtT[:, c, :], sh[:, c, :],
                                            start=(c == 0), stop=(c == 15)),
               waits=tuple(wgt) if c == 0 else (),
               sig=f"gt_mm{P}" if c == 15 else None)
        op("v", lambda e: e.tensor_copy(qf, pA[0:64]), waits=(f"qk_mm{P}",))
        op("v", lambda e: e.tensor_copy(kf, pA[64:128]), sig=f"kf_ev{P}")

        wpa = [f"gt_mm{P}"]
        if ti == 0:
            wpa.append("c_gb")
        op("s", lambda e: e.activation(tmp1, pB[0:64], AF.Abs, bias=gb),
           waits=tuple(wpa), sig=f"pa{P}")
        op("s", lambda e: e.activation(tmp2, tmp1, AF.Exp, scale=-1.0))
        op("s", lambda e: e.activation(tmp2, tmp2, AF.Ln, bias=1.0),
           sig=f"pl{P}")
        op("s", lambda e: e.activation(tmp1, pB[0:64], AF.Relu, bias=gb),
           sig=f"pr{P}")
        wtb = [] if ti > 0 else ["c_tb"]
        op("s", lambda e: e.activation(tau, pB[64:128], AF.Sigmoid, bias=tb),
           waits=tuple(wtb), sig=f"tau_ev{P}")
        op("v", lambda e: e.tensor_tensor(gg, tmp1, tmp2, ALU.add),
           waits=(f"pl{P}", f"pr{P}"), sig=f"gg{P}")
        op("s", lambda e: e.activation(tmp1, gg, AF.Ln), waits=(f"gg{P}",),
           sig=f"plng{P}")
        op("v", lambda e: e.tensor_tensor(tmp2, tau, tmp1, ALU.mult),
           waits=(f"plng{P}", f"tau_ev{P}"), sig=f"pt{P}")
        op("s", lambda e: e.activation(it, tmp2, AF.Exp), waits=(f"pt{P}",),
           sig=f"it{P}")
        op("v", lambda e: e.tensor_tensor(rt, gg, tau, ALU.mult),
           waits=(f"tau_ev{P}",), sig=f"rt{P}")

        for c in range(NCHT):
            sl = slice(c * CH, (c + 1) * CH)
            op("v", lambda e, sl=sl: e.tensor_tensor_scan(
                bpos[:, sl], ones64[:, 0:CH], rt[:, sl], 0.0, ALU.mult, ALU.add),
               waits=(f"rt{P}", "c_on") if c == 0 else (),
               sig=f"bpos{P}" if c == NCHT - 1 else None)
        op("v", lambda e: e.tensor_copy(
            btot.unsqueeze(2),
            bpos.rearrange("p (c t) -> p c t", t=CH)[:, :, CH - 1:CH]),
           waits=(f"bpos{P}",), sig=f"btot{P}")
        op("s", lambda e: e.activation(decay, btot, AF.Exp, scale=-1.0),
           waits=(f"btot{P}",), sig=f"decay{P}")
        wtm = ["Bacc_init"] if ti == 0 else [f"E{Pm}", f"bacc{Pm}"]
        op("v", lambda e: e.tensor_copy(tmpE[:, 0:1], Bacc[:, 0:1]),
           waits=tuple(wtm), sig=f"tE{P}_0")
        for cc in range(1, NCHT):
            op("v", lambda e, cc=cc: e.tensor_tensor(
                tmpE[:, cc:cc + 1], tmpE[:, cc - 1:cc], btot[:, cc - 1:cc],
                ALU.add), waits=(f"tE{P}_{cc-1}", f"btot{P}"),
               sig=f"tE{P}_{cc}")
        op("v", lambda e: e.tensor_single_scalar(tmpE, tmpE, 1.0, ALU.mult),
           waits=(f"tE{P}_{NCHT-1}",), sig=f"tmpE{P}")
        op("s", lambda e, ti=ti: e.activation(E_all[:, ti * NCHT:(ti + 1) * NCHT],
                                              tmpE, AF.Exp, scale=-1.0),
           waits=(f"tmpE{P}",), sig=f"E{P}")
        op("v", lambda e: e.tensor_tensor(
            Bacc[:, 0:1], tmpE[:, NCHT - 1:NCHT], btot[:, NCHT - 1:NCHT],
            ALU.add), waits=(f"tmpE{P}", f"E{P}"), sig=f"bacc{P}")
        if debug and ti == 0:
            op("q", lambda e: e.dma_start(out=o_dbgf[:, 0:NCHT], in_=btot),
               waits=(f"btot{P}",), sig="dbgf0")
            op("q", lambda e: e.dma_start(out=o_dbgf[:, 16:16 + NCHT], in_=tmpE),
               waits=(f"tmpE{P}",), sig="dbgf2")

        op("v", lambda e: e.tensor_tensor(ksq, kf, kf, ALU.mult),
           sig=f"ksq{P}")
        wkn = [f"ksq{P}"]
        if ti > 0:
            wkn.append(f"kdT_ev{Pm}_c{NCHT-1}")
        op("t", lambda e: e.matmul(pSH[0:1, :], ones64[:, 0:1], ksq,
                                   start=True, stop=True),
           waits=tuple(wkn), sig=f"nsq_mm{P}")
        op("s", lambda e: e.activation(rcp1, pSH[0:1, :], AF.Sqrt),
           waits=(f"nsq_mm{P}",), sig=f"nsqrt{P}")
        op("v", lambda e: e.tensor_scalar(rcp1, rcp1, 1.0, 1e-12,
                                          ALU.mult, ALU.max),
           waits=(f"nsqrt{P}",), sig=f"rmax{P}")
        op("v", lambda e: e.reciprocal(rcp1, rcp1), waits=(f"rmax{P}",),
           sig=f"rcpr{P}")
        op("v", lambda e: e.tensor_copy(rcp1b, rcp1), waits=(f"rcpr{P}",),
           sig=f"rcp1b{P}")
        wbc = [f"rcp1b{P}"]
        if ti == 0:
            wbc.append("c_on1")
        op("t", lambda e: e.matmul(pSH, ones1x, rcp1b, start=True, stop=True),
           waits=tuple(wbc), sig=f"bc_mm{P}")
        op("v", lambda e: e.tensor_tensor(kn, kf, pSH, ALU.mult),
           waits=(f"bc_mm{P}",))
        op("v", lambda e: e.tensor_tensor(kn, kn, it, ALU.mult),
           waits=(f"it{P}",), sig=f"kn{P}")

        op("s", lambda e: e.activation(tmp1, bpos, AF.Exp, scale=-1.0),
           waits=(f"bpos{P}", f"pt{P}"), sig=f"expb{P}")
        wqe = [f"expb{P}"]
        if ti > 0:
            wqe.append(f"Amm{Pm}_c{NCHT-1}")
        op("v", lambda e, t0=t0: e.scalar_tensor_tensor(
            qeT_all[:, t0:t0 + TT], qf, SCALE, tmp1, ALU.mult, ALU.mult),
           waits=tuple(wqe), sig=f"qe{P}")
        op("s", lambda e: e.activation(tmp2, bpos, AF.Exp), sig=f"expp{P}")
        op("v", lambda e: e.tensor_tensor(ke, kn, tmp2, ALU.mult),
           waits=(f"expp{P}", f"kn{P}"), sig=f"ke{P}")
        op("v", lambda e: e.memset(d1b, 0.0), waits=(f"ke{P}",))
        op("v", lambda e: e.tensor_copy(
            d1b.rearrange("p (c t) -> p c t", t=CH)[:, :, 0:1],
            btot.unsqueeze(2)), waits=(f"btot{P}",), sig=f"d1b{P}")
        op("v", lambda e: e.tensor_tensor_scan(btr, rep64, d1b, 0.0, ALU.mult,
                                               ALU.add),
           waits=(f"d1b{P}", "rep64_i"), sig=f"btr{P}")
        op("v", lambda e: e.tensor_tensor(tmp3, bpos, btr, ALU.subtract),
           waits=(f"btr{P}", f"nsq_mm{P}"), sig=f"sub{P}")
        op("s", lambda e: e.activation(tmp3, tmp3, AF.Exp), waits=(f"sub{P}",),
           sig=f"expd{P}")
        op("v", lambda e: e.tensor_tensor(kd, kn, tmp3, ALU.mult),
           waits=(f"expd{P}",), sig=f"kd{P}")

        for d in range(16):
            wsr = []
            if d == 0:
                wsr = [f"conv{P}", f"qk_mm{P}", f"gt_mm{P}"]
                if debug and ti == 0:
                    wsr.append("dbg4")
                if ti == 0:
                    wsr.append("c_rw")
            op("v", lambda e, d=d: e.tensor_scalar(
                sh[:, d, :], sh[:, d, :], rw[:, d:d + 1], None, ALU.mult),
               waits=tuple(wsr), sig=f"srw{P}" if d == 15 else None)
        for g in range(4):
            for hh in range(2):
                for jj in range(8):
                    j = hh * 8 + jj
                    wtr = []
                    if jj == 0:
                        wtr = [f"srw{P}"]
                        if hh == 1:
                            wtr.append(f"srw_ev{P}_g{g}_h0")
                        elif g > 0:
                            wtr.append(f"srw_ev{P}_g{g-1}_h1")
                        else:
                            wtr.append(f"xnT_ev{P}_g3")
                    op("t", lambda e, g=g, j=j, jj=jj: e.transpose(
                        pTR[:, jj * 128:(jj + 1) * 128],
                        sh[:, j, g * GRP:(g + 1) * GRP], ident),
                       waits=tuple(wtr),
                       sig=f"srwtr{P}_g{g}_h{hh}" if jj == 7 else None)
                we = [f"srwtr{P}_g{g}_h{hh}"]
                if hh == 0 and g > 0:
                    we.append(f"srw_st{P}_g{g-1}")
                op("v", lambda e, hh=hh: e.tensor_copy(
                    srw_g[:, hh * 1024:(hh + 1) * 1024], pTR),
                   waits=tuple(we), sig=f"srw_ev{P}_g{g}_h{hh}")
            op("q", lambda e, g=g, t0=t0: e.dma_start(
                out=o_srwT[t0 + g * GRP:t0 + (g + 1) * GRP, :], in_=srw_g),
               waits=(f"srw_ev{P}_g{g}_h1",), sig=f"srw_st{P}_g{g}")
        op("s", lambda e: e.copy(rcp1b[0:1, 0:1], rcp1b[0:1, 0:1]),
           waits=(f"srwtr{P}_g3_h1",), sig=f"srwtr_done{P}")

        wig = [f"conv{P}", f"tau_ev{P}"]
        if ti == 0:
            wig.append(ev_ig2)
        else:
            wig.append(f"u_ev{Pm}")
        for c in range(16):
            op("t", lambda e, c=c: e.matmul(pB[0:16, 0:TT], ig1T[:, c, :],
                                            xup[:, c, 3:TT + 3],
                                            start=(c == 0), stop=(c == 15)),
               waits=tuple(wig) if c == 0 else (),
               sig=f"u_mm{P}" if c == 15 else None)
        op("v", lambda e: e.tensor_copy(u16, pB[0:16, 0:TT]),
           waits=(f"u_mm{P}",), sig=f"u_ev{P}")
        for d in range(16):
            pp = [pA, pB][d % 2]
            wz = []
            if d >= 2:
                wz.append(f"zsig{P}_d{d-2}")
            if d == 0:
                wz += [f"u_ev{P}", f"kf_ev{P}"]
            if d == 1:
                wz += [f"u_ev{P}", f"tau_ev{P}"]
            op("t", lambda e, d=d, pp=pp: e.matmul(
                pp[:, 0:TT], ig2T[:, d, :], u16, start=True, stop=True),
               waits=tuple(wz), sig=f"z_mm{P}_d{d}")
            wzs = [f"z_mm{P}_d{d}", f"srwtr_done{P}"]
            if ti == 0 and d == 0:
                wzs.append("c_igb2")
            op("s", lambda e, d=d, pp=pp: e.activation(
                sh[:, d, :], pp[:, 0:TT], AF.Sigmoid, bias=igb2[:, d:d + 1]),
               waits=tuple(wzs), sig=f"zsig{P}_d{d}")
            wvm = [f"zsig{P}_d{d}", f"halo_sv{P}"]
            if debug and ti == 0 and d == 0:
                wvm.append("dbg3")
            op("v", lambda e, d=d: e.tensor_tensor(
                xup[:, d, 3:TT + 3], sh[:, d, :], xup[:, d, 3:TT + 3],
                ALU.mult),
               waits=tuple(wvm),
               sig=f"vmul{P}_d{d}")

        for c in range(NCHT):
            for hh in range(2):
                for dd in range(8):
                    d = hh * 8 + dd
                    wv = []
                    if dd == 0:
                        wv = [f"vmul{P}_d15"]
                        if not (c == 0 and hh == 0):
                            prevc, prevh = (c, 0) if hh == 1 else (c - 1, 1)
                            wv.append(f"vt_ev{P}_c{prevc}_h{prevh}")
                        else:
                            wv.append(f"srw_ev{P}_g3_h1")
                    op("t", lambda e, c=c, d=d, dd=dd: e.transpose(
                        pTR[0:64, dd * 128:(dd + 1) * 128],
                        xup[:, d, 3 + c * CH:3 + (c + 1) * CH], ident),
                       waits=tuple(wv),
                       sig=f"vtr{P}_c{c}_h{hh}" if dd == 7 else None)
                op("v", lambda e, c=c, hh=hh: e.tensor_copy(
                    vt[:, c, hh * 1024:(hh + 1) * 1024], pTR[0:64]),
                   waits=(f"vtr{P}_c{c}_h{hh}",), sig=f"vt_ev{P}_c{c}_h{hh}")

        for c in range(NCHT):
            gl = f"{P}_c{c}"
            glm = f"{P}_c{c-1}"
            lsl = slice(c * CH, (c + 1) * CH)
            qsl = slice(t0 + c * CH, t0 + (c + 1) * CH)
            wA = [f"ke{P}", f"qe{P}"]
            if c == 0:
                wA.append(f"kn{P}")
            else:
                wA += [f"Amask{glm}", f"kdT_ev{glm}"]
            op("t", lambda e, lsl=lsl, qsl=qsl: e.matmul(
                pSH[:, 0:64], ke[:, lsl], qeT_all[:, qsl],
                start=True, stop=True),
               waits=tuple(wA), sig=f"Amm{gl}")
            wK = [f"kd{P}"]
            if ti == 0 and c == 0:
                wK.append("c_id64")
            op("t", lambda e, lsl=lsl: e.transpose(
                pSH[:, 64:128], kd[:, lsl], id64f),
               waits=tuple(wK), sig=f"kdTmm{gl}")
            wM = [f"kdTmm{gl}"]
            if ti == 0 and c == 0:
                wM.append("c_mask")
            op("v", lambda e: e.tensor_tensor(A_sb, pSH[:, 0:64], maskLT,
                                              ALU.mult),
               waits=tuple(wM), sig=f"Amask{gl}")
            op("v", lambda e: e.tensor_copy(kdT, pSH[:, 64:128]),
               sig=f"kdT_ev{gl}")

            for h in range(2):
                hsl = slice(h * 1024, (h + 1) * 1024)
                wo = [f"vt_ev{P}_c{c}_h{h}", f"Amask{gl}"]
                if ti == 0 and c == 0 and h == 0:
                    wo.append("Sb_init")
                if c > 0:
                    wo.append(f"SbU{glm}_h{h}")
                elif ti > 0:
                    wo.append(f"SbU{Pm}_c{NCHT-1}_h{h}")
                if h == 1:
                    wo.append(f"oev{gl}_h0")
                elif c > 0:
                    wo.append(f"oev{glm}_h1")
                elif ti > 0:
                    wo.append(f"oev{Pm}_c{NCHT-1}_h1")
                for s2 in range(2):
                    op("t", lambda e, qsl=qsl, h=h, s2=s2: e.matmul(
                        pO[:, s2 * 512:(s2 + 1) * 512], qeT_all[:, qsl],
                        Sb[:, h * 1024 + s2 * 512:h * 1024 + (s2 + 1) * 512],
                        start=True, stop=False),
                       waits=tuple(wo) if s2 == 0 else ())
                for s2 in range(2):
                    op("t", lambda e, c=c, h=h, s2=s2: e.matmul(
                        pO[:, s2 * 512:(s2 + 1) * 512], A_sb,
                        vt[:, c, h * 1024 + s2 * 512:h * 1024 + (s2 + 1) * 512],
                        start=False, stop=True),
                       sig=f"omm{gl}_h{h}" if s2 == 1 else None)
                row = 64 * (c % 2)
                woe = [f"omm{gl}_h{h}"]
                if c % 2 == 0 and h == 0:
                    if c > 1:
                        woe.append(f"oL_st{P}_c{c-1}")
                    elif ti > 0:
                        woe.append(f"oL_st{Pm}_c{NCHT-1}")
                op("s", lambda e, row=row, hsl=hsl: e.copy(
                    oL_g[row:row + 64, hsl], pO),
                   waits=tuple(woe), sig=f"oev{gl}_h{h}")
                wS = [f"kdT_ev{gl}", f"vt_ev{P}_c{c}_h{h}"]
                if c > 0:
                    wS.append(f"SU{glm}_h{h}")
                elif ti > 0:
                    wS.append(f"SU{Pm}_c{NCHT-1}_h{h}")
                if h == 1:
                    wS.append(f"SU{gl}_h0")
                for s2 in range(2):
                    op("t", lambda e, c=c, h=h, s2=s2: e.matmul(
                        pS[0:64, s2 * 512:(s2 + 1) * 512], kdT,
                        vt[:, c, h * 1024 + s2 * 512:h * 1024 + (s2 + 1) * 512],
                        start=True, stop=True),
                       waits=tuple(wS) if s2 == 0 else (),
                       sig=f"Smm{gl}_h{h}" if s2 == 1 else None)
                wSU = [f"Smm{gl}_h{h}", f"decay{P}"]
                if ti == 0 and c == 0 and h == 0:
                    wSU.append("S_init")
                op("v", lambda e, c=c, hsl=hsl: e.scalar_tensor_tensor(
                    S[:, hsl], S[:, hsl], decay[:, c:c + 1], pS[0:64],
                    ALU.mult, ALU.add),
                   waits=tuple(wSU), sig=f"SU{gl}_h{h}")
                op("s", lambda e, hsl=hsl: e.copy(Sb[:, hsl], S[:, hsl]),
                   waits=(f"SU{gl}_h{h}",), sig=f"SbU{gl}_h{h}")
            if c % 2 == 1:
                op("q", lambda e, t0=t0, c=c: e.dma_start(
                    out=o_oL[t0 + (c - 1) * CH:t0 + (c + 1) * CH, :], in_=oL_g),
                   waits=(f"oev{gl}_h1", f"oev{glm}_h1"),
                   sig=f"oL_st{gl}")

    lastP = f"_t{ntile-1}"
    op("q", lambda e: e.dma_start(out=o_Send, in_=S),
       waits=(f"SU{lastP}_c{NCHT-1}_h1",), sig="S_st")
    op("q", lambda e: e.dma_start(out=o_qeT, in_=qeT_all),
       waits=(f"qe{lastP}",), sig="qeT_st")
    op("q", lambda e: e.dma_start(out=o_E, in_=E_all),
       waits=(f"E{lastP}",), sig="E_st")

    a.reset()
    a.mark()
    wstage = a.alloc([128, I_FFN], F32)
    wq_out = a.alloc([128, I_FFN], BF16)
    qs = emit_quant_scratch(a)
    last_gla = f"oL_st{lastP}_c{NCHT-1}"
    prev_st = None
    for wi, (dsrc, dout, rows, cols) in enumerate([
            (d_wg_sh, o_wgq, 352, H), (d_wu_sh, o_wuq, 352, H),
            (d_wd_sh, o_wdq, 128, I_FFN)]):
        nch = (rows + 127) // 128
        for cidx in range(nch):
            pr = min(128, rows - cidx * 128)
            nm = f"wq{wi}_{cidx}"
            wl = [last_gla, f"srw_st{lastP}_g3", f"xnT_st{lastP}",
                  f"vt_ev{lastP}_c{NCHT-1}_h1", f"vmul{lastP}_d15",
                  f"kd{lastP}", f"bacc{lastP}"] + \
                [f"xn_st{lastP}_g{g}" for g in range(4)]
            if prev_st:
                wl.append(prev_st)
            op("q", lambda e, dsrc=dsrc, cidx=cidx, pr=pr, cols=cols:
               e.dma_start(out=wstage[0:pr, 0:cols],
                           in_=dsrc[cidx * 128:cidx * 128 + pr, :]),
               waits=tuple(wl), sig=nm + "_ld")
            npiece = 2 if cols == H else 4
            pw = cols // npiece
            evs = emit_quant_pieces(
                prog, qs, [nm + f"_q{hw}" for hw in range(npiece)],
                [wstage[0:pr, hw * pw:(hw + 1) * pw] for hw in range(npiece)],
                [wq_out[0:pr, hw * pw:(hw + 1) * pw] for hw in range(npiece)],
                [pr] * npiece, [pw] * npiece, repm32, repmb,
                (nm + "_ld", "c_repm32", "c_repmb"),
                fresh=(wi == 0 and cidx == 0))
            prev_st = op("q", lambda e, dout=dout, cidx=cidx, pr=pr, cols=cols:
                         e.dma_start(out=dout[cidx * 128:cidx * 128 + pr, :],
                                     in_=wq_out[0:pr, 0:cols]),
                         waits=tuple(evs), sig=nm + "_st")

    a.reset()
    for g in reversed(pg):
        g.__exit__(None, None, None)
    ends = dict(
        S_st=prog._n("S_st"), qeT_st=prog._n("qeT_st"), E_st=prog._n("E_st"),
        wq_st=prog._n(prev_st) if not prev_st.startswith(prog.ns) else prev_st,
        oL_st=prog._n(f"oL_st{lastP}_c{NCHT-1}"),
        xn_st=prog._n(f"xn_st{lastP}_g3"), xnT_st=prog._n(f"xnT_st{lastP}"),
        srw_st=prog._n(f"srw_st{lastP}_g3"),
        SbU=prog._n(f"SbU{lastP}_c{NCHT-1}_h1"),
        E=prog._n(f"E{lastP}"), bacc=prog._n(f"bacc{lastP}"),
        wg0=prog._n("wgq_sh_nm") if False else None,
    )
    ends = {k: v for k, v in ends.items() if v}
    return ends


def build_phase_a(ntile=NTILE, debug=False):
    nc = bass.Bass("TRN2", target_bir_lowering=False)
    prog = Prog()
    a = Arena(nc, "arena", 102 * 1024)
    mk = make_mk(nc, {}, False)
    ends = emit_phase_a(nc, prog, a, mk, ntile=ntile, debug=debug)
    prog.op("p", lambda e: e.nop(), waits=tuple(ends.values()))
    with nc.allow_low_precision(reason="bf16 pipeline per problem statement"):
        prog.emit(nc)
    a.close()
    return nc


def emit_phase_b(nc, prog, a, mk, ntile=NTILE, fused=False, qeT_buf=None,
                 E_buf=None, d_Sgather=None, cc_ev=None):
    d_xnT = mk("xnT", [H, TC], BF16, "ExternalInput")
    d_oL = mk("oL", [TC, DI], BF16, "ExternalInput")
    d_srwT = mk("srwT", [TC, DI], BF16, "ExternalInput")
    if not fused:
        d_qeT = mk("qeT", [64, TC], BF16, "ExternalInput")
        d_E = mk("Ecols", [64, 64], F32, "ExternalInput")
        d_Sin = mk("Sin", [64, DI], F32, "ExternalInput")
    d_sel = mk("sel", [64, 8], F32, "ExternalInput") if fused else None
    d_gateT = mk("gateT", [H, DI], F32, "ExternalInput")
    d_downT = mk("downT", [DI, H], F32, "ExternalInput")
    d_nw = mk("nw", [128, 16], F32, "ExternalInput")
    d_ident = mk("ident", [128, 128], BF16, "ExternalInput")
    o_s = mk("s_out", [TC, H], F32, "ExternalOutput")

    op = prog.op

    gateT = a.alloc([128, 8, DI], BF16)
    downT = a.alloc([128, 16, H], BF16)
    nw = a.alloc([128, 16], F32)
    ident = a.alloc([128, 128], BF16)
    SinB = a.alloc([64, DI], BF16)
    qeT_all = qeT_buf if qeT_buf is not None else a.alloc([64, TC], BF16)
    E_all = E_buf if E_buf is not None else a.alloc([64, 64], F32)
    qeC = a.alloc([64, 64], BF16)
    SinF = a.alloc([64, DI], F32)
    selb = a.alloc([64, 8], F32)
    wstage = a.alloc([128, DI], F32)
    Gbuf = wstage[0:64, 0:DI]
    xnT_t = a.alloc([128, 8, TT], BF16)
    oL_g = a.alloc([128, DI], BF16)
    srw_g = a.alloc([128, DI], BF16)
    o2a = a.alloc([128, DI], BF16)
    o2af = a.alloc([128, DI], F32)
    sfg = a.alloc([128, DI], F32)
    o3 = a.alloc([128, DI], F32)
    y = a.alloc([128, DI], BF16)
    yT = a.alloc([128, 16, 128], BF16)
    s_g = a.alloc([128, H], F32)
    ssq = a.alloc([128, 1], F32)

    pg = []
    def palloc(name, shape, dtype):
        g = nc.psum_tensor(prog.ns + name, shape, dtype)
        t = g.__enter__(); pg.append(g)
        return t[:]
    pA = palloc("pA", [128, 512], F32)
    pB = palloc("pB", [128, 512], F32)
    pC = palloc("pC", [64, 1024], F32)
    pS = palloc("pS", [128, 1024], F32)
    pTR = palloc("pTR", [128, 1024], BF16)

    op("q", lambda e: e.dma_start(out=ident, in_=d_ident), sig="c_id")
    if not fused:
        op("q", lambda e: e.dma_start(out=E_all, in_=d_E), sig="c_E")
        op("q", lambda e: e.dma_start(out=qeT_all, in_=d_qeT), sig="c_qeT")
        op("q", lambda e: e.dma_start(out=wstage[0:64, 0:DI], in_=d_Sin),
           sig="sin_ld")
        op("s", lambda e: e.copy(SinB, wstage[0:64, 0:DI]), waits=("sin_ld",),
           sig="sinb")
    else:
        op("q", lambda e: e.dma_start(out=selb, in_=d_sel), sig="sel_ld")
        op("v", lambda e: e.memset(SinF, 0.0), sig="sin_z")
        for j in range(8):
            op("q", lambda e, j=j: e.dma_start(
                out=Gbuf, in_=d_Sgather[64 * j:64 * (j + 1), :]),
               waits=(cc_ev,) if j == 0 else (f"sacc_{j-1}",),
               sig=f"sg_ld_{j}")
            op("v", lambda e, j=j: e.scalar_tensor_tensor(
                SinF, Gbuf, selb[:, j:j + 1], SinF, ALU.mult, ALU.add),
               waits=(f"sg_ld_{j}", "sel_ld", "sin_z"), sig=f"sacc_{j}")
        op("s", lambda e: e.copy(SinB, SinF), waits=("sacc_7",), sig="sinb")
    op("q", lambda e: e.dma_start(out=nw, in_=d_nw), sig="c_nw")
    ev_gw = emit_cast_weight(prog, "wgate", d_gateT, gateT, H, DI, wstage,
                             prev="sinb")
    ev_dw = emit_cast_weight(prog, "wdown", d_downT, downT, DI, H, wstage,
                             prev=ev_gw)
    for d in range(16):
        op("v", lambda e, d=d: e.tensor_scalar(downT[:, d, :], downT[:, d, :],
                                               nw[:, d:d + 1], None, ALU.mult),
           waits=(ev_dw, "c_nw") if d == 0 else (),
           sig="nwfold" if d == 15 else None)

    for ti in range(ntile):
        t0 = ti * TT
        P = f"_t{ti}"
        Pm = f"_t{ti-1}"
        wl = [f"fgmm{Pm}_g3_n3"] if ti > 0 else []
        op("q", lambda e, t0=t0: e.dma_start(
            out=xnT_t,
            in_=d_xnT.rearrange("(c p) t -> p c t", p=128)[:, :, t0:t0 + TT]),
           waits=tuple(wl), sig=f"xnT_ld{P}")
        for g in range(4):
            gl = f"{P}_g{g}"
            glm = f"{P}_g{g-1}" if g > 0 else f"{Pm}_g3"
            gt0 = t0 + g * GRP
            wld = [f"o2a{glm}"] if (g > 0 or ti > 0) else []
            op("q", lambda e, gt0=gt0: e.dma_start(
                out=oL_g, in_=d_oL[gt0:gt0 + GRP, :]),
               waits=tuple(wld), sig=f"oL_ld{gl}")
            op("q", lambda e, gt0=gt0: e.dma_start(
                out=srw_g, in_=d_srwT[gt0:gt0 + GRP, :]),
               sig=f"srw_ld{gl}")
            wo2 = [f"oL_ld{gl}", f"srw_ld{gl}"]
            if g > 0 or ti > 0:
                wo2.append(f"corr{glm}_c1_h1")
            op("v", lambda e: e.tensor_tensor(o2a, oL_g, srw_g, ALU.add),
               waits=tuple(wo2), sig=f"o2ab{gl}")
            op("s", lambda e: e.copy(o2af, o2a), waits=(f"o2ab{gl}",),
               sig=f"o2a{gl}")
            for ns in range(4):
                pp = [pA, pB][ns % 2]
                wf = [f"xnT_ld{P}"]
                if ns >= 2:
                    wf.append(f"sfg{gl}_n{ns-2}")
                elif g > 0 or ti > 0:
                    wf.append(f"sfg{glm}_n{ns + 2}")
                if ti == 0 and g == 0 and ns == 0:
                    wf.append(ev_gw)
                for c in range(8):
                    op("t", lambda e, g=g, ns=ns, c=c, pp=pp: e.matmul(
                        pp, xnT_t[:, c, g * GRP:(g + 1) * GRP],
                        gateT[:, c, ns * 512:(ns + 1) * 512],
                        start=(c == 0), stop=(c == 7)),
                       waits=tuple(wf) if c == 0 else (),
                       sig=f"fgmm{gl}_n{ns}" if c == 7 else None)
                wsf = [f"fgmm{gl}_n{ns}"]
                if g > 0 or ti > 0:
                    wsf.append(f"o3{glm}")
                    wsf.append(f"sq{glm}")
                op("s", lambda e, ns=ns, pp=pp: e.activation(
                    sfg[:, ns * 512:(ns + 1) * 512], pp, AF.Silu),
                   waits=tuple(wsf), sig=f"sfg{gl}_n{ns}")
            for c2 in range(2):
                Cg = (t0 + g * GRP) // CH + c2
                wq = [] if fused else ["c_qeT", "c_E"]
                if not (ti == 0 and g == 0 and c2 == 0):
                    wq.append(f"cmm{gl}_c0_h1" if c2 == 1 else
                              f"cmm{glm}_c1_h1")
                op("v", lambda e, Cg=Cg: e.tensor_scalar(
                    qeC, qeT_all[:, Cg * CH:(Cg + 1) * CH],
                    E_all[:, Cg:Cg + 1], None, ALU.mult),
                   waits=tuple(wq), sig=f"qeC{gl}_c{c2}")
                for h in range(2):
                    wc = [f"qeC{gl}_c{c2}", "sinb"]
                    if h == 1:
                        wc.append(f"corrc{gl}_c{c2}_h0")
                    elif not (ti == 0 and g == 0 and c2 == 0):
                        wc.append(f"corrc{gl}_c0_h1" if c2 == 1 else
                                  f"corrc{glm}_c1_h1")
                    for s2 in range(2):
                        op("t", lambda e, h=h, s2=s2: e.matmul(
                            pC[:, s2 * 512:(s2 + 1) * 512], qeC,
                            SinB[:, h * 1024 + s2 * 512:
                                 h * 1024 + (s2 + 1) * 512],
                            start=True, stop=True),
                           waits=tuple(wc) if s2 == 0 else (),
                           sig=f"cmm{gl}_c{c2}_h{h}" if s2 == 1 else None)
                    op("s", lambda e, c2=c2, h=h: e.copy(
                        o3[64 * c2:64 * c2 + 64][:, h * 1024:(h + 1) * 1024],
                        pC),
                       waits=(f"cmm{gl}_c{c2}_h{h}",),
                       sig=f"corrc{gl}_c{c2}_h{h}")
                    op("v", lambda e, c2=c2, h=h: e.tensor_tensor(
                        o3[64 * c2:64 * c2 + 64][:, h * 1024:(h + 1) * 1024],
                        o3[64 * c2:64 * c2 + 64][:, h * 1024:(h + 1) * 1024],
                        o2af[64 * c2:64 * c2 + 64][:, h * 1024:(h + 1) * 1024],
                        ALU.add),
                       waits=(f"corrc{gl}_c{c2}_h{h}", f"o2a{gl}"),
                       sig=f"corr{gl}_c{c2}_h{h}")
            op("v", lambda e: e.memset(ssq, 0.0))
            op("v", lambda e: e.tensor_tensor(o3, o3, sfg, ALU.mult),
               waits=(f"corr{gl}_c1_h1", f"corr{gl}_c0_h1", f"sfg{gl}_n3"),
               sig=f"o3{gl}")
            op("s", lambda e: e.activation(sfg, o3, AF.Square, accum_out=ssq),
               waits=(f"o3{gl}",), sig=f"sq{gl}")
            op("v", lambda e: e.tensor_scalar(ssq, ssq, 1.0 / DI, NORM_EPS,
                                              ALU.mult, ALU.add),
               waits=(f"sq{gl}",), sig=f"sqs{gl}")
            op("v", lambda e: e.reciprocal(ssq, ssq), waits=(f"sqs{gl}",),
               sig=f"rr{gl}")
            op("s", lambda e: e.activation(ssq, ssq, AF.Sqrt),
               waits=(f"rr{gl}",), sig=f"rs{gl}")
            wy = [f"rs{gl}"]
            if g > 0 or ti > 0:
                wy.append(f"yT_ev{glm}_h1")
            op("v", lambda e: e.tensor_scalar(y, o3, ssq[:, 0:1], None,
                                              ALU.mult),
               waits=tuple(wy), sig=f"y{gl}")
            for hh in range(2):
                for jj in range(8):
                    j = hh * 8 + jj
                    wtr = []
                    if jj == 0:
                        wtr = [f"y{gl}"]
                        if hh == 1:
                            wtr.append(f"yT_ev{gl}_h0")
                        elif g > 0 or ti > 0:
                            wtr.append(f"yT_ev{glm}_h1")
                        if ti == 0 and g == 0 and j == 0:
                            wtr.append("c_id")
                    op("t", lambda e, j=j, jj=jj: e.transpose(
                        pTR[:, jj * 128:(jj + 1) * 128],
                        y[:, j * 128:(j + 1) * 128], ident),
                       waits=tuple(wtr),
                       sig=f"yTtr{gl}_h{hh}" if jj == 7 else None)
                whh = [f"yTtr{gl}_h{hh}"]
                if hh == 0 and (g > 0 or ti > 0):
                    whh.append(f"dmm{glm}_n1")
                op("v", lambda e, hh=hh: e.tensor_copy(
                    yT[:, hh * 8:(hh + 1) * 8, :],
                    pTR.rearrange("p (c t) -> p c t", t=128)),
                   waits=tuple(whh), sig=f"yT_ev{gl}_h{hh}")
            for ns in range(2):
                wd = [f"yT_ev{gl}_h1", "nwfold"]
                if g > 0 or ti > 0:
                    wd.append(f"s_ev{glm}")
                for c in range(16):
                    op("t", lambda e, ns=ns, c=c: e.matmul(
                        pS[:, ns * 512:(ns + 1) * 512], yT[:, c, :],
                        downT[:, c, ns * 512:(ns + 1) * 512],
                        start=(c == 0), stop=(c == 15)),
                       waits=tuple(wd) if (c == 0 and ns == 0) else (),
                       sig=f"dmm{gl}_n{ns}" if c == 15 else None)
            wse = [f"dmm{gl}_n0", f"dmm{gl}_n1"]
            if g > 0 or ti > 0:
                wse.append(f"s_st{glm}")
            op("s", lambda e: e.copy(s_g, pS), waits=tuple(wse),
               sig=f"s_ev{gl}")
            op("q", lambda e, gt0=gt0: e.dma_start(
                out=o_s[gt0:gt0 + GRP, :], in_=s_g),
               waits=(f"s_ev{gl}",), sig=f"s_st{gl}")

    for g in reversed(pg):
        g.__exit__(None, None, None)
    lb = f"_t{ntile-1}_g3"
    return dict(s_st=prog._n(f"s_st{lb}"), s_ev=prog._n(f"s_ev{lb}"),
                yT=prog._n(f"yT_ev{lb}_h1"), sq=prog._n(f"sq{lb}"),
                o3=prog._n(f"o3{lb}"))


def build_phase_b(ntile=NTILE):
    nc = bass.Bass("TRN2", target_bir_lowering=False)
    prog = Prog()
    a = Arena(nc, "arena", 100 * 1024)
    mk = make_mk(nc, {}, False)
    ends = emit_phase_b(nc, prog, a, mk, ntile=ntile)
    prog.op("p", lambda e: e.nop(), waits=tuple(ends.values()))
    with nc.allow_low_precision(reason="bf16 pipeline per problem statement"):
        prog.emit(nc)
    a.close()
    return nc


def emit_phase_c(nc, prog, a, mk, ngrp=TC // GRP, fused=False, wq_evs=()):
    d_xn = mk("xn_tm", [TC, H], F32, "ExternalInput")
    d_s = mk("s_out" if fused else "s_in", [TC, H], F32, "ExternalInput")
    d_x = mk("x", [TC + 3, H], F32, "ExternalInput")
    if fused:
        d_wgq = mk("wgq_g", [I_FFN, H], BF16, "ExternalInput")
        d_wuq = mk("wuq_g", [I_FFN, H], BF16, "ExternalInput")
        d_wdq = mk("wdq_g", [H, I_FFN], BF16, "ExternalInput")
    else:
        d_wgq = mk("wgq", [I_FFN, H], BF16, "ExternalInput")
        d_wuq = mk("wuq", [I_FFN, H], BF16, "ExternalInput")
        d_wdq = mk("wdq", [H, I_FFN], BF16, "ExternalInput")
    d_repm32 = mk("repm32c", [128, 704], F32, "ExternalInput")
    d_repmb = mk("repmb", [128, QW_MAX], BF16, "ExternalInput")
    d_ident = mk("ident", [128, 128], BF16, "ExternalInput")
    o_out = mk("out", [TC, H], F32, "ExternalOutput")

    op = prog.op

    wgqT = a.alloc([128, 8, I_FFN], BF16)
    wuqT = a.alloc([128, 8, I_FFN], BF16)
    wdqT = a.alloc([128, 22, H], BF16)
    repm32 = a.alloc([128, 704], F32)
    repmb = a.alloc([128, QW_MAX], BF16)
    ident = a.alloc([128, 128], BF16)
    xn_g = a.alloc([128, H], F32)
    x_g = a.alloc([128, H], F32)
    s_gc = a.alloc([128, H], F32)
    xq = a.alloc([128, H], BF16)
    hbuf = a.alloc([128, I_FFN], F32)
    hq = a.alloc([128, I_FFN], BF16)
    hqT = a.alloc([128, 22, 128], BF16)
    xqT = hqT[:, 0:8, :]
    g_g = a.alloc([128, H], F32)
    qs = emit_quant_scratch(a)
    sg1 = g_g[:, 0:512]
    wld = hbuf.bitcast(BF16)[:, 0:I_FFN]
    t1 = xn_g

    pg = []
    def palloc(name, shape, dtype):
        g = nc.psum_tensor(prog.ns + name, shape, dtype)
        t = g.__enter__(); pg.append(g)
        return t[:]
    pA = palloc("pA", [128, 512], F32)
    pB = palloc("pB", [128, 512], F32)
    pS = palloc("pS", [128, 1024], F32)
    pTR = palloc("pTR", [128, 1024], BF16)

    op("q", lambda e: e.dma_start(out=repm32, in_=d_repm32), sig="c_repm")
    op("q", lambda e: e.dma_start(out=repmb, in_=d_repmb), sig="c_repmb")
    op("q", lambda e: e.dma_start(out=ident, in_=d_ident), sig="c_id")

    def wtrans(nm, dram, dst, rows, cols, prev_ev):
        nrc = rows // 128
        ncc = cols // 128
        last_evac = prev_ev
        for rc in range(nrc):
            wlw = [f"{nm}_r{rc-1}_done"] if rc > 0 else (
                ([prev_ev] if prev_ev else []) + list(wq_evs))
            op("q", lambda e, rc=rc, cols=cols: e.dma_start(
                out=wld[:, 0:cols], in_=dram[rc * 128:(rc + 1) * 128, :]),
               waits=tuple(wlw), sig=f"{nm}_r{rc}_ld")
            evs = []
            for cc in range(ncc):
                jj = cc % 8
                wt = []
                if jj == 0:
                    wt = [f"{nm}_r{rc}_ld", "c_id"]
                    if cc >= 8:
                        wt.append(f"{nm}_r{rc}_e{cc - 8}")
                    elif last_evac:
                        wt.append(last_evac)
                op("t", lambda e, cc=cc, jj=jj: e.transpose(
                    pTR[:, jj * 128:(jj + 1) * 128],
                    wld[:, cc * 128:(cc + 1) * 128], ident),
                   waits=tuple(wt),
                   sig=f"{nm}_r{rc}_t{cc}" if (jj == 7 or cc == ncc - 1)
                   else None)
                if jj == 7 or cc == ncc - 1:
                    c0 = (cc // 8) * 8
                    nn = cc - c0 + 1
                    ev = op("v", lambda e, rc=rc, c0=c0, nn=nn:
                            e.tensor_copy(
                                dst[:, c0:c0 + nn, rc * 128:(rc + 1) * 128],
                                pTR[:, 0:nn * 128].rearrange(
                                    "p (c t) -> p c t", t=128)),
                            waits=(f"{nm}_r{rc}_t{cc}",),
                            sig=f"{nm}_r{rc}_e{c0}")
                    evs.append(ev)
                    last_evac = ev
            op("v", lambda e: e.tensor_copy(xn_g[0:1, 0:1], wld[0:1, 0:1]),
               waits=tuple(evs), sig=f"{nm}_r{rc}_done")
        return last_evac, f"{nm}_r{nrc-1}_done"

    ev_wg_e, ev_wg = wtrans("twg", d_wgq, wgqT, I_FFN, H,
                            wq_evs[0] if wq_evs else None)
    ev_wu_e, ev_wu = wtrans("twu", d_wuq, wuqT, I_FFN, H, ev_wg)
    ev_wd_e, ev_wd = wtrans("twd", d_wdq, wdqT, H, I_FFN, ev_wu)

    ISL = [(0, 512), (512, 1024), (1024, 1536), (1536, 2048), (2048, 2560),
           (2560, 2816)]

    for gi in range(ngrp):
        gl = f"_g{gi}"
        glm = f"_g{gi-1}"
        gt0 = gi * GRP
        wxn = [f"xq{glm}_h1_f"] if gi > 0 else [ev_wd]
        op("q", lambda e, gt0=gt0: e.dma_start(out=xn_g,
                                               in_=d_xn[gt0:gt0 + GRP, :]),
           waits=tuple(wxn), sig=f"xn_ld{gl}")
        wx = [f"out_st{glm}"] if gi > 0 else []
        op("q", lambda e, gt0=gt0: e.dma_start(
            out=x_g, in_=d_x[3 + gt0:3 + gt0 + GRP, :]),
           waits=tuple(wx), sig=f"x_ld{gl}")
        ws = [f"t1{glm}"] if gi > 0 else []
        op("q", lambda e, gt0=gt0: e.dma_start(out=s_gc,
                                               in_=d_s[gt0:gt0 + GRP, :]),
           waits=tuple(ws), sig=f"s_ld{gl}")
        ev_xq = emit_quant_pieces(
            prog, qs, [f"xq{gl}_h{hw}_f" for hw in range(2)],
            [xn_g[:, hw * 512:(hw + 1) * 512] for hw in range(2)],
            [xq[:, hw * 512:(hw + 1) * 512] for hw in range(2)],
            [128, 128], [512, 512], repm32, repmb,
            (f"xn_ld{gl}", "c_repm", "c_repmb"), fresh=(gi == 0))[-1]
        for j in range(8):
            wt = []
            if j == 0:
                wt = [ev_xq]
                wt.append(f"hqT_ev{glm}_b2" if gi > 0 else ev_wd_e)
            op("t", lambda e, j=j: e.transpose(
                pTR[:, j * 128:(j + 1) * 128], xq[:, j * 128:(j + 1) * 128],
                ident),
               waits=tuple(wt), sig=f"xqtr{gl}" if j == 7 else None)
        wxv = [f"xqtr{gl}"]
        if gi > 0:
            wxv += [f"gumm{glm}_s5", f"ffnmm{glm}_n1"]
        op("v", lambda e: e.tensor_copy(
            xqT, pTR.rearrange("p (c t) -> p c t", t=128)),
           waits=tuple(wxv), sig=f"xqT_ev{gl}")
        for si, (i0, i1) in enumerate(ISL):
            wm = [f"xqT_ev{gl}"]
            if si == 0:
                if gi > 0:
                    wm.append(f"h_ev{glm}_s5")
            else:
                wm.append(f"h_ev{gl}_s{si-1}")
            for c in range(8):
                op("t", lambda e, c=c, i0=i0, i1=i1: e.matmul(
                    pA[:, 0:i1 - i0], xqT[:, c, :], wgqT[:, c, i0:i1],
                    start=(c == 0), stop=(c == 7)),
                   waits=tuple(wm) if c == 0 else ())
            for c in range(8):
                op("t", lambda e, c=c, i0=i0, i1=i1: e.matmul(
                    pB[:, 0:i1 - i0], xqT[:, c, :], wuqT[:, c, i0:i1],
                    start=(c == 0), stop=(c == 7)),
                   sig=f"gumm{gl}_s{si}" if c == 7 else None)
            wsg = [f"gumm{gl}_s{si}"]
            op("s", lambda e, i0=i0, i1=i1: e.activation(
                sg1[:, 0:i1 - i0], pA[:, 0:i1 - i0], AF.Silu),
               waits=tuple(wsg), sig=f"sg{gl}_s{si}")
            op("v", lambda e, i0=i0, i1=i1: e.tensor_tensor(
                hbuf[:, i0:i1], sg1[:, 0:i1 - i0], pB[:, 0:i1 - i0], ALU.mult),
               waits=(f"sg{gl}_s{si}",), sig=f"h_ev{gl}_s{si}")
        PW = I_FFN // 4
        ev_hq = emit_quant_pieces(
            prog, qs, [f"hq{gl}_f{hw}" for hw in range(4)],
            [hbuf[:, hw * PW:(hw + 1) * PW] for hw in range(4)],
            [hq[:, hw * PW:(hw + 1) * PW] for hw in range(4)],
            [128] * 4, [PW] * 4, repm32, repmb,
            [(f"h_ev{gl}_s{si}",) for si in (1, 2, 4, 5)])
        for j in range(22):
            jj = j % 8
            wt = []
            if jj == 0:
                wt = list(ev_hq)
                if j > 0:
                    wt.append(f"hqT_ev{gl}_b{j // 8 - 1}")
                else:
                    wt.append(f"xqT_ev{gl}")
            op("t", lambda e, j=j, jj=jj: e.transpose(
                pTR[:, jj * 128:(jj + 1) * 128], hq[:, j * 128:(j + 1) * 128],
                ident),
               waits=tuple(wt),
               sig=f"hqtr{gl}_b{j // 8}" if (jj == 7 or j == 21) else None)
            if jj == 7 or j == 21:
                c0 = (j // 8) * 8
                nn = j - c0 + 1
                whv = [f"hqtr{gl}_b{j // 8}"]
                if j // 8 == 0 and gi > 0:
                    whv.append(f"ffnmm{glm}_n1")
                op("v", lambda e, c0=c0, nn=nn: e.tensor_copy(
                    hqT[:, c0:c0 + nn, :],
                    pTR[:, 0:nn * 128].rearrange("p (c t) -> p c t", t=128)),
                   waits=tuple(whv), sig=f"hqT_ev{gl}_b{j // 8}")
        for ns in range(2):
            wd = [f"hqT_ev{gl}_b2"]
            if gi > 0 and ns == 0:
                wd.append(f"g_ev{glm}")
            for c in range(22):
                op("t", lambda e, ns=ns, c=c: e.matmul(
                    pS[:, ns * 512:(ns + 1) * 512], hqT[:, c, :],
                    wdqT[:, c, ns * 512:(ns + 1) * 512],
                    start=(c == 0), stop=(c == 21)),
                   waits=tuple(wd) if c == 0 else (),
                   sig=f"ffnmm{gl}_n{ns}" if c == 21 else None)
        wgv = [f"ffnmm{gl}_n0", f"ffnmm{gl}_n1"]
        if gi > 0:
            wgv.append(f"t1{glm}")
        op("s", lambda e: e.activation(g_g, pS, AF.Silu), waits=tuple(wgv),
           sig=f"g_ev{gl}")
        op("v", lambda e: e.tensor_tensor(t1, s_gc, g_g, ALU.mult),
           waits=(f"g_ev{gl}", f"s_ld{gl}"), sig=f"t1{gl}")
        op("v", lambda e: e.tensor_tensor(x_g, t1, x_g, ALU.add),
           waits=(f"x_ld{gl}",), sig=f"out{gl}")
        op("q", lambda e, gt0=gt0: e.dma_start(out=o_out[gt0:gt0 + GRP, :],
                                               in_=x_g),
           waits=(f"out{gl}",), sig=f"out_st{gl}")

    for g in reversed(pg):
        g.__exit__(None, None, None)
    return dict(out_st=prog._n(f"out_st_g{ngrp-1}"))


def build_phase_c(ngrp=TC // GRP):
    nc = bass.Bass("TRN2", target_bir_lowering=False)
    prog = Prog()
    a = Arena(nc, "arena", 102 * 1024)
    mk = make_mk(nc, {}, False)
    ends = emit_phase_c(nc, prog, a, mk, ngrp=ngrp)
    prog.op("p", lambda e: e.nop(), waits=tuple(ends.values()))
    with nc.allow_low_precision(reason="bf16 pipeline per problem statement"):
        prog.emit(nc)
    a.close()
    return nc


def build_fused(ntile=NTILE, ngrp=TC // GRP):
    nc = bass.Bass("TRN2", target_bir_lowering=False)
    prog = Prog()
    a = Arena(nc, "arena", 102 * 1024)
    cache = {}
    mk = make_mk(nc, cache, True)

    qeT_all = a.alloc([64, TC], BF16)
    E_all = a.alloc([64, 64], F32)
    a.mark()

    prog.ns = "A."
    ends_a = emit_phase_a(nc, prog, a, mk, ntile=ntile, fused=True,
                          qeT_buf=qeT_all, E_buf=E_all)
    prog.ns = ""

    d_Send = cache["Send"]
    d_Sg = mk("Sgather", [64 * NCORE, DI], F32, "Internal")
    d_wgg = mk("wgq_g", [I_FFN, H], BF16, "Internal")
    d_wug = mk("wuq_g", [I_FFN, H], BF16, "Internal")
    d_wdg = mk("wdq_g", [H, I_FFN], BF16, "Internal")
    rg = [list(range(NCORE))]
    prog.op("pc", lambda e: e.collective_compute(
        "AllGather", ALU.bypass, replica_groups=rg, ins=[d_Send.opt()],
        outs=[d_Sg.opt()]), waits=(ends_a["S_st"],), sig="@cc_S")
    prog.op("pc", lambda e: e.collective_compute(
        "AllGather", ALU.bypass, replica_groups=rg,
        ins=[cache["wgq_sh"].opt()],
        outs=[d_wgg.opt()]), waits=(ends_a["wq_st"],), sig="@cc_wg")
    prog.op("pc", lambda e: e.collective_compute(
        "AllGather", ALU.bypass, replica_groups=rg,
        ins=[cache["wuq_sh"].opt()],
        outs=[d_wug.opt()]), sig="@cc_wu")
    prog.op("pc", lambda e: e.collective_compute(
        "AllGather", ALU.bypass, replica_groups=rg,
        ins=[cache["wdq_sh"].opt()],
        outs=[d_wdg.opt()]), sig="@cc_wd")

    prog.barrier("ab", list(ends_a.values()))
    a.reset()
    a.mark()

    prog.ns = "B."
    ends_b = emit_phase_b(nc, prog, a, mk, ntile=ntile, fused=True,
                          qeT_buf=qeT_all, E_buf=E_all, d_Sgather=d_Sg,
                          cc_ev="@cc_S")
    prog.ns = ""

    prog.barrier("bc", list(ends_b.values()))
    a.reset()
    a.clear()

    prog.ns = "C."
    ends_c = emit_phase_c(nc, prog, a, mk, ngrp=ngrp, fused=True,
                          wq_evs=("@cc_wg", "@cc_wu", "@cc_wd"))
    prog.ns = ""

    prog.op("p", lambda e: e.nop(), waits=tuple(ends_c.values()))
    with nc.allow_low_precision(reason="bf16 pipeline per problem statement"):
        prog.emit(nc)
    a.close()
    return nc


def _host_prep(inputs):
    f = lambda k: np.ascontiguousarray(inputs[k], dtype=np.float32)
    up_w = f("L0_attn_up_w"); gate_w = f("L0_attn_gate_w")
    conv_w = f("L0_attn_conv_w"); conv_b = f("L0_attn_conv_b")
    q_w = f("L0_attn_q_w"); k_w = f("L0_attn_k_w")
    g_w = f("L0_attn_g_w"); g_b = f("L0_attn_g_b")
    tau_w = f("L0_attn_tau_w"); tau_b = f("L0_attn_tau_b")
    ig_w1 = f("L0_attn_ig_w1"); ig_w2 = f("L0_attn_ig_w2")
    ig_b2 = f("L0_attn_ig_b2"); rw_ = f("L0_attn_residual_weight")
    nw_ = f("L0_attn_norm_weight"); down_w = f("L0_attn_down_w")

    c = {}
    c["upT"] = np.ascontiguousarray(up_w.T)
    c["qkT"] = np.ascontiguousarray(np.concatenate([q_w.T, k_w.T], axis=1))
    c["gtT"] = np.ascontiguousarray(np.concatenate([g_w.T, tau_w.T], axis=1))
    c["ig1T"] = np.ascontiguousarray(ig_w1.T)
    c["ig2T"] = np.ascontiguousarray(ig_w2.T)
    c["convw"] = np.ascontiguousarray(
        conv_w.reshape(16, 128, 4).transpose(1, 0, 2).reshape(128, 64))
    c["convb"] = np.ascontiguousarray(conv_b.reshape(16, 128).T)
    c["igb2"] = np.ascontiguousarray(ig_b2.reshape(16, 128).T)
    c["rw"] = np.ascontiguousarray(rw_.reshape(16, 128).T)
    c["nw"] = np.ascontiguousarray(nw_.reshape(16, 128).T)
    c["gb"] = np.ascontiguousarray(g_b.reshape(64, 1))
    c["tb"] = np.ascontiguousarray(tau_b.reshape(64, 1))
    c["ident"] = np.eye(128, dtype=BF)
    c["id64f"] = np.eye(64, dtype=np.float32)
    jj, ii = np.meshgrid(np.arange(64), np.arange(64), indexing="ij")
    c["maskLT"] = (ii >= jj).astype(np.float32)
    c["ones64"] = np.ones((64, 64), np.float32)
    c["ones1x"] = np.ones((1, 64), dtype=BF)
    rm = np.ones((128, I_FFN), dtype=BF)
    rm[:, 0::16] = 0
    c["repm"] = rm
    c["repmb"] = np.ascontiguousarray(rm[:, 0:QW_MAX])
    rm32 = np.ones((128, 1024), dtype=np.float32)
    rm32[:, 0::16] = 0
    c["repm32"] = rm32
    c["gateT"] = np.ascontiguousarray(gate_w.T)
    c["downT"] = np.ascontiguousarray(down_w.T)
    c["wg"] = f("L0_ffn_w_gate"); c["wu"] = f("L0_ffn_w_up")
    c["wd"] = f("L0_ffn_w_down")
    c["x"] = f("L0_attn_hidden_states")
    return c


def _shard_x(x):
    out = []
    for cid in range(NCORE):
        b = cid // 2
        t0 = (cid % 2) * TC
        if t0 == 0:
            sl = np.concatenate([np.zeros((3, H), np.float32), x[b, 0:TC]], 0)
        else:
            sl = x[b, t0 - 3:t0 + TC]
        out.append(np.ascontiguousarray(sl, dtype=np.float32))
    return out


def kernel(**inputs):
    c = _host_prep(inputs)
    xs = _shard_x(c["x"])

    nc = build_fused()
    in_maps = []
    for cid in range(NCORE):
        r0 = cid * 352
        rd0 = cid * 128
        sel = np.zeros((64, 8), np.float32)
        if cid % 2 == 1:
            sel[:, cid - 1] = 1.0
        in_maps.append(dict(
            x=xs[cid], upT=c["upT"], qkT=c["qkT"], gtT=c["gtT"],
            ig1T=c["ig1T"], ig2T=c["ig2T"], convw=c["convw"], convb=c["convb"],
            igb2=c["igb2"], rw=c["rw"], gb=c["gb"], tb=c["tb"],
            ident=c["ident"], id64f=c["id64f"], maskLT=c["maskLT"],
            ones64=c["ones64"], ones1x=c["ones1x"], repm32=c["repm32"],
            wg_sh=np.ascontiguousarray(c["wg"][r0:r0 + 352]),
            wu_sh=np.ascontiguousarray(c["wu"][r0:r0 + 352]),
            wd_sh=np.ascontiguousarray(c["wd"][rd0:rd0 + 128]),
            sel=sel, gateT=c["gateT"], downT=c["downT"], nw=c["nw"],
            repm32c=c["repm32"][:, 0:704].copy(), repmb=c["repmb"],
        ))
    rc = run_bass_kernel_spmd(nc, in_maps, core_ids=list(range(NCORE))).results

    out = np.empty((B, T, H), np.float32)
    for cid in range(NCORE):
        b = cid // 2
        t0 = (cid % 2) * TC
        out[b, t0:t0 + TC] = rc[cid]["out"]
    return out


def kernel_unfused(**inputs):
    c = _host_prep(inputs)
    xs = _shard_x(c["x"])

    nc_a = build_phase_a()
    in_maps = []
    for cid in range(NCORE):
        r0 = cid * 352
        rd0 = cid * 128
        in_maps.append(dict(
            x=xs[cid], upT=c["upT"], qkT=c["qkT"], gtT=c["gtT"],
            ig1T=c["ig1T"], ig2T=c["ig2T"], convw=c["convw"], convb=c["convb"],
            igb2=c["igb2"], rw=c["rw"], gb=c["gb"], tb=c["tb"],
            ident=c["ident"], id64f=c["id64f"], maskLT=c["maskLT"],
            ones64=c["ones64"], ones1x=c["ones1x"],
            repm32=c["repm32"], repmb=c["repmb"],
            wg_sh=np.ascontiguousarray(c["wg"][r0:r0 + 352]),
            wu_sh=np.ascontiguousarray(c["wu"][r0:r0 + 352]),
            wd_sh=np.ascontiguousarray(c["wd"][rd0:rd0 + 128]),
        ))
    ra = run_bass_kernel_spmd(nc_a, in_maps, core_ids=list(range(NCORE))).results

    wgq = np.ascontiguousarray(
        np.concatenate([ra[i]["wgq_sh"] for i in range(NCORE)], 0))
    wuq = np.ascontiguousarray(
        np.concatenate([ra[i]["wuq_sh"] for i in range(NCORE)], 0))
    wdq = np.ascontiguousarray(
        np.concatenate([ra[i]["wdq_sh"] for i in range(NCORE)], 0))

    nc_b = build_phase_b()
    in_maps = []
    for cid in range(NCORE):
        sin = (np.zeros((64, DI), np.float32) if cid % 2 == 0
               else np.ascontiguousarray(ra[cid - 1]["Send"], np.float32))
        in_maps.append(dict(
            xnT=ra[cid]["xnT"], oL=ra[cid]["oL"], srwT=ra[cid]["srwT"],
            qeT=ra[cid]["qeT"], Ecols=ra[cid]["Ecols"], Sin=sin,
            gateT=c["gateT"], downT=c["downT"], nw=c["nw"], ident=c["ident"]))
    rb = run_bass_kernel_spmd(nc_b, in_maps, core_ids=list(range(NCORE))).results

    nc_c = build_phase_c()
    in_maps = []
    for cid in range(NCORE):
        in_maps.append(dict(
            xn_tm=ra[cid]["xn_tm"], s_in=rb[cid]["s_out"], x=xs[cid],
            wgq=wgq, wuq=wuq, wdq=wdq, repm32c=c["repm32"][:, 0:704].copy(),
            repmb=c["repmb"], ident=c["ident"]))
    rc = run_bass_kernel_spmd(nc_c, in_maps, core_ids=list(range(NCORE))).results

    out = np.empty((B, T, H), np.float32)
    for cid in range(NCORE):
        b = cid // 2
        t0 = (cid % 2) * TC
        out[b, t0:t0 + TC] = rc[cid]["out"]
    return out
